# version 01
# baseline (speedup 1.0000x reference)
import math
from contextlib import ExitStack

import numpy as np
import ml_dtypes

import concourse.bass as bass
import concourse.mybir as mybir
from concourse.bass_utils import run_bass_kernel_spmd

F32 = mybir.dt.float32
BF16 = mybir.dt.bfloat16
AF = mybir.ActivationFunctionType
ALU = mybir.AluOpType

D = 1024
T = 2048
GT = 8
HEADS = 8
DH = 64
INNER = 512
HID = 2730
WIN = 512
NW = 4
NMODE = 16
KAPPA = 0.5
LN10 = math.log(10.0)

BIGNEG = -1.0e30


def _f32(x):
    return np.ascontiguousarray(x, dtype=np.float32)


def host_constants():
    c = {}
    c["c_ident"] = _f32(np.eye(128))
    c["c_ones"] = _f32(np.ones((128, 128)))
    c["c_ones_bf"] = np.ascontiguousarray(np.ones((128, 128)), dtype=ml_dtypes.bfloat16)
    bd = np.zeros((128, 128), np.float32)
    bd[:64, :64] = 1.0
    bd[64:, 64:] = 1.0
    c["c_bd64"] = _f32(bd)
    c["c_bd64_bf"] = bd.astype(ml_dtypes.bfloat16)
    P = np.zeros((128, 128), np.float32)
    for h0 in (0, 64):
        for i in range(32):
            P[h0 + i + 32, h0 + i] = -1.0
            P[h0 + i, h0 + i + 32] = 1.0
    c["c_rot"] = _f32(P)
    c["c_rot_bf"] = P.astype(ml_dtypes.bfloat16)
    m_lo = np.zeros((128, 128), np.float32)
    m_hi = np.zeros((128, 128), np.float32)
    for p in range(128):
        m_lo[p, :p] = BIGNEG
        m_hi[p, p + 1:] = BIGNEG
    c["c_mlo"] = _f32(m_lo)
    c["c_mhi"] = _f32(m_hi)
    c["c_tramp"] = _f32(np.arange(T)[None, :])
    halfcol = np.array([2 * ((p % 64) % 32) for p in range(128)], np.float32)[:, None]
    c["c_halfcol"] = _f32(halfcol)
    half64 = np.arange(0, 64, 2, dtype=np.float64)
    invf64 = 1.0 / 10000.0 ** (half64 / 64.0)
    fr = np.arange(2048, dtype=np.float64)[:, None] * invf64[None, :]
    jmap = np.array([(p % 64) % 32 for p in range(128)])
    kw = np.round(fr[:, jmap].T / (2 * np.pi))
    c["c_krnd"] = _f32(kw)
    En = np.zeros((128, 16), np.float32)
    for p in range(128):
        En[p, p % 16] = 1.0
    c["c_En"] = _f32(En)
    c["c_En2"] = _f32(En.T.copy())
    Edq = np.zeros((128, 8), np.float32)
    for p in range(128):
        Edq[p, p // 16] = 1.0
    c["c_Edq"] = _f32(Edq)
    phases = np.linspace(0.0, 2.0 * np.pi * (1.0 - 1.0 / NMODE), NMODE)
    h0re = (0.01 * np.cos(phases)).astype(np.float32)
    h0im = (0.01 * np.sin(phases)).astype(np.float32)
    c["c_h0re"] = _f32(h0re[:, None])
    c["c_h0im"] = _f32(h0im[:, None])
    return c


def host_prep(inputs):
    x = np.asarray(inputs["L0_attn_a_x"])
    nb = x.shape[0]
    consts = host_constants()

    def dvec(name):
        v = np.asarray(inputs[name])
        return _f32(v.reshape(GT, 128).T)

    common = {
        "w_aq": _f32(inputs["L0_attn_a_Wq"]),
        "w_akv": _f32(inputs["L0_attn_a_Wkv"]),
        "w_ao": _f32(inputs["L0_attn_a_Wo"]),
        "w_bin": _f32(inputs["L0_attn_b_W_in"]),
        "w_bxp": _f32(inputs["L0_attn_b_W_xproj"]),
        "w_bdt": _f32(inputs["L0_attn_b_W_dt"]),
        "b_bdt": dvec("L0_attn_b_b_dt"),
        "w_bout": _f32(inputs["L0_attn_b_W_out"]),
        "b_bout": dvec("L0_attn_b_b_out"),
        "w_bres": None,
        "a_log": _f32(np.asarray(inputs["L0_attn_b_A_log"])[:, None]),
        "w_qkv0": _f32(inputs["L0_ffn_wqkv"][0]),
        "w_qkv1": _f32(inputs["L0_ffn_wqkv"][1]),
        "qs0": _f32(np.asarray(inputs["L0_ffn_q_scale"][0])[:, None]),
        "qs1": _f32(np.asarray(inputs["L0_ffn_q_scale"][1])[:, None]),
        "ks0": _f32(np.asarray(inputs["L0_ffn_k_scale"][0])[:, None]),
        "ks1": _f32(np.asarray(inputs["L0_ffn_k_scale"][1])[:, None]),
        "w_fo0": _f32(inputs["L0_ffn_wo"][0]),
        "w_fo1": _f32(inputs["L0_ffn_wo"][1]),
        "lg0": _f32(np.asarray(inputs["L0_ffn_ln_g"][0]).reshape(GT, 128).T),
        "lg1": _f32(np.asarray(inputs["L0_ffn_ln_g"][1]).reshape(GT, 128).T),
        "lb0": _f32(np.asarray(inputs["L0_ffn_ln_b"][0]).reshape(GT, 128).T),
        "lb1": _f32(np.asarray(inputs["L0_ffn_ln_b"][1]).reshape(GT, 128).T),
    }
    wres = np.asarray(inputs["L0_attn_b_W_res"], np.float32)
    wra = np.zeros((D, 64), np.float32)
    wra[:, 0:16] = wres[:, 0::2]
    wra[:, 32:48] = wres[:, 1::2]
    common["w_bres"] = _f32(wra)

    HIDP = 2816
    w1 = np.asarray(inputs["L0_ffn_w1"])
    b1 = np.asarray(inputs["L0_ffn_b1"])
    w2 = np.asarray(inputs["L0_ffn_w2"])
    b2 = np.asarray(inputs["L0_ffn_b2"])
    for l in range(2):
        w1a = np.zeros((D, HIDP), np.float32); w1a[:, :HID] = w1[l][:, :HID]
        w1g = np.zeros((D, HIDP), np.float32); w1g[:, :HID] = w1[l][:, HID:]
        b1a = np.zeros((HIDP,), np.float32); b1a[:HID] = b1[l][:HID]
        b1g = np.zeros((HIDP,), np.float32); b1g[:HID] = b1[l][HID:]
        w2p = np.zeros((HIDP, D), np.float32); w2p[:HID, :] = w2[l]
        common[f"w_f1a{l}"] = _f32(w1a)
        common[f"w_f1g{l}"] = _f32(w1g)
        common[f"b_f1a{l}"] = _f32(b1a.reshape(22, 128).T)
        common[f"b_f1g{l}"] = _f32(b1g.reshape(22, 128).T)
        common[f"w_f2{l}"] = _f32(w2p)
        common[f"b_f2{l}"] = _f32(b2[l].reshape(GT, 128).T)

    in_maps = []
    for b in range(nb):
        m = {"x_in": _f32(x[b].T)}
        m.update(common)
        m.update(consts)
        in_maps.append(m)
    return in_maps



class KB:

    def __init__(self, nc, es):
        self.nc = nc
        self.es = es
        self.sems = {}
        self.cnt = {}

    def sem(self, name):
        if name not in self.sems:
            self.sems[name] = self.es.enter_context(self.nc.semaphore(name))
            self.cnt[name] = 0
        return self.sems[name]

    def inc(self, instr, name, n=1):
        instr.then_inc(self.sem(name), n)
        self.cnt[name] += n
        return self.cnt[name]

    def wait(self, eng, name, val=None):
        eng.wait_ge(self.sem(name), self.cnt[name] if val is None else val)

    def sb(self, name, shape, dtype):
        return self.es.enter_context(self.nc.sbuf_tensor(name, list(shape), dtype))

    def ps(self, name, shape, dtype=F32):
        return self.es.enter_context(self.nc.psum_tensor(name, list(shape), dtype))



def _build(phase="full"):
    nc = bass.Bass("TRN2", target_bir_lowering=False, detect_race_conditions=False)
    AP_IN = {}
    names_f32 = [
        ("x_in", [D, T]), ("w_aq", [D, INNER]), ("w_akv", [D, 2 * INNER]), ("w_ao", [INNER, D]),
        ("w_bin", [D, 2 * D]), ("w_bxp", [D, 128]), ("w_bdt", [64, D]), ("b_bdt", [128, GT]),
        ("w_bout", [D, D]), ("b_bout", [128, GT]), ("w_bres", [D, 64]), ("a_log", [16, 1]),
        ("w_qkv0", [D, 3 * INNER]), ("w_qkv1", [D, 3 * INNER]),
        ("qs0", [DH, 1]), ("qs1", [DH, 1]), ("ks0", [DH, 1]), ("ks1", [DH, 1]),
        ("w_fo0", [INNER, D]), ("w_fo1", [INNER, D]),
        ("lg0", [128, GT]), ("lg1", [128, GT]), ("lb0", [128, GT]), ("lb1", [128, GT]),
        ("w_f1a0", [D, 2816]), ("w_f1g0", [D, 2816]), ("w_f1a1", [D, 2816]), ("w_f1g1", [D, 2816]),
        ("b_f1a0", [128, 22]), ("b_f1g0", [128, 22]), ("b_f1a1", [128, 22]), ("b_f1g1", [128, 22]),
        ("w_f20", [2816, D]), ("w_f21", [2816, D]),
        ("b_f20", [128, GT]), ("b_f21", [128, GT]),
        ("c_ident", [128, 128]), ("c_ones", [128, 128]), ("c_bd64", [128, 128]),
        ("c_rot", [128, 128]), ("c_mlo", [128, 128]), ("c_mhi", [128, 128]),
        ("c_tramp", [1, T]), ("c_halfcol", [128, 1]), ("c_krnd", [128, T]),
        ("c_En", [128, 16]), ("c_En2", [16, 128]), ("c_Edq", [128, 8]),
        ("c_h0re", [16, 1]), ("c_h0im", [16, 1]),
    ]
    names_bf = [("c_ones_bf", [128, 128]), ("c_bd64_bf", [128, 128]), ("c_rot_bf", [128, 128])]
    for n, shp in names_f32:
        AP_IN[n] = nc.dram_tensor(n, shp, F32, kind="ExternalInput").ap()
    for n, shp in names_bf:
        AP_IN[n] = nc.dram_tensor(n, shp, BF16, kind="ExternalInput").ap()
    x_out = nc.dram_tensor("x_out", [D, T], F32, kind="ExternalOutput").ap()
    d_xsp = nc.dram_tensor("d_xsp", [D, T], F32, kind="Internal").ap()
    d_z = nc.dram_tensor("d_z", [D, T], F32, kind="Internal").ap()
    d_dt = nc.dram_tensor("d_dt", [T, D], F32, kind="Internal").ap()
    d_dtx = nc.dram_tensor("d_dtx", [D, T], F32, kind="Internal").ap()
    d_dtx2 = nc.dram_tensor("d_dtx2", [T, D], F32, kind="Internal").ap()
    d_xp = nc.dram_tensor("d_xp", [128, T], F32, kind="Internal").ap()
    d_xisp = nc.dram_tensor("d_xisp", [D, T], F32, kind="Internal").ap()
    d_ff = nc.dram_tensor("d_ff", [2816, T], BF16, kind="Internal").ap()

    with ExitStack() as es:
        kb = KB(nc, es)
        g = Glob(kb, AP_IN, dict(x_out=x_out, d_xsp=d_xsp, d_z=d_z, d_dt=d_dt,
                                 d_dtx=d_dtx, d_dtx2=d_dtx2, d_xp=d_xp, d_xisp=d_xisp, d_ff=d_ff))
        emit_program(g, phase)
    return nc


class Glob:

    def __init__(self, kb, IN, DRAM):
        self.kb = kb
        self.IN = IN
        self.DRAM = DRAM
        nc = kb.nc
        self.nc = nc
        self.XH = kb.sb("XH", [128, GT * (T + 1)], BF16)
        self.AF = kb.sb("AF", [128, 16904], F32)
        self.AB = kb.sb("AB", [128, 26624], BF16)
        self.LNS = kb.sb("LNS", [128, 1536], F32)
        self.SQ = kb.sb("SQ", [128, 1024], F32)
        self.CONST = kb.sb("CONST", [128, 1024], F32)
        self.CONSTB = kb.sb("CONSTB", [128, 512], BF16)
        self.ROT = kb.sb("ROT", [128, 4 * T], BF16)
        self.SMALL = kb.sb("SMALL", [128, 256], F32)
        self.WB = kb.sb("WB", [128, 8192], BF16)
        self.PS = [kb.ps(f"PS{i}", [128, 512], F32) for i in range(8)]
        self.PT = self.PS[7][:, 0:128]
        self.PT2 = self.PS[6]

    def ident(self):
        return self.CONST[:, 0:128]

    def ones(self):
        return self.CONST[:, 128:256]

    def bd64(self):
        return self.CONST[:, 256:384]

    def rot(self):
        return self.CONST[:, 384:512]

    def mlo(self):
        return self.CONST[:, 512:640]

    def mhi(self):
        return self.CONST[:, 640:768]

    def En(self):
        return self.CONST[:, 768:784]

    def Edq(self):
        return self.CONST[:, 784:792]

    def En2(self):
        return self.CONST[0:16, 800:928]

    def onesb(self):
        return self.CONSTB[:, 0:128]

    def bd64b(self):
        return self.CONSTB[:, 128:256]

    def rotb(self):
        return self.CONSTB[:, 256:384]

    def xh_tile(self, k, shift=False):
        base = k * (T + 1)
        off = 0 if shift else 1
        return self.XH[:, base + off: base + off + T]


NCH = 4
CH = 512

AF_X = 0
AF_X_END = 16392
AF_SCR = 16392


def af_xtile(g, gl, shift=False, pad=False):
    base = AF_X + gl * 2049 if False else AF_X + g * 2049
    if pad:
        return gl.AF[:, base: base + 2049]
    off = 0 if shift else 1
    return gl.AF[:, base + off: base + off + T]


class P:
    pass


def emit_program(g, phase):
    global _LN_DEBUG
    kb, nc = g.kb, g.nc
    IN, DRAM = g.IN, g.DRAM
    sy, gp, tn, vc, sc = nc.sync, nc.gpsimd, nc.tensor, nc.vector, nc.scalar

    def xt(gi, shift=False):
        base = AF_X + gi * 2049
        off = 0 if shift else 1
        return g.AF[:, base + off: base + off + T]

    def xt_pad(gi):
        return g.AF[:, AF_X + gi * 2049: AF_X + gi * 2049 + 1]

    def dma(eng, out, in_, cnt="gld"):
        kb.inc(eng.dma_start(out=out, in_=in_), cnt, 16)

    dma(sy, g.CONST[:, 0:128], IN["c_ident"])
    dma(sy, g.CONST[:, 128:256], IN["c_ones"])
    dma(sy, g.CONST[:, 256:384], IN["c_bd64"])
    dma(sy, g.CONST[:, 384:512], IN["c_rot"])
    dma(sy, g.CONST[:, 512:640], IN["c_mlo"])
    dma(sy, g.CONST[:, 640:768], IN["c_mhi"])
    dma(sy, g.CONST[:, 768:784], IN["c_En"])
    dma(sy, g.CONST[:, 784:792], IN["c_Edq"])
    dma(sy, g.CONST[0:16, 800:928], IN["c_En2"])
    dma(sy, g.CONST[:, 928:929], IN["c_halfcol"])
    dma(sy, g.CONSTB[:, 0:128], IN["c_ones_bf"])
    dma(sy, g.CONSTB[:, 128:256], IN["c_bd64_bf"])
    dma(sy, g.CONSTB[:, 256:384], IN["c_rot_bf"])
    for i, nm in enumerate(["qs0", "ks0", "qs1", "ks1"]):
        src = IN[nm].unsqueeze(0).broadcast_to([2, DH, 1])
        dma(gp, g.SMALL[:, i:i + 1], src, "gpl")
    dma(sy, g.SMALL[:, 8:16], IN["b_bdt"])
    dma(sy, g.SMALL[:, 16:24], IN["b_bout"])
    dma(sy, g.SMALL[:, 24:32], IN["b_f20"])
    dma(sy, g.SMALL[:, 32:40], IN["b_f21"])
    dma(sy, g.SMALL[:, 40:48], IN["lg0"])
    dma(sy, g.SMALL[:, 48:56], IN["lg1"])
    dma(sy, g.SMALL[:, 56:64], IN["lb0"])
    dma(sy, g.SMALL[:, 64:72], IN["lb1"])
    dma(sy, g.SMALL[:, 72:94], IN["b_f1a0"])
    dma(sy, g.SMALL[:, 94:116], IN["b_f1g0"])
    dma(sy, g.SMALL[:, 116:138], IN["b_f1a1"])
    dma(sy, g.SMALL[:, 138:160], IN["b_f1g1"])
    dma(sy, g.SMALL[0:16, 160:161], IN["c_h0re"])
    dma(sy, g.SMALL[0:16, 161:162], IN["c_h0im"])
    dma(sy, g.SMALL[0:16, 162:163], IN["a_log"])

    QS8 = [g.SMALL[:, 0:1], g.SMALL[:, 2:3]]
    KS = [g.SMALL[:, 1:2], g.SMALL[:, 3:4]]
    LG = [g.SMALL[:, 40:48], g.SMALL[:, 48:56]]
    LB = [g.SMALL[:, 56:64], g.SMALL[:, 64:72]]
    B2 = [g.SMALL[:, 24:32], g.SMALL[:, 32:40]]
    B1A = [g.SMALL[:, 72:94], g.SMALL[:, 116:138]]
    B1G = [g.SMALL[:, 94:116], g.SMALL[:, 138:160]]

    CB = {}
    _cb_vals = [("eps5", 1e-5), ("n16", -16.0), ("lnbt", (0.4 * DH) / (1.4 * DH)),
                ("eps8", 1e-8), ("eps10", 1e-10), ("one", 1.0)]
    for _i, (_nm, _v) in enumerate(_cb_vals):
        col = 168 + _i
        kb.inc(gp.memset(g.SMALL[:, col:col + 1], _v), "gvv")
        CB[_nm] = g.SMALL[:, col:col + 1]

    TR = g.AF[:, 0:2048]
    ANG = g.AF[:, 2048:4096]
    KF = g.AF[:, 4096:6144]
    SN = g.AF[:, 6144:8192]
    CS = g.AF[:, 8192:10240]
    POW = g.AF[:, 10240:12288]
    SCL = g.AF[:, 12288:14336]
    ISCL = g.AF[:, 14336:16384]
    ivf = g.SMALL[:, 164:165]
    lnb = g.SMALL[:, 165:166]
    dma(gp, TR, IN["c_tramp"].unsqueeze(0).broadcast_to([128, 1, T]), "gpl")
    kb.wait(sc, "gld")
    kb.inc(sc.activation(ivf, g.CONST[:, 928:929], AF.Exp, scale=-math.log(10000.0) / DH), "ssc")
    kb.wait(sc, "gvv")
    kb.inc(sc.activation(lnb, g.CONST[:, 928:929], AF.Ln, scale=1.0 / (1.4 * DH), bias=CB["lnbt"]), "ssc")
    kb.wait(vc, "gld")
    kb.wait(vc, "gpl")
    kb.wait(vc, "ssc")
    kb.inc(vc.tensor_scalar_mul(g.SMALL[:, 0:1], g.SMALL[:, 0:1], 8.0), "vv")
    kb.inc(vc.tensor_scalar_mul(g.SMALL[:, 2:3], g.SMALL[:, 2:3], 8.0), "vv")
    TWO_PI = 2.0 * math.pi
    kb.inc(vc.tensor_scalar_mul(ANG, TR, ivf), "vv")
    dma(sy, KF, IN["c_krnd"])
    kb.wait(vc, "gld")
    kb.inc(vc.scalar_tensor_tensor(ANG, KF, -TWO_PI, ANG, ALU.mult, ALU.add), "vv")
    kb.inc(vc.tensor_scalar(ANG, ANG, math.pi, -math.pi, ALU.min, ALU.max), "vv")
    kb.wait(sc, "vv")
    kb.inc(sc.activation(SN, ANG, AF.Sin), "ssc")
    kb.inc(vc.tensor_scalar_add(CS, ANG, math.pi / 2.0), "vv")
    kb.inc(vc.tensor_single_scalar(KF, CS, math.pi, ALU.is_gt), "vv")
    kb.inc(vc.scalar_tensor_tensor(CS, KF, -TWO_PI, CS, ALU.mult, ALU.add), "vv")
    kb.wait(sc, "vv")
    kb.inc(sc.activation(CS, CS, AF.Sin), "ssc")
    kb.inc(vc.tensor_scalar(POW, TR, 1.0 / (WIN // 2), -(T // 2) / (WIN // 2), ALU.mult, ALU.add), "vv")
    kb.inc(vc.tensor_scalar_mul(POW, POW, lnb), "vv")
    kb.wait(sc, "vv")
    kb.inc(sc.activation(SCL, POW, AF.Exp), "ssc")
    kb.inc(sc.activation(ISCL, POW, AF.Exp, scale=-1.0), "ssc")
    kb.wait(vc, "ssc")
    kb.inc(vc.tensor_mul(g.ROT[:, 0:T], CS, SCL), "vv")
    kb.inc(vc.tensor_mul(g.ROT[:, T:2 * T], SN, SCL), "vv")
    kb.inc(vc.tensor_mul(g.ROT[:, 2 * T:3 * T], CS, ISCL), "vv")
    kb.inc(vc.tensor_mul(g.ROT[:, 3 * T:4 * T], SN, ISCL), "vv")

    if phase == "rot":
        kb.wait(vc, "vv")
        for i4 in range(4):
            kb.inc(vc.tensor_copy(g.AF[:, AF_X + i4 * 2049 + 1: AF_X + i4 * 2049 + 1 + T],
                                  g.ROT[:, i4 * T:(i4 + 1) * T]), "vv")
        kb.inc(vc.tensor_copy(g.AF[:, AF_X + 4 * 2049 + 1: AF_X + 4 * 2049 + 1 + T], ANG), "vv")
        kb.inc(vc.tensor_copy(g.AF[:, AF_X + 5 * 2049 + 1: AF_X + 5 * 2049 + 1 + T], SN), "vv")
        kb.inc(vc.tensor_copy(g.AF[:, AF_X + 6 * 2049 + 1: AF_X + 6 * 2049 + 1 + T], SCL), "vv")
        kb.wait(gp, "vv")
        for gi in range(GT):
            kb.inc(gp.dma_start(out=DRAM["x_out"][gi * 128:(gi + 1) * 128, :],
                                in_=g.AF[:, AF_X + gi * 2049 + 1: AF_X + gi * 2049 + 1 + T]), "gout", 16)
        gp.wait_ge(kb.sem("gout"), kb.cnt["gout"])
        return

    kb.wait(gp, "vv")
    kb.wait(sy, "vv")
    for gi in range(GT):
        kb.inc(gp.memset(xt_pad(gi), 0.0), "gvv")
        dma(sy, xt(gi), IN["x_in"][gi * 128:(gi + 1) * 128, :])
    for gi in range(GT):
        kb.inc(gp.memset(g.XH[:, gi * (T + 1):gi * (T + 1) + 1], 0.0), "gvv")

    def emit_finish_from_x_early():
        kb.wait(gp, "vv")
        kb.wait(gp, "ssc")
        kb.wait(gp, "mm")
        for gi in range(GT):
            kb.inc(gp.dma_start(out=DRAM["x_out"][gi * 128:(gi + 1) * 128, :], in_=xt(gi)), "gout", 16)
        gp.wait_ge(kb.sem("gout"), kb.cnt["gout"])

    def emit_ln(src_fn, out_bf_fn, out_f32_fn=None, wait_store=False, gamma=None):
        mu = g.LNS[:, 0:CH]
        var = g.LNS[:, CH:2 * CH]
        rstd = g.LNS[:, 2 * CH:3 * CH]
        kb.wait(tn, "gld")
        kb.wait(vc, "gld")
        for c in range(NCH):
            sl = slice(c * CH, (c + 1) * CH)
            psS, psQ = g.PS[0], g.PS[1]
            kb.wait(tn, "vv")
            kb.wait(tn, "ssc")
            for k in range(GT):
                kb.inc(tn.matmul(psS[:, 0:CH], g.ones(), src_fn(k)[:, sl],
                                 start=(k == 0), stop=(k == GT - 1)), "mm")
            for k in range(GT):
                kb.wait(vc, "mm")
                kb.inc(vc.tensor_mul(g.SQ[:, 0:CH], src_fn(k)[:, sl], src_fn(k)[:, sl]), "vv")
                kb.wait(tn, "vv")
                kb.inc(tn.matmul(psQ[:, 0:CH], g.ones(), g.SQ[:, 0:CH],
                                 start=(k == 0), stop=(k == GT - 1)), "mm")
            kb.wait(vc, "mm")
            if _LN_DEBUG and c == 0:
                kb.inc(vc.tensor_copy(xt(0)[:, 0:CH], psS[:, 0:CH] if _LN_DEBUG == "sum" else psQ[:, 0:CH]), "vv")
            kb.inc(vc.tensor_scalar_mul(mu, psS[:, 0:CH], 1.0 / D), "vv")
            kb.inc(vc.tensor_scalar_mul(var, psQ[:, 0:CH], 1.0 / D), "vv")
            kb.inc(vc.tensor_mul(g.SQ[:, 0:CH], mu, mu), "vv")
            kb.inc(vc.tensor_sub(var, var, g.SQ[:, 0:CH]), "vv")
            kb.wait(sc, "vv")
            kb.inc(sc.activation(g.SQ[:, CH:2 * CH], var, AF.Ln, bias=CB["eps5"]), "ssc")
            kb.inc(sc.activation(rstd, g.SQ[:, CH:2 * CH], AF.Exp, scale=-0.5), "ssc")
            kb.wait(vc, "ssc")
            if wait_store:
                kb.wait(vc, "gst")
            for k in range(GT):
                kb.inc(vc.tensor_sub(g.SQ[:, 0:CH], src_fn(k)[:, sl], mu), "vv")
                if out_f32_fn is not None:
                    kb.inc(vc.tensor_mul(out_f32_fn(k)[:, sl], g.SQ[:, 0:CH], rstd), "vv")
                    kb.wait(sc, "vv")
                    kb.inc(sc.copy(out_bf_fn(k)[:, sl], out_f32_fn(k)[:, sl]), "ssc")
                elif gamma is not None:
                    kb.inc(vc.scalar_tensor_tensor(out_bf_fn(k)[:, sl], rstd, gamma[:, k:k + 1],
                                                   g.SQ[:, 0:CH], ALU.mult, ALU.mult), "vv")
                else:
                    kb.inc(vc.tensor_mul(out_bf_fn(k)[:, sl], g.SQ[:, 0:CH], rstd), "vv")

    def xh(k, shift=False):
        return g.xh_tile(k, shift)

    def load_w_mslice(wap, m, wb_off, kt=GT):
        for k in range(kt):
            dma(gp, g.WB[:, wb_off + k * 128: wb_off + (k + 1) * 128],
                wap[k * 128:(k + 1) * 128, m * 128:(m + 1) * 128], "gwt")
        return [g.WB[:, wb_off + k * 128: wb_off + (k + 1) * 128] for k in range(kt)]

    kb.wait(vc, "gld")
    kb.wait(vc, "gvv")
    emit_ln(lambda k: xt(k), lambda k: xh(k))

    if phase == "ln1":
        kb.wait(vc, "ssc")
        if not _LN_DEBUG:
            for gi in range(GT):
                kb.inc(vc.tensor_copy(xt(gi), xh(gi)), "vv")
        emit_finish_from_x_early()
        return

    AB_K = 0
    AB_VAUG = 8192
    AB_Q = 16384
    AB_AO = 18432
    AB_PT = 20480
    AB_VS = 21504

    def vaug_tile(h, tt):
        off = AB_VAUG + (h * 16 + tt) * 64
        return g.AB[:, off: off + 64]

    for m in range(8):
        wts = load_w_mslice(IN["w_akv"], m, 1024 * m)
    kb.wait(tn, "gwt")
    for m in range(8):
        wts = [g.WB[:, 1024 * m + k * 128: 1024 * m + (k + 1) * 128] for k in range(GT)]
        for c in range(NCH):
            sl = slice(c * CH, (c + 1) * CH)
            ps = g.PS[2]
            kb.wait(tn, "vv")
            kb.wait(tn, "ssc")
            for k in range(GT):
                kb.inc(tn.matmul(ps[:, 0:CH], wts[k], xh(k)[:, sl],
                                 start=(k == 0), stop=(k == GT - 1)), "mm")
            if m < 4:
                kraw = g.AB[:, AB_K + m * T: AB_K + (m + 1) * T]
                kb.wait(sc, "mm")
                kb.inc(sc.copy(kraw[:, sl], ps[:, 0:CH]), "ssc")
                kb.wait(vc, "ssc")
                kb.inc(vc.tensor_mul(g.SQ[:, 0:CH], kraw[:, sl], kraw[:, sl]), "vv")
                kb.wait(tn, "vv")
                kb.inc(tn.matmul(g.PS[3][:, 0:CH], g.bd64(), g.SQ[:, 0:CH],
                                 start=True, stop=True), "mm")
                kb.wait(vc, "mm")
                kb.inc(vc.tensor_scalar_max(g.SQ[:, 0:CH], g.PS[3][:, 0:CH], 1e-24), "vv")
                kb.wait(sc, "vv")
                kb.inc(sc.activation(g.SQ[:, CH:2 * CH], g.SQ[:, 0:CH], AF.Ln), "ssc")
                kb.inc(sc.activation(g.SQ[:, 0:CH], g.SQ[:, CH:2 * CH], AF.Exp, scale=-0.5), "ssc")
                kb.wait(vc, "ssc")
                kb.inc(vc.tensor_mul(kraw[:, sl], kraw[:, sl], g.SQ[:, 0:CH]), "vv")
            else:
                h0 = (m - 4) * 2
                vsc = g.AB[:, AB_VS: AB_VS + CH]
                kb.wait(sc, "mm")
                kb.wait(sc, "gst")
                kb.inc(sc.copy(vsc, ps[:, 0:CH]), "ssc")
                kb.wait(sy, "ssc")
                for hh in range(2):
                    for sb4 in range(4):
                        tt = c * 4 + sb4
                        src = vsc[hh * 64:(hh + 1) * 64, sb4 * 128:(sb4 + 1) * 128]
                        dst = vaug_tile(h0 + hh, tt)
                        kb.inc(sy.dma_start(out=dst, in_=src, transpose=True), "gst", 16)

    kb.wait(gp, "mm")
    for m in range(4):
        load_w_mslice(IN["w_aq"], m, 1024 * m)
    for k4 in range(4):
        for mo in range(8):
            dma(gp, g.WB[:, 4096 + k4 * 1024 + mo * 128: 4096 + k4 * 1024 + (mo + 1) * 128],
                IN["w_ao"][k4 * 128:(k4 + 1) * 128, mo * 128:(mo + 1) * 128], "gwt")

    for c in range(NCH):
        sl = slice(c * CH, (c + 1) * CH)
        kb.wait(tn, "gwt")
        for m in range(4):
            wts = [g.WB[:, 1024 * m + k * 128: 1024 * m + (k + 1) * 128] for k in range(GT)]
            ps = g.PS[2]
            kb.wait(tn, "vv")
            kb.wait(tn, "ssc")
            for k in range(GT):
                kb.inc(tn.matmul(ps[:, 0:CH], wts[k], xh(k)[:, sl],
                                 start=(k == 0), stop=(k == GT - 1)), "mm")
            qraw = g.AB[:, AB_Q + m * CH: AB_Q + (m + 1) * CH]
            kb.wait(sc, "mm")
            kb.inc(sc.copy(qraw, ps[:, 0:CH]), "ssc")
            kb.wait(vc, "ssc")
            kb.inc(vc.tensor_mul(g.SQ[:, 0:CH], qraw, qraw), "vv")
            kb.wait(tn, "vv")
            kb.inc(tn.matmul(g.PS[3][:, 0:CH], g.bd64(), g.SQ[:, 0:CH], start=True, stop=True), "mm")
            kb.wait(vc, "mm")
            kb.inc(vc.tensor_scalar_max(g.SQ[:, 0:CH], g.PS[3][:, 0:CH], 1e-24), "vv")
            kb.wait(sc, "vv")
            kb.inc(sc.activation(g.SQ[:, CH:2 * CH], g.SQ[:, 0:CH], AF.Ln), "ssc")
            kb.inc(sc.activation(g.SQ[:, 0:CH], g.SQ[:, CH:2 * CH], AF.Exp, scale=-0.5), "ssc")
            kb.wait(vc, "ssc")
            kb.inc(vc.tensor_mul(qraw, qraw, g.SQ[:, 0:CH]), "vv")
        kb.wait(tn, "gst")
        for h in range(8):
            km, hh = h // 2, h % 2
            hsl = slice(hh * 64, (hh + 1) * 64)
            tp = (0, hh * 64)
            ktile = g.AB[:, AB_K + km * T: AB_K + (km + 1) * T]
            qslice = g.AB[:, AB_Q + km * CH: AB_Q + (km + 1) * CH][hsl, :]
            psO = g.PS[4]
            psR = g.PS[5]
            for tt in range(16):
                psL = g.PT2
                kb.wait(tn, "vv")
                kb.wait(tn, "ssc")
                kb.inc(tn.matmul(psL[:, 0:CH], ktile[hsl, tt * 128:(tt + 1) * 128],
                                 qslice, start=True, stop=True), "mm")
                pt = g.AB[:, AB_PT + (tt % 2) * CH: AB_PT + (tt % 2) * CH + CH]
                kb.wait(sc, "mm")
                kb.inc(sc.activation(pt, psL[:, 0:CH], AF.Exp, scale=16.0, bias=CB["n16"]), "ssc")
                kb.wait(tn, "ssc")
                kb.inc(tn.matmul(psO[hsl, 0:CH], vaug_tile(h, tt), pt,
                                 start=(tt == 0), stop=(tt == 15), tile_position=tp), "mm")
                kb.inc(tn.matmul(psR[hsl, 0:CH], g.CONSTB[:, 0:64], pt,
                                 start=(tt == 0), stop=(tt == 15), tile_position=tp), "mm")
            kb.wait(vc, "mm")
            kb.inc(vc.tensor_scalar_max(g.LNS[hsl, 0:CH], psR[hsl, 0:CH], 1e-6), "vv")
            kb.wait(sc, "vv")
            kb.inc(sc.activation(g.LNS[hsl, CH:2 * CH], g.LNS[hsl, 0:CH], AF.Ln), "ssc")
            kb.inc(sc.activation(g.LNS[hsl, 0:CH], g.LNS[hsl, CH:2 * CH], AF.Exp, scale=-1.0), "ssc")
            kb.wait(vc, "ssc")
            ao = g.AB[:, AB_AO + km * CH: AB_AO + (km + 1) * CH][hsl, :]
            kb.inc(vc.tensor_mul(ao, psO[hsl, 0:CH], g.LNS[hsl, 0:CH]), "vv")
        for mo in range(8):
            ps = g.PS[2]
            kb.wait(tn, "vv")
            kb.wait(tn, "ssc")
            for k4 in range(4):
                wo = g.WB[:, 4096 + k4 * 1024 + mo * 128: 4096 + k4 * 1024 + (mo + 1) * 128]
                aot = g.AB[:, AB_AO + k4 * CH: AB_AO + (k4 + 1) * CH]
                kb.inc(tn.matmul(ps[:, 0:CH], wo, aot, start=(k4 == 0), stop=(k4 == 3)), "mm")
            kb.wait(vc, "mm")
            kb.inc(vc.tensor_add(xt(mo)[:, sl], xt(mo)[:, sl], ps[:, 0:CH]), "vv")

    def emit_ffn_layer(l, sub=None):
        WQKV = IN["w_qkv0"] if l == 0 else IN["w_qkv1"]
        WFO = IN["w_fo0"] if l == 0 else IN["w_fo1"]
        W1A = IN["w_f1a0"] if l == 0 else IN["w_f1a1"]
        W1G = IN["w_f1g0"] if l == 0 else IN["w_f1g1"]
        W2 = IN["w_f20"] if l == 0 else IN["w_f21"]
        qs8, ks = QS8[l], KS[l]
        lg, lb = LG[l], LB[l]
        b1a, b1g = B1A[l], B1G[l]
        b2 = B2[l]

        FB_K = 0
        FB_V = 8192
        FB_Q = 16384
        FB_AO = 18432
        FB_S1 = 20480
        FB_S2 = 20992
        FB_VS = 21504
        FB_PN = 22016
        FB_PT = 23296
        FB_LBB = 24576
        FB_FF = 0
        FB_W = 22528

        def vtok_tile(h, tt):
            off = FB_V + (h * 16 + tt) * 64
            return g.AB[:, off: off + 64]

        ROT_CSQ = g.ROT[:, 0:T]
        ROT_SNQ = g.ROT[:, T:2 * T]
        ROT_CSK = g.ROT[:, 2 * T:3 * T]
        ROT_SNK = g.ROT[:, 3 * T:4 * T]

        def xi(k, shift=False):
            return xt(k, shift)

        kb.wait(gp, "mm")
        for m in range(8):
            load_w_mslice(WQKV, m, 1024 * m)
        kb.wait(tn, "gwt")
        kb.wait(tn, "vv")

        def l2norm_gain_rot(m, c, ps, gaincol, cs_tab, sn_tab, dst_tile_ap):
            sl = slice(c * CH, (c + 1) * CH)
            raw = g.AB[:, FB_VS: FB_VS + CH]
            kb.wait(sc, "mm")
            if _K_DEBUG == 1:
                kb.inc(sc.copy(dst_tile_ap, ps[:, 0:CH]), "ssc")
                return
            kb.inc(sc.copy(raw, ps[:, 0:CH]), "ssc")
            kb.wait(vc, "ssc")
            kb.inc(vc.tensor_mul(g.SQ[:, 0:CH], raw, raw), "vv")
            kb.wait(tn, "vv")
            kb.inc(tn.matmul(g.PS[3][:, 0:CH], g.bd64(), g.SQ[:, 0:CH], start=True, stop=True), "mm")
            kb.wait(vc, "mm")
            kb.inc(vc.tensor_scalar_max(g.SQ[:, 0:CH], g.PS[3][:, 0:CH], 1e-24), "vv")
            kb.wait(sc, "vv")
            kb.inc(sc.activation(g.SQ[:, CH:2 * CH], g.SQ[:, 0:CH], AF.Ln), "ssc")
            kb.inc(sc.activation(g.SQ[:, 0:CH], g.SQ[:, CH:2 * CH], AF.Exp, scale=-0.5), "ssc")
            kb.wait(vc, "ssc")
            s1 = g.AB[:, FB_S1: FB_S1 + CH]
            s2 = g.AB[:, FB_S2: FB_S2 + CH]
            kb.inc(vc.scalar_tensor_tensor(s1, g.SQ[:, 0:CH], gaincol, raw, ALU.mult, ALU.mult), "vv")
            if _K_DEBUG == 2:
                kb.inc(vc.tensor_copy(dst_tile_ap, s1), "vv")
                return
            kb.wait(tn, "vv")
            kb.inc(tn.matmul(g.PT2[:, 0:CH], g.rotb(), s1, start=True, stop=True), "mm")
            kb.wait(vc, "mm")
            kb.inc(vc.tensor_mul(s2, s1, cs_tab[:, sl]), "vv")
            kb.inc(vc.tensor_mul(s1, g.PT2[:, 0:CH], sn_tab[:, sl]), "vv")
            kb.inc(vc.tensor_add(dst_tile_ap, s2, s1), "vv")

        for m in range(4, 12):
            if m < 8:
                wts = [g.WB[:, 1024 * m + k * 128: 1024 * m + (k + 1) * 128] for k in range(GT)]
            for c in range(NCH):
                sl = slice(c * CH, (c + 1) * CH)
                ps = g.PS[2]
                kb.wait(tn, "vv")
                kb.wait(tn, "ssc")
                if m >= 8:
                    wts = [g.WB[:, 1024 * (m - 8) + k * 128: 1024 * (m - 8) + (k + 1) * 128] for k in range(GT)]
                for k in range(GT):
                    kb.inc(tn.matmul(ps[:, 0:CH], wts[k], xh(k, shift=(k >= 4))[:, sl],
                                     start=(k == 0), stop=(k == GT - 1)), "mm")
                if m < 8:
                    km = m - 4
                    ktile = g.AB[:, FB_K + km * T: FB_K + (km + 1) * T]
                    l2norm_gain_rot(m, c, ps, ks, ROT_CSK, ROT_SNK, ktile[:, sl])
                else:
                    h0 = (m - 8) * 2
                    vsc = g.AB[:, FB_VS: FB_VS + CH]
                    kb.wait(sc, "mm")
                    kb.wait(sc, "gst")
                    kb.inc(sc.copy(vsc, ps[:, 0:CH]), "ssc")
                    kb.wait(sy, "ssc")
                    for hh in range(2):
                        for sb4 in range(4):
                            tt = c * 4 + sb4
                            kb.inc(sy.dma_start(out=vtok_tile(h0 + hh, tt),
                                                in_=vsc[hh * 64:(hh + 1) * 64, sb4 * 128:(sb4 + 1) * 128],
                                                transpose=True), "gst", 16)
            if m == 7:
                kb.wait(gp, "mm")
                for mv in range(4):
                    load_w_mslice(WQKV, 8 + mv, 1024 * mv)
                for k4 in range(4):
                    for mo in range(8):
                        dma(gp, g.WB[:, 4096 + k4 * 1024 + mo * 128: 4096 + k4 * 1024 + (mo + 1) * 128],
                            WFO[k4 * 128:(k4 + 1) * 128, mo * 128:(mo + 1) * 128], "gwt")
                kb.wait(tn, "gwt")

        if sub == "K":
            kb.wait(vc, "mm")
            kb.wait(vc, "ssc")
            for km in range(4):
                kb.inc(vc.tensor_copy(xt(km), g.AB[:, FB_K + km * T: FB_K + (km + 1) * T]), "vv")
            return

        kb.wait(gp, "mm")
        for m in range(4):
            load_w_mslice(WQKV, m, 1024 * m)
        kb.wait(tn, "gwt")
        for w in range(NW):
            wsl = slice(w * WIN, (w + 1) * WIN)
            for m in range(4):
                wts = [g.WB[:, 1024 * m + k * 128: 1024 * m + (k + 1) * 128] for k in range(GT)]
                ps = g.PS[2]
                kb.wait(tn, "vv")
                kb.wait(tn, "ssc")
                for k in range(GT):
                    kb.inc(tn.matmul(ps[:, 0:CH], wts[k], xh(k, shift=(k >= 4))[:, wsl],
                                     start=(k == 0), stop=(k == GT - 1)), "mm")
                qtile = g.AB[:, FB_Q + m * CH: FB_Q + (m + 1) * CH]
                l2norm_gain_rot(m, w, ps, qs8, ROT_CSQ, ROT_SNQ, qtile)
            for Tt0 in range(1):
                pass
            for km in range(4):
                ktile = g.AB[:, FB_K + km * T: FB_K + (km + 1) * T]
                for Tt in range(4):
                    gstart = w * WIN + Tt * 128
                    if w == 0:
                        blocks = list(range(Tt + 1))
                        nb_ = len(blocks)
                    else:
                        blocks = [(w - 1) * 4 + Tt + j for j in range(5)]
                        nb_ = 5
                    for hh in range(2):
                        h = 2 * km + hh
                        hsl = slice(hh * 64, (hh + 1) * 64)
                        tp = (0, hh * 64)
                        qsl = g.AB[:, FB_Q + km * CH + Tt * 128: FB_Q + km * CH + (Tt + 1) * 128][hsl, :]
                        kb.wait(tn, "vv")
                        kb.wait(tn, "ssc")
                        for j, bt in enumerate(blocks):
                            out = g.PT if j == nb_ - 1 else g.PS[4][:, j * 128:(j + 1) * 128]
                            kb.inc(tn.matmul(out, qsl, ktile[hsl, bt * 128:(bt + 1) * 128],
                                             start=True, stop=True), "mm")
                        kb.wait(vc, "mm")
                        kb.inc(vc.tensor_add(g.PT, g.PT, g.mhi()), "vv")
                        if w > 0:
                            kb.inc(vc.tensor_add(g.PS[4][:, 0:128], g.PS[4][:, 0:128], g.mlo()), "vv")
                        nm1 = g.LNS[:, 0:1]
                        nm2 = g.LNS[:, 1:2]
                        if nb_ > 1:
                            kb.inc(vc.reduce_max(nm1, g.PS[4][:, 0:(nb_ - 1) * 128],
                                                 axis=mybir.AxisListType.X), "vv")
                            kb.inc(vc.reduce_max(nm2, g.PT, axis=mybir.AxisListType.X), "vv")
                            kb.inc(vc.tensor_max(nm1, nm1, nm2), "vv")
                        else:
                            kb.inc(vc.reduce_max(nm1, g.PT, axis=mybir.AxisListType.X), "vv")
                        kb.inc(vc.tensor_scalar_mul(nm2, nm1, -1.0), "vv")
                        pn = g.AB[:, FB_PN + hh * 640: FB_PN + hh * 640 + 640]
                        rs1 = g.LNS[:, 2:3]
                        rs2 = g.LNS[:, 3:4]
                        kb.wait(sc, "vv")
                        if nb_ > 1:
                            kb.inc(sc.activation(pn[:, 0:(nb_ - 1) * 128], g.PS[4][:, 0:(nb_ - 1) * 128],
                                                 AF.Exp, bias=nm2, accum_out=rs1), "ssc")
                            kb.inc(sc.activation(pn[:, (nb_ - 1) * 128: nb_ * 128], g.PT,
                                                 AF.Exp, bias=nm2, accum_out=rs2), "ssc")
                            kb.wait(vc, "ssc")
                            kb.inc(vc.tensor_add(rs1, rs1, rs2), "vv")
                        else:
                            kb.inc(sc.activation(pn[:, 0:128], g.PT, AF.Exp,
                                                 bias=nm2, accum_out=rs1), "ssc")
                        kb.wait(sc, "vv")
                        kb.inc(sc.activation(rs2, rs1, AF.Ln), "ssc")
                        kb.inc(sc.activation(rs1, rs2, AF.Exp, scale=-1.0), "ssc")
                        kb.wait(vc, "ssc")
                        kb.inc(vc.tensor_scalar_mul(pn[:, 0:nb_ * 128], pn[:, 0:nb_ * 128], rs1), "vv")
                        kb.wait(sy, "vv")
                        ptb = g.AB[:, FB_PT + hh * 640: FB_PT + hh * 640 + 640]
                        for j in range(nb_):
                            kb.inc(sy.dma_start(out=ptb[:, j * 128:(j + 1) * 128],
                                                in_=pn[:, j * 128:(j + 1) * 128], transpose=True), "gst", 16)
                        kb.wait(tn, "gst")
                        kb.wait(tn, "vv")
                        for j, bt in enumerate(blocks):
                            kb.inc(tn.matmul(g.PS[5][hsl, Tt * 128:(Tt + 1) * 128], vtok_tile(h, bt),
                                             ptb[:, j * 128:(j + 1) * 128],
                                             start=(j == 0), stop=(j == nb_ - 1), tile_position=tp), "mm")
                ao = g.AB[:, FB_AO + km * CH: FB_AO + (km + 1) * CH]
                kb.wait(sc, "mm")
                kb.inc(sc.copy(ao, g.PS[5][:, 0:CH]), "ssc")
            kb.wait(tn, "ssc")
            for mo in range(8):
                ps = g.PS[2]
                kb.wait(tn, "vv")
                kb.wait(tn, "ssc")
                for k4 in range(4):
                    wo = g.WB[:, 4096 + k4 * 1024 + mo * 128: 4096 + k4 * 1024 + (mo + 1) * 128]
                    aot = g.AB[:, FB_AO + k4 * CH: FB_AO + (k4 + 1) * CH]
                    kb.inc(tn.matmul(ps[:, 0:CH], wo, aot, start=(k4 == 0), stop=(k4 == 3)), "mm")
                kb.wait(vc, "mm")
                kb.inc(vc.tensor_add(xi(mo)[:, wsl], xi(mo)[:, wsl], ps[:, 0:CH]), "vv")
        if sub == "A":
            return

        emit_ln(lambda k: xi(k, shift=(k >= 4)), lambda k: xh(k), gamma=lg)
        if sub == "HN":
            kb.wait(vc, "ssc")
            for gi in range(GT):
                kb.inc(vc.tensor_copy(xt(gi), xh(gi)), "vv")
            return

        FB_FF = 0
        FB_W1A = 22528
        FB_W1G = 23552
        FB_GL = 24576
        lbb = g.CONSTB[:, 384:392]
        kb.wait(sc, "vv")
        kb.inc(sc.copy(lbb, lb), "ssc")
        NH = 22
        bca = g.SMALL[:, 176:198]
        bcg = g.SMALL[:, 198:220]
        d_ff = DRAM["d_ff"]
        for th in range(2):
            for m in range(NH):
                kb.wait(gp, "mm")
                kb.wait(gp, "gst")
                for k in range(GT):
                    dma(gp, g.AB[:, FB_W1A + k * 128: FB_W1A + (k + 1) * 128],
                        W1A[k * 128:(k + 1) * 128, m * 128:(m + 1) * 128], "gwt")
                    dma(gp, g.AB[:, FB_W1G + k * 128: FB_W1G + (k + 1) * 128],
                        W1G[k * 128:(k + 1) * 128, m * 128:(m + 1) * 128], "gwt")
                kb.wait(tn, "gwt")
                if th == 0:
                    kb.wait(tn, "vv")
                    kb.wait(tn, "ssc")
                    for k in range(GT):
                        kb.inc(tn.matmul(g.PT[:, 0:1], g.AB[:, FB_W1A + k * 128: FB_W1A + (k + 1) * 128],
                                         lbb[:, k:k + 1], start=(k == 0), stop=(k == GT - 1)), "mm")
                    for k in range(GT):
                        kb.inc(tn.matmul(g.PT[:, 1:2], g.AB[:, FB_W1G + k * 128: FB_W1G + (k + 1) * 128],
                                         lbb[:, k:k + 1], start=(k == 0), stop=(k == GT - 1)), "mm")
                    kb.wait(vc, "mm")
                    kb.inc(vc.tensor_add(bca[:, m:m + 1], g.PT[:, 0:1], b1a[:, m:m + 1]), "vv")
                    kb.inc(vc.tensor_add(bcg[:, m:m + 1], g.PT[:, 1:2], b1g[:, m:m + 1]), "vv")
                for c2 in range(2):
                    sl = slice(th * 1024 + c2 * CH, th * 1024 + (c2 + 1) * CH)
                    psA, psG = g.PS[0], g.PS[1]
                    kb.wait(tn, "vv")
                    kb.wait(tn, "ssc")
                    for k in range(GT):
                        kb.inc(tn.matmul(psG[:, 0:CH], g.AB[:, FB_W1G + k * 128: FB_W1G + (k + 1) * 128],
                                         xh(k)[:, sl], start=(k == 0), stop=(k == GT - 1)), "mm")
                    for k in range(GT):
                        kb.inc(tn.matmul(psA[:, 0:CH], g.AB[:, FB_W1A + k * 128: FB_W1A + (k + 1) * 128],
                                         xh(k)[:, sl], start=(k == 0), stop=(k == GT - 1)), "mm")
                    gl = g.AB[:, FB_GL: FB_GL + CH]
                    kb.wait(sc, "mm")
                    kb.inc(sc.activation(gl, psG[:, 0:CH], AF.Gelu, bias=bcg[:, m:m + 1]), "ssc")
                    ffsl = g.AB[:, FB_FF + m * 1024 + c2 * CH: FB_FF + m * 1024 + (c2 + 1) * CH]
                    kb.wait(vc, "ssc")
                    kb.inc(vc.scalar_tensor_tensor(ffsl, psA[:, 0:CH], bca[:, m:m + 1], gl,
                                                   ALU.add, ALU.mult), "vv")
            kb.wait(sy, "vv")
            for m in range(NH):
                kb.inc(sy.dma_start(out=d_ff[m * 128:(m + 1) * 128, th * 1024:(th + 1) * 1024],
                                    in_=g.AB[:, FB_FF + m * 1024: FB_FF + (m + 1) * 1024]), "gst", 16)
        FB_FR = 0
        FB_W2 = 4096
        kb.wait(gp, "gst")
        for q in range(4):
            mos = [2 * q, 2 * q + 1]
            for m in range(NH):
                buf = m % 2
                kb.wait(gp, "mm")
                dma(gp, g.AB[:, FB_FR + buf * 2048: FB_FR + (buf + 1) * 2048],
                    d_ff[m * 128:(m + 1) * 128, :], "gwt")
                dma(gp, g.AB[:, FB_W2 + buf * 256: FB_W2 + buf * 256 + 256],
                    W2[m * 128:(m + 1) * 128, q * 256:(q + 1) * 256], "gwt")
                kb.wait(tn, "gwt")
                kb.wait(tn, "vv")
                for dd in range(2):
                    for c4 in range(4):
                        bank = g.PS[dd * 4 + c4]
                        kb.inc(tn.matmul(bank[:, 0:CH],
                                         g.AB[:, FB_W2 + buf * 256 + dd * 128: FB_W2 + buf * 256 + (dd + 1) * 128],
                                         g.AB[:, FB_FR + buf * 2048 + c4 * CH: FB_FR + buf * 2048 + (c4 + 1) * CH],
                                         start=(m == 0), stop=(m == NH - 1)), "mm")
                if m == NH - 1:
                    kb.wait(vc, "mm")
                    for dd in range(2):
                        mo = mos[dd]
                        for c4 in range(4):
                            sl = slice(c4 * CH, (c4 + 1) * CH)
                            kb.inc(vc.scalar_tensor_tensor(xi(mo)[:, sl], g.PS[dd * 4 + c4][:, 0:CH],
                                                           b2[:, mo:mo + 1], xi(mo)[:, sl],
                                                           ALU.add, ALU.add), "vv")

    def emit_attn_b():
        emit_ln(lambda k: xt(k), lambda k: xh(k))
        kb.wait(sy, "vv")
        kb.wait(sy, "ssc")
        for gi in range(GT):
            kb.inc(sy.dma_start(out=DRAM["d_xsp"][gi * 128:(gi + 1) * 128, :], in_=xt(gi)), "gst", 16)

        AB_XIN = 0
        W_IN = IN["w_bin"]
        fstage = g.SQ[:, 0:CH]
        for m in range(16):
            kb.wait(gp, "mm")
            wts = load_w_mslice(W_IN, m, 0)
            kb.wait(tn, "gwt")
            for c in range(NCH):
                sl = slice(c * CH, (c + 1) * CH)
                ps = g.PS[2]
                kb.wait(tn, "vv")
                kb.wait(tn, "ssc")
                for k in range(GT):
                    kb.inc(tn.matmul(ps[:, 0:CH], wts[k], xh(k)[:, sl],
                                     start=(k == 0), stop=(k == GT - 1)), "mm")
                if m < 8:
                    xin_t = g.AB[:, AB_XIN + m * T: AB_XIN + (m + 1) * T]
                    kb.wait(sc, "mm")
                    kb.inc(sc.activation(xin_t[:, sl], ps[:, 0:CH], AF.Silu), "ssc")
                    kb.wait(sc, "gst")
                    kb.inc(sc.activation(fstage, ps[:, 0:CH], AF.Silu), "ssc")
                    if phase == "attn_bxin":
                        kb.inc(sc.copy(xt(m)[:, sl], fstage), "ssc")
                    kb.wait(sy, "ssc")
                    kb.inc(sy.dma_start(out=DRAM["d_dtx"][m * 128:(m + 1) * 128, sl], in_=fstage), "gst", 16)
                else:
                    kb.wait(sc, "mm")
                    kb.wait(sc, "gst")
                    kb.inc(sc.copy(fstage, ps[:, 0:CH]), "ssc")
                    kb.wait(sy, "ssc")
                    kb.inc(sy.dma_start(out=DRAM["d_z"][(m - 8) * 128:(m - 7) * 128, sl], in_=fstage), "gst", 16)
        if phase == "attn_bxin":
            return

        AB_XP = 16384
        XPB = g.AB[:, AB_XP: AB_XP + T]
        kb.wait(gp, "mm")
        for k in range(GT):
            dma(gp, g.WB[:, k * 128: (k + 1) * 128], IN["w_bxp"][k * 128:(k + 1) * 128, :], "gwt")
        kb.wait(tn, "gwt")
        for c in range(NCH):
            sl = slice(c * CH, (c + 1) * CH)
            ps = g.PS[2]
            kb.wait(tn, "vv")
            kb.wait(tn, "ssc")
            for k in range(GT):
                kb.inc(tn.matmul(ps[:, 0:CH], g.WB[:, k * 128:(k + 1) * 128],
                                 g.AB[:, AB_XIN + k * T: AB_XIN + k * T + T][:, sl],
                                 start=(k == 0), stop=(k == GT - 1)), "mm")
            kb.wait(sc, "mm")
            kb.inc(sc.copy(XPB[:, sl], ps[:, 0:CH]), "ssc")
            kb.wait(sc, "gst")
            kb.inc(sc.copy(fstage, ps[:, 0:CH]), "ssc")
            kb.wait(sy, "ssc")
            kb.inc(sy.dma_start(out=DRAM["d_xp"][:, sl], in_=fstage), "gst", 16)

        NXR = g.XH[0:16, 0:8192].bitcast(F32)
        kb.wait(gp, "mm")
        for k in range(GT):
            dma(gp, g.WB[:, 1024 + k * 64: 1024 + (k + 1) * 64], IN["w_bres"][k * 128:(k + 1) * 128, :], "gwt")
        kb.wait(tn, "gwt")
        U = g.LNS[0:16, 0:CH]
        V = g.LNS[0:16, CH:2 * CH]
        W2 = g.LNS[0:16, 2 * CH:3 * CH]
        for c in range(NCH):
            sl = slice(c * CH, (c + 1) * CH)
            ps = g.PS[3]
            ps2 = g.PS[2]
            kb.wait(tn, "vv")
            kb.wait(tn, "ssc")
            for k in range(GT):
                kb.inc(tn.matmul(ps[0:64, 0:CH], g.WB[:, 1024 + k * 64: 1024 + (k + 1) * 64],
                                 g.AB[:, AB_XIN + k * T: AB_XIN + k * T + T][:, sl],
                                 start=(k == 0), stop=(k == GT - 1)), "mm")
            kb.wait(vc, "mm")
            kb.inc(vc.tensor_scalar(U, ps[0:16, 0:CH], 1e-10, None, ALU.add), "vv")
            kb.wait(sc, "mm")
            kb.inc(sc.copy(V, ps[32:48, 0:CH]), "ssc")
            kb.wait(vc, "ssc")
            kb.inc(vc.tensor_mul(W2, U, U), "vv")
            kb.inc(vc.tensor_mul(ps2[0:16, 0:CH], V, V), "vv")
            kb.inc(vc.tensor_add(W2, W2, ps2[0:16, 0:CH]), "vv")
            kb.wait(sc, "vv")
            kb.inc(sc.activation(W2, W2, AF.Ln), "ssc")
            kb.inc(sc.activation(W2, W2, AF.Exp, scale=-0.5), "ssc")
            kb.wait(vc, "ssc")
            kb.inc(vc.tensor_mul(NXR[:, 2 * c * CH: 2 * (c + 1) * CH: 2], U, W2), "vv")
            kb.inc(vc.tensor_mul(NXR[:, 2 * c * CH + 1: 2 * (c + 1) * CH: 2], V, W2), "vv")
        kb.wait(gp, "mm")
        for ms in range(GT):
            dma(gp, g.WB[0:64, 2048 + ms * 128: 2048 + (ms + 1) * 128],
                IN["w_bdt"][:, ms * 128:(ms + 1) * 128], "gwt")
        kb.wait(tn, "gwt")
        xring = g.AF[:, 16392:16904]
        for ms in range(GT):
            for c in range(NCH):
                sl = slice(c * CH, (c + 1) * CH)
                ps = g.PS[2]
                kb.wait(tn, "vv")
                kb.wait(tn, "ssc")
                kb.inc(tn.matmul(ps[:, 0:CH], g.WB[0:64, 2048 + ms * 128: 2048 + (ms + 1) * 128],
                                 XPB[0:64, :][:, sl], start=True, stop=True), "mm")
                bcol = g.SMALL[:, 8 + ms: 9 + ms]
                t1 = g.SQ[:, 0:CH]
                t2 = g.SQ[:, CH:2 * CH]
                t3 = g.LNS[:, 0:CH]
                kb.wait(vc, "mm")
                kb.inc(vc.tensor_scalar(t1, ps[:, 0:CH], bcol, None, ALU.add), "vv")
                kb.inc(vc.scalar_tensor_tensor(t3, t1, -1.0, t1, ALU.mult, ALU.min), "vv")
                kb.wait(sc, "vv")
                kb.inc(sc.activation(t2, t3, AF.Exp), "ssc")
                kb.inc(sc.activation(t2, t2, AF.Ln, bias=CB["one"]), "ssc")
                kb.wait(vc, "ssc")
                kb.inc(vc.tensor_scalar(t1, ps[:, 0:CH], bcol, 0.0, ALU.add, ALU.max), "vv")
                kb.inc(vc.tensor_add(t1, t1, t2), "vv")
                if phase == "attn_bdt":
                    kb.inc(vc.tensor_copy(xt(ms)[:, sl], t1), "vv")
                kb.wait(gp, "vv")
                dma(gp, xring, DRAM["d_dtx"][ms * 128:(ms + 1) * 128, sl], "gxr")
                kb.wait(vc, "gxr")
                kb.inc(vc.tensor_mul(t2, t1, xring), "vv")
                kb.wait(tn, "vv")
                for which, buf, dram in ((0, t1, DRAM["d_dt"]), (1, t2, DRAM["d_dtx2"])):
                    for j4 in range(4):
                        pt = g.PS[4 + which]
                        kb.inc(tn.transpose(pt[:, 0:128], buf[:, j4 * 128:(j4 + 1) * 128], g.ident()), "mm")
                        kb.wait(sc, "mm")
                        kb.wait(sc, "gst")
                        kb.wait(sc, "gso")
                        kb.inc(sc.copy(fstage[:, 0:128], pt[:, 0:128]), "ssc")
                        dst = dram[c * CH + j4 * 128: c * CH + (j4 + 1) * 128, ms * 128:(ms + 1) * 128]
                        kb.wait(sy, "ssc")
                        kb.inc(sy.dma_start(out=dst, in_=fstage[:, 0:128]), "gst", 16)
                        kb.wait(tn, "gst")

        if phase == "attn_bdt":
            return
        emit_scan()
        if phase == "attn_by":
            kb.wait(vc, "mm")
            for gi in range(GT):
                kb.inc(vc.tensor_copy(xt(gi), g.AB[:, gi * T: (gi + 1) * T]), "vv")
            return
        if phase == "attn_bh":
            XO = DRAM["x_out"]
            kb.wait(gp, "scv")
            kb.wait(gp, "ssc")
            kb.wait(gp, "vv")
            HSd = g.SQ[:, 0:256]
            MUd = g.SQ[:, 256:384]
            AFd = g.AF
            NXRd = g.XH[0:16, 0:8192].bitcast(F32)
            kb.inc(gp.dma_start(out=XO[0:128, 0:1024], in_=g.SQ[:]), "gout", 16)
            kb.inc(gp.dma_start(out=XO[128:256, 0:1536], in_=g.LNS[:]), "gout", 16)
            kb.inc(gp.dma_start(out=XO[256:384, 0:256], in_=g.SMALL[:]), "gout", 16)
            kb.inc(gp.dma_start(out=XO[384:512, :], in_=AFd[:, 4096:6144]), "gout", 16)
            kb.inc(gp.dma_start(out=XO[512:640, :], in_=AFd[:, 6144:8192]), "gout", 16)
            kb.inc(gp.dma_start(out=XO[640:768, :], in_=AFd[:, 8192:10240]), "gout", 16)
            kb.inc(gp.dma_start(out=XO[768:896, :], in_=AFd[:, 10240:12288]), "gout", 16)
            kb.inc(gp.dma_start(out=XO[896:912, :], in_=NXRd[:, 0:2048]), "gout", 16)
            kb.inc(gp.dma_start(out=XO[912:928, :], in_=NXRd[:, 2048:4096]), "gout", 16)
            kb.inc(gp.dma_start(out=XO[928:944, 0:2048], in_=AFd[:, 12288:14336][0:16, :]), "gout", 16)
            TINWd = g.XH[0:16, 8192:16384].bitcast(F32)
            kb.inc(gp.dma_start(out=XO[944:960, 0:2048], in_=TINWd[:, 0:2048]), "gout", 16)
            gp.wait_ge(kb.sem("gout"), kb.cnt["gout"])
            return

        kb.wait(gp, "mm")
        for k in range(GT):
            dma(gp, g.WB[:, k * 1024:(k + 1) * 1024], IN["w_bout"][k * 128:(k + 1) * 128, :], "gwt")
        zscr = g.AB[:, 16384 + 0: 16384 + CH]
        for k in range(GT):
            for c in range(NCH):
                sl = slice(c * CH, (c + 1) * CH)
                kb.wait(gp, "vv")
                dma(gp, xring, DRAM["d_z"][k * 128:(k + 1) * 128, sl], "gxr")
                kb.wait(sc, "gxr")
                kb.inc(sc.activation(zscr, xring, AF.Silu), "ssc")
                kb.wait(vc, "ssc")
                kb.inc(vc.tensor_mul(g.AB[:, k * T: k * T + T][:, sl],
                                     g.AB[:, k * T: k * T + T][:, sl], zscr), "vv")
        kb.wait(tn, "gwt")
        kb.wait(tn, "vv")
        for mo in range(GT):
            for c in range(NCH):
                sl = slice(c * CH, (c + 1) * CH)
                ps = g.PS[2]
                kb.wait(tn, "vv")
                kb.wait(tn, "ssc")
                for k in range(GT):
                    kb.inc(tn.matmul(ps[:, 0:CH], g.WB[:, k * 1024 + mo * 128: k * 1024 + (mo + 1) * 128],
                                     g.AB[:, k * T: k * T + T][:, sl],
                                     start=(k == 0), stop=(k == GT - 1)), "mm")
                kb.wait(gp, "vv")
                dma(gp, xring, DRAM["d_xsp"][mo * 128:(mo + 1) * 128, sl], "gxr")
                kb.wait(vc, "gxr")
                kb.wait(vc, "mm")
                kb.inc(vc.scalar_tensor_tensor(g.SQ[:, 0:CH], ps[:, 0:CH], g.SMALL[:, 16 + mo:17 + mo],
                                               xring, ALU.add, ALU.add), "vv")
                kb.inc(vc.tensor_copy(xt(mo)[:, sl], g.SQ[:, 0:CH]), "vv")

    def emit_scan():
        CHS = 16
        HF = 16
        NCHUNK = T // CHS
        DTA = g.AF[:, 0:2048]
        ALPH = [g.AF[:, 2048:4096]]
        BRE = g.AF[:, 4096:6144]
        BIM = g.AF[:, 6144:8192]
        CRE = g.AF[:, 8192:10240]
        CIM = g.AF[:, 10240:12288]
        DTX = [g.AF[:, 12288 + i * 2048: 12288 + (i + 1) * 2048] for i in range(2)]
        NXR = g.XH[0:16, 0:8192].bitcast(F32)
        HS = g.SQ[:, 0:256]
        MU = g.SQ[:, 256:384]
        UB = g.SQ[:, 384:512]
        U2B = g.SQ[:, 512:640]
        HH = g.SQ[:, 640:896]
        SCR = [g.LNS[:, 256 + i * 128: 256 + (i + 1) * 128] for i in range(10)]
        YB = g.AB[:, 0:16384]
        ANEG = g.SMALL[:, 224:225]
        H0R = g.SMALL[:, 225:226]
        H0I = g.SMALL[:, 226:227]
        CQ = g.SMALL[:, 228:230]
        TIN = g.LNS[0:16, 0:16]
        pcar = g.PS[0][0:16, 0:256]
        pcols = g.PS[1][:, 0:2]
        pys = [g.PS[2], g.PS[3]]

        kb.wait(gp, "vv")
        kb.wait(gp, "ssc")
        dma(gp, ANEG, IN["a_log"].unsqueeze(0).broadcast_to([8, 16, 1]), "gpl")
        dma(gp, H0R, IN["c_h0re"].unsqueeze(0).broadcast_to([8, 16, 1]), "gpl")
        dma(gp, H0I, IN["c_h0im"].unsqueeze(0).broadcast_to([8, 16, 1]), "gpl")

        def bc_dma(dst, row0):
            src = DRAM["d_xp"][row0: row0 + 31: 2, :].unsqueeze(0).broadcast_to([8, 16, T])
            dma(gp, dst, src, "gpl")
        kb.wait(gp, "gst")
        sy.wait_ge(kb.sem("gso"), kb.cnt.get("gso", 0))
        bc_dma(BRE, 64)
        bc_dma(BIM, 65)
        bc_dma(CRE, 96)
        bc_dma(CIM, 97)
        kb.wait(sc, "gpl")
        kb.inc(sc.activation(ANEG, ANEG, AF.Exp), "ssc")
        kb.wait(vc, "ssc")
        kb.wait(vc, "gst")
        kb.inc(vc.tensor_scalar_mul(ANEG, ANEG, -1.0), "vv")
        kb.inc(vc.tensor_copy(HS[:, 0:128], H0R.broadcast_to([128, 128])), "vv")
        kb.inc(vc.tensor_copy(HS[:, 128:256], H0I.broadcast_to([128, 128])), "vv")
        kb.wait(gp, "gst")
        kb.inc(gp.memset(MU[:], 0.01), "gvv")
        kb.wait(vc, "gvv")

        def chunk_dma(ci):
            base = ci * CHS
            src_dt = DRAM["d_dt"].rearrange("t (dq f) -> dq t f", dq=GT)
            src_dx = DRAM["d_dtx2"].rearrange("t (dq f) -> dq t f", dq=GT)
            for dqi in range(GT):
                s1_ = src_dt[dqi, base: base + CHS, :].unsqueeze(0).broadcast_to([16, CHS, 128])
                d1_ = DTA[dqi * 16:(dqi + 1) * 16, :].rearrange("p (t f) -> p t f", t=CHS)
                kb.inc(sy.dma_start(out=d1_, in_=s1_), "gsc", 16)
                s2_ = src_dx[dqi, base: base + CHS, :].unsqueeze(0).broadcast_to([16, CHS, 128])
                d2_ = DTX[ci % 2][dqi * 16:(dqi + 1) * 16, :].rearrange("p (t f) -> p t f", t=CHS)
                kb.inc(sy.dma_start(out=d2_, in_=s2_), "gsc", 16)

        SV = kb.sem("scv")
        SA = kb.sem("sca")
        SM = kb.sem("scm")
        SE = kb.sem("sce")
        nV, nA, nM = 6, 4, 3
        vb = ab = mb = 0
        LN10C = math.log(10.0)
        TINW = g.XH[0:16, 8192:16384].bitcast(F32)
        TW = [TINW[:, j * 128:(j + 1) * 128] for j in range(9)]
        TIN2 = TINW[:, 1152:1154]
        pcolsW = g.PS[1][:, 0:256]
        EPS10_16 = CB["eps10"][0:16, 0:1]

        chunk_dma(0)
        kb.wait(sc, "gsc")
        kb.inc(sc.activation(ALPH[0], DTA[:], AF.Exp, scale=ANEG), "ssc")
        kb.wait(tn, "vv")
        kb.wait(tn, "gvv")

        def emit_half(ci, hf, slot):
            nonlocal vb, ab, mb
            gstep0 = ci * CHS + hf * HF
            nstep = HF if _STEP_LIM is None else _STEP_LIM
            for i in range(nstep):
                t = gstep0 + i
                tn.wait_ge(SV, nV * gstep0 + i * nV)
                tn.matmul(pcar, g.En(), HS[:], start=True, stop=True).then_inc(SM, 1)
                vc.wait_ge(SM, nM * gstep0 + i * nM + 1)
                vc.reduce_sum(TIN2, pcar.rearrange("p (c f) -> p c f", c=2), axis=mybir.AxisListType.X)
                vc.tensor_scalar(UB[:], HS[:, 0:128], 1e-10, None, ALU.add)
                vc.tensor_mul(SCR[0], UB[:], UB[:])
                vc.tensor_mul(SCR[1], HS[:, 128:256], HS[:, 128:256])
                vc.tensor_add(SCR[0], SCR[0], SCR[1]).then_inc(SV, 1)
                sc.wait_ge(SV, nV * gstep0 + i * nV + 1)
                sc.wait_ge(SM, nM * gstep0 + i * nM)
                sc.activation(SCR[2], MU[:], AF.Square)
                sc.activation(SCR[3], SCR[2], AF.Ln, bias=CB["eps8"])
                sc.activation(SCR[2], SCR[3], AF.Exp, scale=0.5)
                sc.activation(SCR[3], SCR[2], AF.Ln, bias=CB["eps8"])
                sc.activation(SCR[4], SCR[0], AF.Ln).then_inc(SA, 1)
                nu_ap = NXR[:, 2 * t: 2 * t + 1]
                ni_ap = NXR[:, 2 * t + 1: 2 * t + 2]
                vc.wait_ge(SA, nA * gstep0 + i * nA + 1)
                vc.tensor_scalar(TW[0], TIN2[:, 0:1].broadcast_to([16, 128]), 1.0 / D, 1e-10, ALU.mult, ALU.add)
                vc.tensor_scalar(TW[1], TIN2[:, 1:2].broadcast_to([16, 128]), 1.0 / D, None, ALU.mult)
                vc.tensor_mul(TW[2], TW[0], TW[0])
                vc.tensor_mul(TW[3], TW[1], TW[1])
                vc.tensor_add(TW[4], TW[2], TW[3])
                vc.tensor_scalar(TW[2], TW[0], nu_ap, None, ALU.mult)
                vc.scalar_tensor_tensor(TW[6], TW[1], ni_ap, TW[2], ALU.mult, ALU.add).then_inc(SV, 1)
                sc.wait_ge(SV, nV * gstep0 + i * nV + 2)
                sc.activation(TW[5], TW[4], AF.Ln)
                sc.activation(TW[5], TW[5], AF.Exp, scale=-0.5).then_inc(SA, 1)
                vc.wait_ge(SA, nA * gstep0 + i * nA + 2)
                vc.tensor_mul(TW[6], TW[6], TW[5])
                vc.tensor_scalar(TW[7], TW[6], KAPPA / 2, -KAPPA / 2, ALU.mult, ALU.add)
                vc.tensor_scalar(TW[8], TW[6], -KAPPA / 2, 1.0 + KAPPA / 2, ALU.mult, ALU.add).then_inc(SV, 1)
                tn.wait_ge(SV, nV * gstep0 + i * nV + 3)
                tn.matmul(pcolsW, g.En2(), TINW[:, 896:1152], start=True, stop=True).then_inc(SM, 1)
                vc.wait_ge(SM, nM * gstep0 + i * nM + 2)
                alp = ALPH[slot][:, i * 128:(i + 1) * 128]
                vc.tensor_scalar(SCR[5], alp, pcolsW[:, 0:1], pcolsW[:, 128:129], ALU.mult, ALU.add)
                vc.tensor_mul(SCR[6], SCR[5], SCR[3])
                vc.tensor_scalar(SCR[6], SCR[6], LN10C, None, ALU.min)
                vc.scalar_tensor_tensor(SCR[7], SCR[4], -0.5, SCR[6], ALU.mult, ALU.add).then_inc(SV, 1)
                sc.wait_ge(SV, nV * gstep0 + i * nV + 4)
                sc.activation(SCR[8], SCR[7], AF.Exp).then_inc(SA, 1)
                vc.wait_ge(SA, nA * gstep0 + i * nA + 3)
                dtxs = DTX[ci % 2][:, i * 128:(i + 1) * 128]
                bres = BRE[:, t: t + 1]
                bims = BIM[:, t: t + 1]
                vc.tensor_mul(SCR[9], SCR[8], UB[:])
                vc.scalar_tensor_tensor(HH[:, 0:128], dtxs, bres, SCR[9], ALU.mult, ALU.add)
                vc.tensor_mul(SCR[9], SCR[8], HS[:, 128:256])
                vc.scalar_tensor_tensor(HH[:, 128:256], dtxs, bims, SCR[9], ALU.mult, ALU.add)
                vc.tensor_mul(SCR[0], HH[:, 0:128], HH[:, 0:128])
                vc.tensor_mul(SCR[1], HH[:, 128:256], HH[:, 128:256])
                vc.tensor_add(SCR[5], SCR[0], SCR[1])
                vc.tensor_scalar(U2B[:], HH[:, 0:128], 1e-10, None, ALU.add)
                vc.tensor_mul(SCR[6], U2B[:], U2B[:])
                vc.tensor_add(SCR[6], SCR[6], SCR[1]).then_inc(SV, 1)
                sc.wait_ge(SV, nV * gstep0 + i * nV + 5)
                sc.activation(SCR[7], SCR[5], AF.Ln, bias=CB["eps8"])
                sc.activation(SCR[5], SCR[7], AF.Exp, scale=0.5)
                sc.activation(SCR[7], SCR[5], AF.Exp, scale=2.0)
                sc.activation(SCR[7], SCR[7], AF.Ln, bias=CB["one"])
                sc.activation(SCR[7], SCR[7], AF.Exp, scale=-1.0)
                sc.activation(SCR[0], SCR[6], AF.Ln)
                sc.activation(SCR[0], SCR[0], AF.Exp, scale=-0.5).then_inc(SA, 1)
                vc.wait_ge(SA, nA * gstep0 + i * nA + 4)
                vc.tensor_scalar(MU[:], SCR[7], -2.0, 1.0, ALU.mult, ALU.add)
                vc.tensor_mul(SCR[1], MU[:], SCR[0])
                vc.tensor_mul(HS[:, 0:128], SCR[1], U2B[:])
                vc.tensor_mul(HS[:, 128:256], SCR[1], HH[:, 128:256])
                cres = CRE[:, t: t + 1]
                cims = CIM[:, t: t + 1]
                edq3 = g.Edq().unsqueeze(1).broadcast_to([128, 16, 8])
                vc.tensor_scalar(SCR[2].rearrange("p (r f) -> p r f", r=16), edq3, cres, None, ALU.mult)
                vc.tensor_scalar(SCR[3].rearrange("p (r f) -> p r f", r=16), edq3, cims, None, ALU.mult).then_inc(SV, 1)
                tn.wait_ge(SV, nV * gstep0 + i * nV + 6)
                ycol = (gstep0 % 64) * 8
                pyb = pys[(gstep0 // 64) % 2]
                oy = pyb[:, ycol + i * 8: ycol + i * 8 + 8]
                tn.matmul(oy, HS[:, 0:128], SCR[2][:, 0:8], start=True, stop=False)
                tn.matmul(oy, HS[:, 128:256], SCR[3][:, 0:8], start=False, stop=True).then_inc(SM, 1)
            vb += nV * nstep
            ab += nA * nstep
            mb += nM * nstep
            kb.cnt["scv"] = vb
            kb.cnt["sca"] = ab
            kb.cnt["scm"] = mb

        NRUN = NCHUNK if _SCAN_LIM is None else _SCAN_LIM
        for ci in range(NRUN):
            if ci % 4 == 0 and ci // 4 >= 2:
                tn.wait_ge(SE, ci // 4 - 1)
            if ci + 1 < NCHUNK:
                sy.wait_ge(kb.sem("ssc"), kb.cnt["ssc"])
                if ci >= 1:
                    sy.wait_ge(SV, vb)
                chunk_dma(ci + 1)
            emit_half(ci, 0, 0)
            if ci + 1 < NCHUNK:
                kb.wait(sc, "gsc")
                sc.wait_ge(SV, vb)
                kb.inc(sc.activation(ALPH[0], DTA[:], AF.Exp, scale=ANEG), "ssc")
            if (ci % 4) == 3:
                blk64 = ci // 4
                bank = blk64 % 2
                sc.wait_ge(SM, mb)
                yout = YB.rearrange("p (dq t) -> p dq t", dq=GT)[:, :, blk64 * 64:(blk64 + 1) * 64]
                yin = pys[bank].rearrange("p (t dq) -> p t dq", dq=GT)
                kb.inc(sc.copy(yout.transpose([0, 2, 1]), yin), "sce")
        kb.wait(vc, "sce")
        kb.wait(vc, "scm", mb)
        kb.wait(sc, "scm", mb)
        kb.wait(tn, "sce")

    def emit_ffn_call():
        kb.wait(sy, "vv")
        kb.wait(sy, "ssc")
        for gi in range(GT):
            kb.inc(sy.dma_start(out=DRAM["d_xsp"][gi * 128:(gi + 1) * 128, :], in_=xt(gi)), "gst", 16)
        if "scv" in kb.cnt:
            kb.wait(gp, "scv")
            kb.wait(gp, "scm")
        kb.wait(gp, "vv")
        kb.wait(gp, "ssc")
        for gi in range(GT):
            kb.inc(gp.memset(g.XH[:, gi * (T + 1):gi * (T + 1) + 1], 0.0), "gvv")
        kb.wait(tn, "gvv")
        emit_ln(lambda k: xt(k), lambda k: xh(k), out_f32_fn=lambda k: xt(k), wait_store=True)
        if phase == "ffnLN2":
            return
        if phase == "xh2":
            kb.wait(vc, "ssc")
            for gi in range(GT):
                kb.inc(vc.tensor_copy(xt(gi), xh(gi)), "vv")
            return
        subm = {"ffnA0": "A", "ffnK": "K", "ffnHN": "HN"}.get(phase)
        emit_ffn_layer(0, sub=subm)
        if phase in ("ffnA0", "ffnL0", "ffnK", "ffnHN"):
            return
        kb.wait(vc, "ssc")
        kb.wait(vc, "mm")
        for gi in range(GT):
            kb.inc(vc.tensor_copy(xh(gi), xt(gi)), "vv")
        if phase == "ffnL0x":
            return
        emit_ffn_layer(1)
        ring = g.AF[:, 16392:16904]
        for gi in range(GT):
            for c in range(NCH):
                sl = slice(c * CH, (c + 1) * CH)
                kb.wait(gp, "vv")
                dma(gp, ring, DRAM["d_xsp"][gi * 128:(gi + 1) * 128, sl], "gxr")
                kb.wait(vc, "gxr")
                kb.inc(vc.tensor_add(xt(gi)[:, sl], xt(gi)[:, sl], ring), "vv")

    if phase == "attn_a":
        emit_finish_from_x_early()
        return
    emit_ffn_call()
    if phase in ("ffn1", "ffnLN2", "ffnA0", "ffnL0", "ffnL0x", "ffnK", "xh2", "ffnHN"):
        emit_finish_from_x_early()
        return
    emit_attn_b()
    if phase == "attn_bh":
        return
    if phase.startswith("attn_b"):
        emit_finish_from_x_early()
        return
    emit_ffn_call()
    emit_finish_from_x_early()

    def emit_finish_from_x():
        kb.wait(gp, "vv")
        kb.wait(gp, "ssc")
        kb.wait(gp, "mm")
        for gi in range(GT):
            dma(gp, DRAM["x_out"][gi * 128:(gi + 1) * 128, :], xt(gi), "gout")
        gp.wait_ge(kb.sem("gout"), kb.cnt["gout"])




def kernel(**inputs):
    in_maps = host_prep(inputs)
    nc = _build(_PHASE)
    res = run_bass_kernel_spmd(nc, in_maps, core_ids=list(range(len(in_maps))))
    out = np.stack([np.ascontiguousarray(np.asarray(r["x_out"], dtype=np.float32).T)
                    for r in res.results], axis=0)
    return out


_PHASE = "full"
_SCAN_LIM = None
_STEP_LIM = None
_LN_DEBUG = None
_K_DEBUG = False


# version 02
# speedup vs baseline: 1.1175x; 1.1175x over previous
import math
from contextlib import ExitStack

import numpy as np
import ml_dtypes

import concourse.bass as bass
import concourse.mybir as mybir
from concourse.bass_utils import run_bass_kernel_spmd

F32 = mybir.dt.float32
BF16 = mybir.dt.bfloat16
AF = mybir.ActivationFunctionType
ALU = mybir.AluOpType

D = 1024
T = 2048
GT = 8
HEADS = 8
DH = 64
INNER = 512
HID = 2730
WIN = 512
NW = 4
NMODE = 16
KAPPA = 0.5
LN10 = math.log(10.0)

BIGNEG = -1.0e30


def _f32(x):
    return np.ascontiguousarray(x, dtype=np.float32)


def host_constants():
    c = {}
    c["c_ident"] = _f32(np.eye(128))
    c["c_ones"] = _f32(np.ones((128, 128)))
    c["c_ones_bf"] = np.ascontiguousarray(np.ones((128, 128)), dtype=ml_dtypes.bfloat16)
    bd = np.zeros((128, 128), np.float32)
    bd[:64, :64] = 1.0
    bd[64:, 64:] = 1.0
    c["c_bd64"] = _f32(bd)
    c["c_bd64_bf"] = bd.astype(ml_dtypes.bfloat16)
    P = np.zeros((128, 128), np.float32)
    for h0 in (0, 64):
        for i in range(32):
            P[h0 + i + 32, h0 + i] = -1.0
            P[h0 + i, h0 + i + 32] = 1.0
    c["c_rot"] = _f32(P)
    c["c_rot_bf"] = P.astype(ml_dtypes.bfloat16)
    m_lo = np.zeros((128, 128), np.float32)
    m_hi = np.zeros((128, 128), np.float32)
    for p in range(128):
        m_lo[p, :p] = BIGNEG
        m_hi[p, p + 1:] = BIGNEG
    c["c_mlo"] = _f32(m_lo)
    c["c_mhi"] = _f32(m_hi)
    c["c_tramp"] = _f32(np.arange(T)[None, :])
    halfcol = np.array([2 * ((p % 64) % 32) for p in range(128)], np.float32)[:, None]
    c["c_halfcol"] = _f32(halfcol)
    half64 = np.arange(0, 64, 2, dtype=np.float64)
    invf64 = 1.0 / 10000.0 ** (half64 / 64.0)
    fr = np.arange(2048, dtype=np.float64)[:, None] * invf64[None, :]
    jmap = np.array([(p % 64) % 32 for p in range(128)])
    kw = np.round(fr[:, jmap].T / (2 * np.pi))
    c["c_krnd"] = _f32(kw)
    En = np.zeros((128, 16), np.float32)
    for p in range(128):
        En[p, p % 16] = 1.0
    c["c_En"] = _f32(En)
    c["c_En2"] = _f32(En.T.copy())
    Edq = np.zeros((128, 8), np.float32)
    for p in range(128):
        Edq[p, p // 16] = 1.0
    c["c_Edq"] = _f32(Edq)
    phases = np.linspace(0.0, 2.0 * np.pi * (1.0 - 1.0 / NMODE), NMODE)
    h0re = (0.01 * np.cos(phases)).astype(np.float32)
    h0im = (0.01 * np.sin(phases)).astype(np.float32)
    c["c_h0re"] = _f32(h0re[:, None])
    c["c_h0im"] = _f32(h0im[:, None])
    return c


def host_prep(inputs):
    x = np.asarray(inputs["L0_attn_a_x"])
    nb = x.shape[0]
    consts = host_constants()

    def dvec(name):
        v = np.asarray(inputs[name])
        return _f32(v.reshape(GT, 128).T)

    common = {
        "w_aq": _f32(inputs["L0_attn_a_Wq"]),
        "w_akv": _f32(inputs["L0_attn_a_Wkv"]),
        "w_ao": _f32(inputs["L0_attn_a_Wo"]),
        "w_bin": _f32(inputs["L0_attn_b_W_in"]),
        "w_bxp": _f32(inputs["L0_attn_b_W_xproj"]),
        "w_bdt": _f32(inputs["L0_attn_b_W_dt"]),
        "b_bdt": dvec("L0_attn_b_b_dt"),
        "w_bout": _f32(inputs["L0_attn_b_W_out"]),
        "b_bout": dvec("L0_attn_b_b_out"),
        "w_bres": None,
        "a_log": _f32(np.asarray(inputs["L0_attn_b_A_log"])[:, None]),
        "w_qkv0": _f32(inputs["L0_ffn_wqkv"][0]),
        "w_qkv1": _f32(inputs["L0_ffn_wqkv"][1]),
        "qs0": _f32(np.asarray(inputs["L0_ffn_q_scale"][0])[:, None]),
        "qs1": _f32(np.asarray(inputs["L0_ffn_q_scale"][1])[:, None]),
        "ks0": _f32(np.asarray(inputs["L0_ffn_k_scale"][0])[:, None]),
        "ks1": _f32(np.asarray(inputs["L0_ffn_k_scale"][1])[:, None]),
        "w_fo0": _f32(inputs["L0_ffn_wo"][0]),
        "w_fo1": _f32(inputs["L0_ffn_wo"][1]),
        "lg0": _f32(np.asarray(inputs["L0_ffn_ln_g"][0]).reshape(GT, 128).T),
        "lg1": _f32(np.asarray(inputs["L0_ffn_ln_g"][1]).reshape(GT, 128).T),
        "lb0": _f32(np.asarray(inputs["L0_ffn_ln_b"][0]).reshape(GT, 128).T),
        "lb1": _f32(np.asarray(inputs["L0_ffn_ln_b"][1]).reshape(GT, 128).T),
    }
    wres = np.asarray(inputs["L0_attn_b_W_res"], np.float32)
    wra = np.zeros((D, 64), np.float32)
    wra[:, 0:16] = wres[:, 0::2]
    wra[:, 32:48] = wres[:, 1::2]
    common["w_bres"] = _f32(wra)

    HIDP = 2816
    w1 = np.asarray(inputs["L0_ffn_w1"])
    b1 = np.asarray(inputs["L0_ffn_b1"])
    w2 = np.asarray(inputs["L0_ffn_w2"])
    b2 = np.asarray(inputs["L0_ffn_b2"])
    for l in range(2):
        w1a = np.zeros((D, HIDP), np.float32); w1a[:, :HID] = w1[l][:, :HID]
        w1g = np.zeros((D, HIDP), np.float32); w1g[:, :HID] = w1[l][:, HID:]
        b1a = np.zeros((HIDP,), np.float32); b1a[:HID] = b1[l][:HID]
        b1g = np.zeros((HIDP,), np.float32); b1g[:HID] = b1[l][HID:]
        w2p = np.zeros((HIDP, D), np.float32); w2p[:HID, :] = w2[l]
        common[f"w_f1a{l}"] = _f32(w1a)
        common[f"w_f1g{l}"] = _f32(w1g)
        common[f"b_f1a{l}"] = _f32(b1a.reshape(22, 128).T)
        common[f"b_f1g{l}"] = _f32(b1g.reshape(22, 128).T)
        common[f"w_f2{l}"] = _f32(w2p)
        common[f"b_f2{l}"] = _f32(b2[l].reshape(GT, 128).T)

    in_maps = []
    for b in range(nb):
        m = {"x_in": _f32(x[b].T)}
        m.update(common)
        m.update(consts)
        in_maps.append(m)
    return in_maps



class KB:

    def __init__(self, nc, es):
        self.nc = nc
        self.es = es
        self.sems = {}
        self.cnt = {}

    def sem(self, name):
        if name not in self.sems:
            self.sems[name] = self.es.enter_context(self.nc.semaphore(name))
            self.cnt[name] = 0
        return self.sems[name]

    def inc(self, instr, name, n=1):
        instr.then_inc(self.sem(name), n)
        self.cnt[name] += n
        return self.cnt[name]

    def wait(self, eng, name, val=None):
        eng.wait_ge(self.sem(name), self.cnt[name] if val is None else val)

    def sb(self, name, shape, dtype):
        return self.es.enter_context(self.nc.sbuf_tensor(name, list(shape), dtype))

    def ps(self, name, shape, dtype=F32):
        return self.es.enter_context(self.nc.psum_tensor(name, list(shape), dtype))



def _build(phase="full"):
    nc = bass.Bass("TRN2", target_bir_lowering=False, detect_race_conditions=False)
    AP_IN = {}
    names_f32 = [
        ("x_in", [D, T]), ("w_aq", [D, INNER]), ("w_akv", [D, 2 * INNER]), ("w_ao", [INNER, D]),
        ("w_bin", [D, 2 * D]), ("w_bxp", [D, 128]), ("w_bdt", [64, D]), ("b_bdt", [128, GT]),
        ("w_bout", [D, D]), ("b_bout", [128, GT]), ("w_bres", [D, 64]), ("a_log", [16, 1]),
        ("w_qkv0", [D, 3 * INNER]), ("w_qkv1", [D, 3 * INNER]),
        ("qs0", [DH, 1]), ("qs1", [DH, 1]), ("ks0", [DH, 1]), ("ks1", [DH, 1]),
        ("w_fo0", [INNER, D]), ("w_fo1", [INNER, D]),
        ("lg0", [128, GT]), ("lg1", [128, GT]), ("lb0", [128, GT]), ("lb1", [128, GT]),
        ("w_f1a0", [D, 2816]), ("w_f1g0", [D, 2816]), ("w_f1a1", [D, 2816]), ("w_f1g1", [D, 2816]),
        ("b_f1a0", [128, 22]), ("b_f1g0", [128, 22]), ("b_f1a1", [128, 22]), ("b_f1g1", [128, 22]),
        ("w_f20", [2816, D]), ("w_f21", [2816, D]),
        ("b_f20", [128, GT]), ("b_f21", [128, GT]),
        ("c_ident", [128, 128]), ("c_ones", [128, 128]), ("c_bd64", [128, 128]),
        ("c_rot", [128, 128]), ("c_mlo", [128, 128]), ("c_mhi", [128, 128]),
        ("c_tramp", [1, T]), ("c_halfcol", [128, 1]), ("c_krnd", [128, T]),
        ("c_En", [128, 16]), ("c_En2", [16, 128]), ("c_Edq", [128, 8]),
        ("c_h0re", [16, 1]), ("c_h0im", [16, 1]),
    ]
    names_bf = [("c_ones_bf", [128, 128]), ("c_bd64_bf", [128, 128]), ("c_rot_bf", [128, 128])]
    for n, shp in names_f32:
        AP_IN[n] = nc.dram_tensor(n, shp, F32, kind="ExternalInput").ap()
    for n, shp in names_bf:
        AP_IN[n] = nc.dram_tensor(n, shp, BF16, kind="ExternalInput").ap()
    x_out = nc.dram_tensor("x_out", [D, T], F32, kind="ExternalOutput").ap()
    d_xsp = nc.dram_tensor("d_xsp", [D, T], F32, kind="Internal").ap()
    d_z = nc.dram_tensor("d_z", [D, T], F32, kind="Internal").ap()
    d_dt = nc.dram_tensor("d_dt", [T, D], F32, kind="Internal").ap()
    d_dtx = nc.dram_tensor("d_dtx", [D, T], F32, kind="Internal").ap()
    d_dtx2 = nc.dram_tensor("d_dtx2", [T, D], F32, kind="Internal").ap()
    d_xp = nc.dram_tensor("d_xp", [128, T], F32, kind="Internal").ap()
    d_xisp = nc.dram_tensor("d_xisp", [D, T], F32, kind="Internal").ap()
    d_ff = nc.dram_tensor("d_ff", [2816, T], BF16, kind="Internal").ap()

    with ExitStack() as es:
        kb = KB(nc, es)
        g = Glob(kb, AP_IN, dict(x_out=x_out, d_xsp=d_xsp, d_z=d_z, d_dt=d_dt,
                                 d_dtx=d_dtx, d_dtx2=d_dtx2, d_xp=d_xp, d_xisp=d_xisp, d_ff=d_ff))
        emit_program(g, phase)
    return nc


class Glob:

    def __init__(self, kb, IN, DRAM):
        self.kb = kb
        self.IN = IN
        self.DRAM = DRAM
        nc = kb.nc
        self.nc = nc
        self.XH = kb.sb("XH", [128, GT * (T + 1)], BF16)
        self.AF = kb.sb("AF", [128, 16904], F32)
        self.AB = kb.sb("AB", [128, 26624], BF16)
        self.LNS = kb.sb("LNS", [128, 1536], F32)
        self.SQ = kb.sb("SQ", [128, 1024], F32)
        self.CONST = kb.sb("CONST", [128, 1024], F32)
        self.CONSTB = kb.sb("CONSTB", [128, 512], BF16)
        self.ROT = kb.sb("ROT", [128, 4 * T], BF16)
        self.SMALL = kb.sb("SMALL", [128, 256], F32)
        self.WB = kb.sb("WB", [128, 8192], BF16)
        self.PS = [kb.ps(f"PS{i}", [128, 512], F32) for i in range(8)]
        self.PT = self.PS[7][:, 0:128]
        self.PT2 = self.PS[6]

    def ident(self):
        return self.CONST[:, 0:128]

    def ones(self):
        return self.CONST[:, 128:256]

    def bd64(self):
        return self.CONST[:, 256:384]

    def rot(self):
        return self.CONST[:, 384:512]

    def mlo(self):
        return self.CONST[:, 512:640]

    def mhi(self):
        return self.CONST[:, 640:768]

    def En(self):
        return self.CONST[:, 768:784]

    def Edq(self):
        return self.CONST[:, 784:792]

    def En2(self):
        return self.CONST[0:16, 800:928]

    def onesb(self):
        return self.CONSTB[:, 0:128]

    def bd64b(self):
        return self.CONSTB[:, 128:256]

    def rotb(self):
        return self.CONSTB[:, 256:384]

    def xh_tile(self, k, shift=False):
        base = k * (T + 1)
        off = 0 if shift else 1
        return self.XH[:, base + off: base + off + T]


NCH = 4
CH = 512

AF_X = 0
AF_X_END = 16392
AF_SCR = 16392


def af_xtile(g, gl, shift=False, pad=False):
    base = AF_X + gl * 2049 if False else AF_X + g * 2049
    if pad:
        return gl.AF[:, base: base + 2049]
    off = 0 if shift else 1
    return gl.AF[:, base + off: base + off + T]


class P:
    pass


def emit_program(g, phase):
    global _LN_DEBUG
    kb, nc = g.kb, g.nc
    IN, DRAM = g.IN, g.DRAM
    sy, gp, tn, vc, sc = nc.sync, nc.gpsimd, nc.tensor, nc.vector, nc.scalar

    def xt(gi, shift=False):
        base = AF_X + gi * 2049
        off = 0 if shift else 1
        return g.AF[:, base + off: base + off + T]

    def xt_pad(gi):
        return g.AF[:, AF_X + gi * 2049: AF_X + gi * 2049 + 1]

    def dma(eng, out, in_, cnt="gld"):
        kb.inc(eng.dma_start(out=out, in_=in_), cnt, 16)

    dma(sy, g.CONST[:, 0:128], IN["c_ident"])
    dma(sy, g.CONST[:, 128:256], IN["c_ones"])
    dma(sy, g.CONST[:, 256:384], IN["c_bd64"])
    dma(sy, g.CONST[:, 384:512], IN["c_rot"])
    dma(sy, g.CONST[:, 512:640], IN["c_mlo"])
    dma(sy, g.CONST[:, 640:768], IN["c_mhi"])
    dma(sy, g.CONST[:, 768:784], IN["c_En"])
    dma(sy, g.CONST[:, 784:792], IN["c_Edq"])
    dma(sy, g.CONST[0:16, 800:928], IN["c_En2"])
    dma(sy, g.CONST[:, 928:929], IN["c_halfcol"])
    dma(sy, g.CONSTB[:, 0:128], IN["c_ones_bf"])
    dma(sy, g.CONSTB[:, 128:256], IN["c_bd64_bf"])
    dma(sy, g.CONSTB[:, 256:384], IN["c_rot_bf"])
    for i, nm in enumerate(["qs0", "ks0", "qs1", "ks1"]):
        src = IN[nm].unsqueeze(0).broadcast_to([2, DH, 1])
        dma(gp, g.SMALL[:, i:i + 1], src, "gpl")
    dma(sy, g.SMALL[:, 8:16], IN["b_bdt"])
    dma(sy, g.SMALL[:, 16:24], IN["b_bout"])
    dma(sy, g.SMALL[:, 24:32], IN["b_f20"])
    dma(sy, g.SMALL[:, 32:40], IN["b_f21"])
    dma(sy, g.SMALL[:, 40:48], IN["lg0"])
    dma(sy, g.SMALL[:, 48:56], IN["lg1"])
    dma(sy, g.SMALL[:, 56:64], IN["lb0"])
    dma(sy, g.SMALL[:, 64:72], IN["lb1"])
    dma(sy, g.SMALL[:, 72:94], IN["b_f1a0"])
    dma(sy, g.SMALL[:, 94:116], IN["b_f1g0"])
    dma(sy, g.SMALL[:, 116:138], IN["b_f1a1"])
    dma(sy, g.SMALL[:, 138:160], IN["b_f1g1"])
    dma(sy, g.SMALL[0:16, 160:161], IN["c_h0re"])
    dma(sy, g.SMALL[0:16, 161:162], IN["c_h0im"])
    dma(sy, g.SMALL[0:16, 162:163], IN["a_log"])

    QS8 = [g.SMALL[:, 0:1], g.SMALL[:, 2:3]]
    KS = [g.SMALL[:, 1:2], g.SMALL[:, 3:4]]
    LG = [g.SMALL[:, 40:48], g.SMALL[:, 48:56]]
    LB = [g.SMALL[:, 56:64], g.SMALL[:, 64:72]]
    B2 = [g.SMALL[:, 24:32], g.SMALL[:, 32:40]]
    B1A = [g.SMALL[:, 72:94], g.SMALL[:, 116:138]]
    B1G = [g.SMALL[:, 94:116], g.SMALL[:, 138:160]]

    CB = {}
    _cb_vals = [("eps5", 1e-5), ("n16", -16.0), ("lnbt", (0.4 * DH) / (1.4 * DH)),
                ("eps8", 1e-8), ("eps10", 1e-10), ("one", 1.0)]
    for _i, (_nm, _v) in enumerate(_cb_vals):
        col = 168 + _i
        kb.inc(gp.memset(g.SMALL[:, col:col + 1], _v), "gvv")
        CB[_nm] = g.SMALL[:, col:col + 1]

    TR = g.AF[:, 0:2048]
    ANG = g.AF[:, 2048:4096]
    KF = g.AF[:, 4096:6144]
    SN = g.AF[:, 6144:8192]
    CS = g.AF[:, 8192:10240]
    POW = g.AF[:, 10240:12288]
    SCL = g.AF[:, 12288:14336]
    ISCL = g.AF[:, 14336:16384]
    ivf = g.SMALL[:, 164:165]
    lnb = g.SMALL[:, 165:166]
    dma(gp, TR, IN["c_tramp"].unsqueeze(0).broadcast_to([128, 1, T]), "gpl")
    kb.wait(sc, "gld")
    kb.inc(sc.activation(ivf, g.CONST[:, 928:929], AF.Exp, scale=-math.log(10000.0) / DH), "ssc")
    kb.wait(sc, "gvv")
    kb.inc(sc.activation(lnb, g.CONST[:, 928:929], AF.Ln, scale=1.0 / (1.4 * DH), bias=CB["lnbt"]), "ssc")
    kb.wait(vc, "gld")
    kb.wait(vc, "gpl")
    kb.wait(vc, "ssc")
    kb.inc(vc.tensor_scalar_mul(g.SMALL[:, 0:1], g.SMALL[:, 0:1], 8.0), "vv")
    kb.inc(vc.tensor_scalar_mul(g.SMALL[:, 2:3], g.SMALL[:, 2:3], 8.0), "vv")
    TWO_PI = 2.0 * math.pi
    kb.inc(vc.tensor_scalar_mul(ANG, TR, ivf), "vv")
    dma(sy, KF, IN["c_krnd"])
    kb.wait(vc, "gld")
    kb.inc(vc.scalar_tensor_tensor(ANG, KF, -TWO_PI, ANG, ALU.mult, ALU.add), "vv")
    kb.inc(vc.tensor_scalar(ANG, ANG, math.pi, -math.pi, ALU.min, ALU.max), "vv")
    kb.wait(sc, "vv")
    kb.inc(sc.activation(SN, ANG, AF.Sin), "ssc")
    kb.inc(vc.tensor_scalar_add(CS, ANG, math.pi / 2.0), "vv")
    kb.inc(vc.tensor_single_scalar(KF, CS, math.pi, ALU.is_gt), "vv")
    kb.inc(vc.scalar_tensor_tensor(CS, KF, -TWO_PI, CS, ALU.mult, ALU.add), "vv")
    kb.wait(sc, "vv")
    kb.inc(sc.activation(CS, CS, AF.Sin), "ssc")
    kb.inc(vc.tensor_scalar(POW, TR, 1.0 / (WIN // 2), -(T // 2) / (WIN // 2), ALU.mult, ALU.add), "vv")
    kb.inc(vc.tensor_scalar_mul(POW, POW, lnb), "vv")
    kb.wait(sc, "vv")
    kb.inc(sc.activation(SCL, POW, AF.Exp), "ssc")
    kb.inc(sc.activation(ISCL, POW, AF.Exp, scale=-1.0), "ssc")
    kb.wait(vc, "ssc")
    kb.inc(vc.tensor_mul(g.ROT[:, 0:T], CS, SCL), "vv")
    kb.inc(vc.tensor_mul(g.ROT[:, T:2 * T], SN, SCL), "vv")
    kb.inc(vc.tensor_mul(g.ROT[:, 2 * T:3 * T], CS, ISCL), "vv")
    kb.inc(vc.tensor_mul(g.ROT[:, 3 * T:4 * T], SN, ISCL), "vv")

    if phase == "rot":
        kb.wait(vc, "vv")
        for i4 in range(4):
            kb.inc(vc.tensor_copy(g.AF[:, AF_X + i4 * 2049 + 1: AF_X + i4 * 2049 + 1 + T],
                                  g.ROT[:, i4 * T:(i4 + 1) * T]), "vv")
        kb.inc(vc.tensor_copy(g.AF[:, AF_X + 4 * 2049 + 1: AF_X + 4 * 2049 + 1 + T], ANG), "vv")
        kb.inc(vc.tensor_copy(g.AF[:, AF_X + 5 * 2049 + 1: AF_X + 5 * 2049 + 1 + T], SN), "vv")
        kb.inc(vc.tensor_copy(g.AF[:, AF_X + 6 * 2049 + 1: AF_X + 6 * 2049 + 1 + T], SCL), "vv")
        kb.wait(gp, "vv")
        for gi in range(GT):
            kb.inc(gp.dma_start(out=DRAM["x_out"][gi * 128:(gi + 1) * 128, :],
                                in_=g.AF[:, AF_X + gi * 2049 + 1: AF_X + gi * 2049 + 1 + T]), "gout", 16)
        gp.wait_ge(kb.sem("gout"), kb.cnt["gout"])
        return

    kb.wait(gp, "vv")
    kb.wait(sy, "vv")
    for gi in range(GT):
        kb.inc(gp.memset(xt_pad(gi), 0.0), "gvv")
        dma(sy, xt(gi), IN["x_in"][gi * 128:(gi + 1) * 128, :])
    for gi in range(GT):
        kb.inc(gp.memset(g.XH[:, gi * (T + 1):gi * (T + 1) + 1], 0.0), "gvv")

    def emit_finish_from_x_early():
        kb.wait(gp, "vv")
        kb.wait(gp, "ssc")
        kb.wait(gp, "mm")
        for gi in range(GT):
            kb.inc(gp.dma_start(out=DRAM["x_out"][gi * 128:(gi + 1) * 128, :], in_=xt(gi)), "gout", 16)
        gp.wait_ge(kb.sem("gout"), kb.cnt["gout"])

    def emit_ln(src_fn, out_bf_fn, out_f32_fn=None, wait_store=False, gamma=None):
        mu = g.LNS[:, 0:CH]
        var = g.LNS[:, CH:2 * CH]
        rstd = g.LNS[:, 2 * CH:3 * CH]
        kb.wait(tn, "gld")
        kb.wait(vc, "gld")
        for c in range(NCH):
            sl = slice(c * CH, (c + 1) * CH)
            psS, psQ = g.PS[0], g.PS[1]
            kb.wait(tn, "vv")
            kb.wait(tn, "ssc")
            for k in range(GT):
                kb.inc(tn.matmul(psS[:, 0:CH], g.ones(), src_fn(k)[:, sl],
                                 start=(k == 0), stop=(k == GT - 1)), "mm")
            for k in range(GT):
                kb.wait(vc, "mm")
                kb.inc(vc.tensor_mul(g.SQ[:, 0:CH], src_fn(k)[:, sl], src_fn(k)[:, sl]), "vv")
                kb.wait(tn, "vv")
                kb.inc(tn.matmul(psQ[:, 0:CH], g.ones(), g.SQ[:, 0:CH],
                                 start=(k == 0), stop=(k == GT - 1)), "mm")
            kb.wait(vc, "mm")
            if _LN_DEBUG and c == 0:
                kb.inc(vc.tensor_copy(xt(0)[:, 0:CH], psS[:, 0:CH] if _LN_DEBUG == "sum" else psQ[:, 0:CH]), "vv")
            kb.inc(vc.tensor_scalar_mul(mu, psS[:, 0:CH], 1.0 / D), "vv")
            kb.inc(vc.tensor_scalar_mul(var, psQ[:, 0:CH], 1.0 / D), "vv")
            kb.inc(vc.tensor_mul(g.SQ[:, 0:CH], mu, mu), "vv")
            kb.inc(vc.tensor_sub(var, var, g.SQ[:, 0:CH]), "vv")
            kb.wait(sc, "vv")
            kb.inc(sc.activation(g.SQ[:, CH:2 * CH], var, AF.Ln, bias=CB["eps5"]), "ssc")
            kb.inc(sc.activation(rstd, g.SQ[:, CH:2 * CH], AF.Exp, scale=-0.5), "ssc")
            kb.wait(vc, "ssc")
            if wait_store:
                kb.wait(vc, "gst")
            for k in range(GT):
                kb.inc(vc.tensor_sub(g.SQ[:, 0:CH], src_fn(k)[:, sl], mu), "vv")
                if out_f32_fn is not None:
                    kb.inc(vc.tensor_mul(out_f32_fn(k)[:, sl], g.SQ[:, 0:CH], rstd), "vv")
                    kb.wait(sc, "vv")
                    kb.inc(sc.copy(out_bf_fn(k)[:, sl], out_f32_fn(k)[:, sl]), "ssc")
                elif gamma is not None:
                    kb.inc(vc.scalar_tensor_tensor(out_bf_fn(k)[:, sl], rstd, gamma[:, k:k + 1],
                                                   g.SQ[:, 0:CH], ALU.mult, ALU.mult), "vv")
                else:
                    kb.inc(vc.tensor_mul(out_bf_fn(k)[:, sl], g.SQ[:, 0:CH], rstd), "vv")

    def xh(k, shift=False):
        return g.xh_tile(k, shift)

    def load_w_mslice(wap, m, wb_off, kt=GT):
        for k in range(kt):
            dma(gp, g.WB[:, wb_off + k * 128: wb_off + (k + 1) * 128],
                wap[k * 128:(k + 1) * 128, m * 128:(m + 1) * 128], "gwt")
        return [g.WB[:, wb_off + k * 128: wb_off + (k + 1) * 128] for k in range(kt)]

    kb.wait(vc, "gld")
    kb.wait(vc, "gvv")
    emit_ln(lambda k: xt(k), lambda k: xh(k))

    if phase == "ln1":
        kb.wait(vc, "ssc")
        if not _LN_DEBUG:
            for gi in range(GT):
                kb.inc(vc.tensor_copy(xt(gi), xh(gi)), "vv")
        emit_finish_from_x_early()
        return

    AB_K = 0
    AB_VAUG = 8192
    AB_Q = 16384
    AB_AO = 18432
    AB_PT = 20480
    AB_VS = 21504

    def vaug_tile(h, tt):
        off = AB_VAUG + (h * 16 + tt) * 64
        return g.AB[:, off: off + 64]

    for m in range(8):
        wts = load_w_mslice(IN["w_akv"], m, 1024 * m)
    kb.wait(tn, "gwt")
    for m in range(8):
        wts = [g.WB[:, 1024 * m + k * 128: 1024 * m + (k + 1) * 128] for k in range(GT)]
        for c in range(NCH):
            sl = slice(c * CH, (c + 1) * CH)
            ps = g.PS[2]
            kb.wait(tn, "vv")
            kb.wait(tn, "ssc")
            for k in range(GT):
                kb.inc(tn.matmul(ps[:, 0:CH], wts[k], xh(k)[:, sl],
                                 start=(k == 0), stop=(k == GT - 1)), "mm")
            if m < 4:
                kraw = g.AB[:, AB_K + m * T: AB_K + (m + 1) * T]
                kb.wait(sc, "mm")
                kb.inc(sc.copy(kraw[:, sl], ps[:, 0:CH]), "ssc")
                kb.wait(vc, "ssc")
                kb.inc(vc.tensor_mul(g.SQ[:, 0:CH], kraw[:, sl], kraw[:, sl]), "vv")
                kb.wait(tn, "vv")
                kb.inc(tn.matmul(g.PS[3][:, 0:CH], g.bd64(), g.SQ[:, 0:CH],
                                 start=True, stop=True), "mm")
                kb.wait(vc, "mm")
                kb.inc(vc.tensor_scalar_max(g.SQ[:, 0:CH], g.PS[3][:, 0:CH], 1e-24), "vv")
                kb.wait(sc, "vv")
                kb.inc(sc.activation(g.SQ[:, CH:2 * CH], g.SQ[:, 0:CH], AF.Ln), "ssc")
                kb.inc(sc.activation(g.SQ[:, 0:CH], g.SQ[:, CH:2 * CH], AF.Exp, scale=-0.5), "ssc")
                kb.wait(vc, "ssc")
                kb.inc(vc.tensor_mul(kraw[:, sl], kraw[:, sl], g.SQ[:, 0:CH]), "vv")
            else:
                h0 = (m - 4) * 2
                vsc = g.AB[:, AB_VS: AB_VS + CH]
                kb.wait(sc, "mm")
                kb.wait(sc, "gst")
                kb.inc(sc.copy(vsc, ps[:, 0:CH]), "ssc")
                kb.wait(sy, "ssc")
                for hh in range(2):
                    for sb4 in range(4):
                        tt = c * 4 + sb4
                        src = vsc[hh * 64:(hh + 1) * 64, sb4 * 128:(sb4 + 1) * 128]
                        dst = vaug_tile(h0 + hh, tt)
                        kb.inc(sy.dma_start(out=dst, in_=src, transpose=True), "gst", 16)

    kb.wait(gp, "mm")
    for m in range(4):
        load_w_mslice(IN["w_aq"], m, 1024 * m)
    for k4 in range(4):
        for mo in range(8):
            dma(gp, g.WB[:, 4096 + k4 * 1024 + mo * 128: 4096 + k4 * 1024 + (mo + 1) * 128],
                IN["w_ao"][k4 * 128:(k4 + 1) * 128, mo * 128:(mo + 1) * 128], "gwt")

    for c in range(NCH):
        sl = slice(c * CH, (c + 1) * CH)
        kb.wait(tn, "gwt")
        for m in range(4):
            wts = [g.WB[:, 1024 * m + k * 128: 1024 * m + (k + 1) * 128] for k in range(GT)]
            ps = g.PS[2]
            kb.wait(tn, "vv")
            kb.wait(tn, "ssc")
            for k in range(GT):
                kb.inc(tn.matmul(ps[:, 0:CH], wts[k], xh(k)[:, sl],
                                 start=(k == 0), stop=(k == GT - 1)), "mm")
            qraw = g.AB[:, AB_Q + m * CH: AB_Q + (m + 1) * CH]
            kb.wait(sc, "mm")
            kb.inc(sc.copy(qraw, ps[:, 0:CH]), "ssc")
            kb.wait(vc, "ssc")
            kb.inc(vc.tensor_mul(g.SQ[:, 0:CH], qraw, qraw), "vv")
            kb.wait(tn, "vv")
            kb.inc(tn.matmul(g.PS[3][:, 0:CH], g.bd64(), g.SQ[:, 0:CH], start=True, stop=True), "mm")
            kb.wait(vc, "mm")
            kb.inc(vc.tensor_scalar_max(g.SQ[:, 0:CH], g.PS[3][:, 0:CH], 1e-24), "vv")
            kb.wait(sc, "vv")
            kb.inc(sc.activation(g.SQ[:, CH:2 * CH], g.SQ[:, 0:CH], AF.Ln), "ssc")
            kb.inc(sc.activation(g.SQ[:, 0:CH], g.SQ[:, CH:2 * CH], AF.Exp, scale=-0.5), "ssc")
            kb.wait(vc, "ssc")
            kb.inc(vc.tensor_mul(qraw, qraw, g.SQ[:, 0:CH]), "vv")
        kb.wait(tn, "gst")
        for h in range(8):
            km, hh = h // 2, h % 2
            hsl = slice(hh * 64, (hh + 1) * 64)
            tp = (0, hh * 64)
            ktile = g.AB[:, AB_K + km * T: AB_K + (km + 1) * T]
            qslice = g.AB[:, AB_Q + km * CH: AB_Q + (km + 1) * CH][hsl, :]
            psO = g.PS[4]
            psR = g.PS[5]
            for tt in range(16):
                psL = g.PT2
                kb.wait(tn, "vv")
                kb.wait(tn, "ssc")
                kb.inc(tn.matmul(psL[:, 0:CH], ktile[hsl, tt * 128:(tt + 1) * 128],
                                 qslice, start=True, stop=True), "mm")
                pt = g.AB[:, AB_PT + (tt % 2) * CH: AB_PT + (tt % 2) * CH + CH]
                kb.wait(sc, "mm")
                kb.inc(sc.activation(pt, psL[:, 0:CH], AF.Exp, scale=16.0, bias=CB["n16"]), "ssc")
                kb.wait(tn, "ssc")
                kb.inc(tn.matmul(psO[hsl, 0:CH], vaug_tile(h, tt), pt,
                                 start=(tt == 0), stop=(tt == 15), tile_position=tp), "mm")
                kb.inc(tn.matmul(psR[hsl, 0:CH], g.CONSTB[:, 0:64], pt,
                                 start=(tt == 0), stop=(tt == 15), tile_position=tp), "mm")
            kb.wait(vc, "mm")
            kb.inc(vc.tensor_scalar_max(g.LNS[hsl, 0:CH], psR[hsl, 0:CH], 1e-6), "vv")
            kb.wait(sc, "vv")
            kb.inc(sc.activation(g.LNS[hsl, CH:2 * CH], g.LNS[hsl, 0:CH], AF.Ln), "ssc")
            kb.inc(sc.activation(g.LNS[hsl, 0:CH], g.LNS[hsl, CH:2 * CH], AF.Exp, scale=-1.0), "ssc")
            kb.wait(vc, "ssc")
            ao = g.AB[:, AB_AO + km * CH: AB_AO + (km + 1) * CH][hsl, :]
            kb.inc(vc.tensor_mul(ao, psO[hsl, 0:CH], g.LNS[hsl, 0:CH]), "vv")
        for mo in range(8):
            ps = g.PS[2]
            kb.wait(tn, "vv")
            kb.wait(tn, "ssc")
            for k4 in range(4):
                wo = g.WB[:, 4096 + k4 * 1024 + mo * 128: 4096 + k4 * 1024 + (mo + 1) * 128]
                aot = g.AB[:, AB_AO + k4 * CH: AB_AO + (k4 + 1) * CH]
                kb.inc(tn.matmul(ps[:, 0:CH], wo, aot, start=(k4 == 0), stop=(k4 == 3)), "mm")
            kb.wait(vc, "mm")
            kb.inc(vc.tensor_add(xt(mo)[:, sl], xt(mo)[:, sl], ps[:, 0:CH]), "vv")

    def emit_ffn_layer(l, sub=None):
        WQKV = IN["w_qkv0"] if l == 0 else IN["w_qkv1"]
        WFO = IN["w_fo0"] if l == 0 else IN["w_fo1"]
        W1A = IN["w_f1a0"] if l == 0 else IN["w_f1a1"]
        W1G = IN["w_f1g0"] if l == 0 else IN["w_f1g1"]
        W2 = IN["w_f20"] if l == 0 else IN["w_f21"]
        qs8, ks = QS8[l], KS[l]
        lg, lb = LG[l], LB[l]
        b1a, b1g = B1A[l], B1G[l]
        b2 = B2[l]

        FB_K = 0
        FB_V = 8192
        FB_Q = 16384
        FB_AO = 18432
        FB_S1 = 20480
        FB_S2 = 20992
        FB_VS = 21504
        FB_PN = 22016
        FB_PT = 23296
        FB_LBB = 24576
        FB_FF = 0
        FB_W = 22528

        def vtok_tile(h, tt):
            off = FB_V + (h * 16 + tt) * 64
            return g.AB[:, off: off + 64]

        ROT_CSQ = g.ROT[:, 0:T]
        ROT_SNQ = g.ROT[:, T:2 * T]
        ROT_CSK = g.ROT[:, 2 * T:3 * T]
        ROT_SNK = g.ROT[:, 3 * T:4 * T]

        def xi(k, shift=False):
            return xt(k, shift)

        kb.wait(gp, "mm")
        for m in range(8):
            load_w_mslice(WQKV, m, 1024 * m)
        kb.wait(tn, "gwt")
        kb.wait(tn, "vv")

        def l2norm_gain_rot(m, c, ps, gaincol, cs_tab, sn_tab, dst_tile_ap):
            sl = slice(c * CH, (c + 1) * CH)
            raw = g.AB[:, FB_VS: FB_VS + CH]
            kb.wait(sc, "mm")
            if _K_DEBUG == 1:
                kb.inc(sc.copy(dst_tile_ap, ps[:, 0:CH]), "ssc")
                return
            kb.inc(sc.copy(raw, ps[:, 0:CH]), "ssc")
            kb.wait(vc, "ssc")
            kb.inc(vc.tensor_mul(g.SQ[:, 0:CH], raw, raw), "vv")
            kb.wait(tn, "vv")
            kb.inc(tn.matmul(g.PS[3][:, 0:CH], g.bd64(), g.SQ[:, 0:CH], start=True, stop=True), "mm")
            kb.wait(vc, "mm")
            kb.inc(vc.tensor_scalar_max(g.SQ[:, 0:CH], g.PS[3][:, 0:CH], 1e-24), "vv")
            kb.wait(sc, "vv")
            kb.inc(sc.activation(g.SQ[:, CH:2 * CH], g.SQ[:, 0:CH], AF.Ln), "ssc")
            kb.inc(sc.activation(g.SQ[:, 0:CH], g.SQ[:, CH:2 * CH], AF.Exp, scale=-0.5), "ssc")
            kb.wait(vc, "ssc")
            s1 = g.AB[:, FB_S1: FB_S1 + CH]
            s2 = g.AB[:, FB_S2: FB_S2 + CH]
            kb.inc(vc.scalar_tensor_tensor(s1, g.SQ[:, 0:CH], gaincol, raw, ALU.mult, ALU.mult), "vv")
            if _K_DEBUG == 2:
                kb.inc(vc.tensor_copy(dst_tile_ap, s1), "vv")
                return
            kb.wait(tn, "vv")
            kb.inc(tn.matmul(g.PT2[:, 0:CH], g.rotb(), s1, start=True, stop=True), "mm")
            kb.wait(vc, "mm")
            kb.inc(vc.tensor_mul(s2, s1, cs_tab[:, sl]), "vv")
            kb.inc(vc.tensor_mul(s1, g.PT2[:, 0:CH], sn_tab[:, sl]), "vv")
            kb.inc(vc.tensor_add(dst_tile_ap, s2, s1), "vv")

        for m in range(4, 12):
            if m < 8:
                wts = [g.WB[:, 1024 * m + k * 128: 1024 * m + (k + 1) * 128] for k in range(GT)]
            for c in range(NCH):
                sl = slice(c * CH, (c + 1) * CH)
                ps = g.PS[2]
                kb.wait(tn, "vv")
                kb.wait(tn, "ssc")
                if m >= 8:
                    wts = [g.WB[:, 1024 * (m - 8) + k * 128: 1024 * (m - 8) + (k + 1) * 128] for k in range(GT)]
                for k in range(GT):
                    kb.inc(tn.matmul(ps[:, 0:CH], wts[k], xh(k, shift=(k >= 4))[:, sl],
                                     start=(k == 0), stop=(k == GT - 1)), "mm")
                if m < 8:
                    km = m - 4
                    ktile = g.AB[:, FB_K + km * T: FB_K + (km + 1) * T]
                    l2norm_gain_rot(m, c, ps, ks, ROT_CSK, ROT_SNK, ktile[:, sl])
                else:
                    h0 = (m - 8) * 2
                    vsc = g.AB[:, FB_VS: FB_VS + CH]
                    kb.wait(sc, "mm")
                    kb.wait(sc, "gst")
                    kb.inc(sc.copy(vsc, ps[:, 0:CH]), "ssc")
                    kb.wait(sy, "ssc")
                    for hh in range(2):
                        for sb4 in range(4):
                            tt = c * 4 + sb4
                            kb.inc(sy.dma_start(out=vtok_tile(h0 + hh, tt),
                                                in_=vsc[hh * 64:(hh + 1) * 64, sb4 * 128:(sb4 + 1) * 128],
                                                transpose=True), "gst", 16)
            if m == 7:
                kb.wait(gp, "mm")
                for mv in range(4):
                    load_w_mslice(WQKV, 8 + mv, 1024 * mv)
                for k4 in range(4):
                    for mo in range(8):
                        dma(gp, g.WB[:, 4096 + k4 * 1024 + mo * 128: 4096 + k4 * 1024 + (mo + 1) * 128],
                            WFO[k4 * 128:(k4 + 1) * 128, mo * 128:(mo + 1) * 128], "gwt")
                kb.wait(tn, "gwt")

        if sub == "K":
            kb.wait(vc, "mm")
            kb.wait(vc, "ssc")
            for km in range(4):
                kb.inc(vc.tensor_copy(xt(km), g.AB[:, FB_K + km * T: FB_K + (km + 1) * T]), "vv")
            return

        kb.wait(gp, "mm")
        for m in range(4):
            load_w_mslice(WQKV, m, 1024 * m)
        kb.wait(tn, "gwt")
        for w in range(NW):
            wsl = slice(w * WIN, (w + 1) * WIN)
            for m in range(4):
                wts = [g.WB[:, 1024 * m + k * 128: 1024 * m + (k + 1) * 128] for k in range(GT)]
                ps = g.PS[2]
                kb.wait(tn, "vv")
                kb.wait(tn, "ssc")
                for k in range(GT):
                    kb.inc(tn.matmul(ps[:, 0:CH], wts[k], xh(k, shift=(k >= 4))[:, wsl],
                                     start=(k == 0), stop=(k == GT - 1)), "mm")
                qtile = g.AB[:, FB_Q + m * CH: FB_Q + (m + 1) * CH]
                l2norm_gain_rot(m, w, ps, qs8, ROT_CSQ, ROT_SNQ, qtile)
            for Tt0 in range(1):
                pass
            for km in range(4):
                ktile = g.AB[:, FB_K + km * T: FB_K + (km + 1) * T]
                for Tt in range(4):
                    gstart = w * WIN + Tt * 128
                    if w == 0:
                        blocks = list(range(Tt + 1))
                        nb_ = len(blocks)
                    else:
                        blocks = [(w - 1) * 4 + Tt + j for j in range(5)]
                        nb_ = 5
                    for hh in range(2):
                        h = 2 * km + hh
                        hsl = slice(hh * 64, (hh + 1) * 64)
                        tp = (0, hh * 64)
                        qsl = g.AB[:, FB_Q + km * CH + Tt * 128: FB_Q + km * CH + (Tt + 1) * 128][hsl, :]
                        kb.wait(tn, "vv")
                        kb.wait(tn, "ssc")
                        for j, bt in enumerate(blocks):
                            out = g.PT if j == nb_ - 1 else g.PS[4][:, j * 128:(j + 1) * 128]
                            kb.inc(tn.matmul(out, qsl, ktile[hsl, bt * 128:(bt + 1) * 128],
                                             start=True, stop=True), "mm")
                        kb.wait(vc, "mm")
                        kb.inc(vc.tensor_add(g.PT, g.PT, g.mhi()), "vv")
                        if w > 0:
                            kb.inc(vc.tensor_add(g.PS[4][:, 0:128], g.PS[4][:, 0:128], g.mlo()), "vv")
                        nm1 = g.LNS[:, 0:1]
                        nm2 = g.LNS[:, 1:2]
                        if nb_ > 1:
                            kb.inc(vc.reduce_max(nm1, g.PS[4][:, 0:(nb_ - 1) * 128],
                                                 axis=mybir.AxisListType.X), "vv")
                            kb.inc(vc.reduce_max(nm2, g.PT, axis=mybir.AxisListType.X), "vv")
                            kb.inc(vc.tensor_max(nm1, nm1, nm2), "vv")
                        else:
                            kb.inc(vc.reduce_max(nm1, g.PT, axis=mybir.AxisListType.X), "vv")
                        kb.inc(vc.tensor_scalar_mul(nm2, nm1, -1.0), "vv")
                        pn = g.AB[:, FB_PN + hh * 640: FB_PN + hh * 640 + 640]
                        rs1 = g.LNS[:, 2:3]
                        rs2 = g.LNS[:, 3:4]
                        kb.wait(sc, "vv")
                        if nb_ > 1:
                            kb.inc(sc.activation(pn[:, 0:(nb_ - 1) * 128], g.PS[4][:, 0:(nb_ - 1) * 128],
                                                 AF.Exp, bias=nm2, accum_out=rs1), "ssc")
                            kb.inc(sc.activation(pn[:, (nb_ - 1) * 128: nb_ * 128], g.PT,
                                                 AF.Exp, bias=nm2, accum_out=rs2), "ssc")
                            kb.wait(vc, "ssc")
                            kb.inc(vc.tensor_add(rs1, rs1, rs2), "vv")
                        else:
                            kb.inc(sc.activation(pn[:, 0:128], g.PT, AF.Exp,
                                                 bias=nm2, accum_out=rs1), "ssc")
                        kb.wait(sc, "vv")
                        kb.inc(sc.activation(rs2, rs1, AF.Ln), "ssc")
                        kb.inc(sc.activation(rs1, rs2, AF.Exp, scale=-1.0), "ssc")
                        kb.wait(vc, "ssc")
                        kb.inc(vc.tensor_scalar_mul(pn[:, 0:nb_ * 128], pn[:, 0:nb_ * 128], rs1), "vv")
                        kb.wait(sy, "vv")
                        ptb = g.AB[:, FB_PT + hh * 640: FB_PT + hh * 640 + 640]
                        for j in range(nb_):
                            kb.inc(sy.dma_start(out=ptb[:, j * 128:(j + 1) * 128],
                                                in_=pn[:, j * 128:(j + 1) * 128], transpose=True), "gst", 16)
                        kb.wait(tn, "gst")
                        kb.wait(tn, "vv")
                        for j, bt in enumerate(blocks):
                            kb.inc(tn.matmul(g.PS[5][hsl, Tt * 128:(Tt + 1) * 128], vtok_tile(h, bt),
                                             ptb[:, j * 128:(j + 1) * 128],
                                             start=(j == 0), stop=(j == nb_ - 1), tile_position=tp), "mm")
                ao = g.AB[:, FB_AO + km * CH: FB_AO + (km + 1) * CH]
                kb.wait(sc, "mm")
                kb.inc(sc.copy(ao, g.PS[5][:, 0:CH]), "ssc")
            kb.wait(tn, "ssc")
            for mo in range(8):
                ps = g.PS[2]
                kb.wait(tn, "vv")
                kb.wait(tn, "ssc")
                for k4 in range(4):
                    wo = g.WB[:, 4096 + k4 * 1024 + mo * 128: 4096 + k4 * 1024 + (mo + 1) * 128]
                    aot = g.AB[:, FB_AO + k4 * CH: FB_AO + (k4 + 1) * CH]
                    kb.inc(tn.matmul(ps[:, 0:CH], wo, aot, start=(k4 == 0), stop=(k4 == 3)), "mm")
                kb.wait(vc, "mm")
                kb.inc(vc.tensor_add(xi(mo)[:, wsl], xi(mo)[:, wsl], ps[:, 0:CH]), "vv")
        if sub == "A":
            return

        emit_ln(lambda k: xi(k, shift=(k >= 4)), lambda k: xh(k), gamma=lg)
        if sub == "HN":
            kb.wait(vc, "ssc")
            for gi in range(GT):
                kb.inc(vc.tensor_copy(xt(gi), xh(gi)), "vv")
            return

        FB_FF = 0
        FB_W1A = 22528
        FB_W1G = 23552
        FB_GL = 24576
        lbb = g.CONSTB[:, 384:392]
        kb.wait(sc, "vv")
        kb.inc(sc.copy(lbb, lb), "ssc")
        NH = 22
        bca = g.SMALL[:, 176:198]
        bcg = g.SMALL[:, 198:220]
        d_ff = DRAM["d_ff"]
        for th in range(2):
            for m in range(NH):
                kb.wait(gp, "mm")
                kb.wait(gp, "gst")
                for k in range(GT):
                    dma(gp, g.AB[:, FB_W1A + k * 128: FB_W1A + (k + 1) * 128],
                        W1A[k * 128:(k + 1) * 128, m * 128:(m + 1) * 128], "gwt")
                    dma(gp, g.AB[:, FB_W1G + k * 128: FB_W1G + (k + 1) * 128],
                        W1G[k * 128:(k + 1) * 128, m * 128:(m + 1) * 128], "gwt")
                kb.wait(tn, "gwt")
                if th == 0:
                    kb.wait(tn, "vv")
                    kb.wait(tn, "ssc")
                    for k in range(GT):
                        kb.inc(tn.matmul(g.PT[:, 0:1], g.AB[:, FB_W1A + k * 128: FB_W1A + (k + 1) * 128],
                                         lbb[:, k:k + 1], start=(k == 0), stop=(k == GT - 1)), "mm")
                    for k in range(GT):
                        kb.inc(tn.matmul(g.PT[:, 1:2], g.AB[:, FB_W1G + k * 128: FB_W1G + (k + 1) * 128],
                                         lbb[:, k:k + 1], start=(k == 0), stop=(k == GT - 1)), "mm")
                    kb.wait(vc, "mm")
                    kb.inc(vc.tensor_add(bca[:, m:m + 1], g.PT[:, 0:1], b1a[:, m:m + 1]), "vv")
                    kb.inc(vc.tensor_add(bcg[:, m:m + 1], g.PT[:, 1:2], b1g[:, m:m + 1]), "vv")
                for c2 in range(2):
                    sl = slice(th * 1024 + c2 * CH, th * 1024 + (c2 + 1) * CH)
                    psA, psG = g.PS[0], g.PS[1]
                    kb.wait(tn, "vv")
                    kb.wait(tn, "ssc")
                    for k in range(GT):
                        kb.inc(tn.matmul(psG[:, 0:CH], g.AB[:, FB_W1G + k * 128: FB_W1G + (k + 1) * 128],
                                         xh(k)[:, sl], start=(k == 0), stop=(k == GT - 1)), "mm")
                    for k in range(GT):
                        kb.inc(tn.matmul(psA[:, 0:CH], g.AB[:, FB_W1A + k * 128: FB_W1A + (k + 1) * 128],
                                         xh(k)[:, sl], start=(k == 0), stop=(k == GT - 1)), "mm")
                    gl = g.AB[:, FB_GL: FB_GL + CH]
                    kb.wait(sc, "mm")
                    kb.inc(sc.activation(gl, psG[:, 0:CH], AF.Gelu, bias=bcg[:, m:m + 1]), "ssc")
                    ffsl = g.AB[:, FB_FF + m * 1024 + c2 * CH: FB_FF + m * 1024 + (c2 + 1) * CH]
                    kb.wait(vc, "ssc")
                    kb.inc(vc.scalar_tensor_tensor(ffsl, psA[:, 0:CH], bca[:, m:m + 1], gl,
                                                   ALU.add, ALU.mult), "vv")
            kb.wait(sy, "vv")
            for m in range(NH):
                kb.inc(sy.dma_start(out=d_ff[m * 128:(m + 1) * 128, th * 1024:(th + 1) * 1024],
                                    in_=g.AB[:, FB_FF + m * 1024: FB_FF + (m + 1) * 1024]), "gst", 16)
        FB_FR = 0
        FB_W2 = 4096
        kb.wait(gp, "gst")
        for q in range(4):
            mos = [2 * q, 2 * q + 1]
            for m in range(NH):
                buf = m % 2
                kb.wait(gp, "mm")
                dma(gp, g.AB[:, FB_FR + buf * 2048: FB_FR + (buf + 1) * 2048],
                    d_ff[m * 128:(m + 1) * 128, :], "gwt")
                dma(gp, g.AB[:, FB_W2 + buf * 256: FB_W2 + buf * 256 + 256],
                    W2[m * 128:(m + 1) * 128, q * 256:(q + 1) * 256], "gwt")
                kb.wait(tn, "gwt")
                kb.wait(tn, "vv")
                for dd in range(2):
                    for c4 in range(4):
                        bank = g.PS[dd * 4 + c4]
                        kb.inc(tn.matmul(bank[:, 0:CH],
                                         g.AB[:, FB_W2 + buf * 256 + dd * 128: FB_W2 + buf * 256 + (dd + 1) * 128],
                                         g.AB[:, FB_FR + buf * 2048 + c4 * CH: FB_FR + buf * 2048 + (c4 + 1) * CH],
                                         start=(m == 0), stop=(m == NH - 1)), "mm")
                if m == NH - 1:
                    kb.wait(vc, "mm")
                    for dd in range(2):
                        mo = mos[dd]
                        for c4 in range(4):
                            sl = slice(c4 * CH, (c4 + 1) * CH)
                            kb.inc(vc.scalar_tensor_tensor(xi(mo)[:, sl], g.PS[dd * 4 + c4][:, 0:CH],
                                                           b2[:, mo:mo + 1], xi(mo)[:, sl],
                                                           ALU.add, ALU.add), "vv")

    def emit_attn_b():
        emit_ln(lambda k: xt(k), lambda k: xh(k))
        kb.wait(sy, "vv")
        kb.wait(sy, "ssc")
        for gi in range(GT):
            kb.inc(sy.dma_start(out=DRAM["d_xsp"][gi * 128:(gi + 1) * 128, :], in_=xt(gi)), "gst", 16)

        AB_XIN = 0
        W_IN = IN["w_bin"]
        fstage = g.SQ[:, 0:CH]
        for m in range(16):
            kb.wait(gp, "mm")
            wts = load_w_mslice(W_IN, m, 0)
            kb.wait(tn, "gwt")
            for c in range(NCH):
                sl = slice(c * CH, (c + 1) * CH)
                ps = g.PS[2]
                kb.wait(tn, "vv")
                kb.wait(tn, "ssc")
                for k in range(GT):
                    kb.inc(tn.matmul(ps[:, 0:CH], wts[k], xh(k)[:, sl],
                                     start=(k == 0), stop=(k == GT - 1)), "mm")
                if m < 8:
                    xin_t = g.AB[:, AB_XIN + m * T: AB_XIN + (m + 1) * T]
                    kb.wait(sc, "mm")
                    kb.inc(sc.activation(xin_t[:, sl], ps[:, 0:CH], AF.Silu), "ssc")
                    kb.wait(sc, "gst")
                    kb.inc(sc.activation(fstage, ps[:, 0:CH], AF.Silu), "ssc")
                    if phase == "attn_bxin":
                        kb.inc(sc.copy(xt(m)[:, sl], fstage), "ssc")
                    kb.wait(sy, "ssc")
                    kb.inc(sy.dma_start(out=DRAM["d_dtx"][m * 128:(m + 1) * 128, sl], in_=fstage), "gst", 16)
                else:
                    kb.wait(sc, "mm")
                    kb.wait(sc, "gst")
                    kb.inc(sc.copy(fstage, ps[:, 0:CH]), "ssc")
                    kb.wait(sy, "ssc")
                    kb.inc(sy.dma_start(out=DRAM["d_z"][(m - 8) * 128:(m - 7) * 128, sl], in_=fstage), "gst", 16)
        if phase == "attn_bxin":
            return

        AB_XP = 16384
        XPB = g.AB[:, AB_XP: AB_XP + T]
        kb.wait(gp, "mm")
        for k in range(GT):
            dma(gp, g.WB[:, k * 128: (k + 1) * 128], IN["w_bxp"][k * 128:(k + 1) * 128, :], "gwt")
        kb.wait(tn, "gwt")
        for c in range(NCH):
            sl = slice(c * CH, (c + 1) * CH)
            ps = g.PS[2]
            kb.wait(tn, "vv")
            kb.wait(tn, "ssc")
            for k in range(GT):
                kb.inc(tn.matmul(ps[:, 0:CH], g.WB[:, k * 128:(k + 1) * 128],
                                 g.AB[:, AB_XIN + k * T: AB_XIN + k * T + T][:, sl],
                                 start=(k == 0), stop=(k == GT - 1)), "mm")
            kb.wait(sc, "mm")
            kb.inc(sc.copy(XPB[:, sl], ps[:, 0:CH]), "ssc")
            kb.wait(sc, "gst")
            kb.inc(sc.copy(fstage, ps[:, 0:CH]), "ssc")
            kb.wait(sy, "ssc")
            kb.inc(sy.dma_start(out=DRAM["d_xp"][:, sl], in_=fstage), "gst", 16)

        NXR = g.XH[0:16, 0:8192].bitcast(F32)
        kb.wait(gp, "mm")
        for k in range(GT):
            dma(gp, g.WB[:, 1024 + k * 64: 1024 + (k + 1) * 64], IN["w_bres"][k * 128:(k + 1) * 128, :], "gwt")
        kb.wait(tn, "gwt")
        U = g.LNS[0:16, 0:CH]
        V = g.LNS[0:16, CH:2 * CH]
        W2 = g.LNS[0:16, 2 * CH:3 * CH]
        for c in range(NCH):
            sl = slice(c * CH, (c + 1) * CH)
            ps = g.PS[3]
            ps2 = g.PS[2]
            kb.wait(tn, "vv")
            kb.wait(tn, "ssc")
            for k in range(GT):
                kb.inc(tn.matmul(ps[0:64, 0:CH], g.WB[:, 1024 + k * 64: 1024 + (k + 1) * 64],
                                 g.AB[:, AB_XIN + k * T: AB_XIN + k * T + T][:, sl],
                                 start=(k == 0), stop=(k == GT - 1)), "mm")
            kb.wait(vc, "mm")
            kb.inc(vc.tensor_scalar(U, ps[0:16, 0:CH], 1e-10, None, ALU.add), "vv")
            kb.wait(sc, "mm")
            kb.inc(sc.copy(V, ps[32:48, 0:CH]), "ssc")
            kb.wait(vc, "ssc")
            kb.inc(vc.tensor_mul(W2, U, U), "vv")
            kb.inc(vc.tensor_mul(ps2[0:16, 0:CH], V, V), "vv")
            kb.inc(vc.tensor_add(W2, W2, ps2[0:16, 0:CH]), "vv")
            kb.wait(sc, "vv")
            kb.inc(sc.activation(W2, W2, AF.Ln), "ssc")
            kb.inc(sc.activation(W2, W2, AF.Exp, scale=-0.5), "ssc")
            kb.wait(vc, "ssc")
            kb.inc(vc.tensor_mul(NXR[:, 2 * c * CH: 2 * (c + 1) * CH: 2], U, W2), "vv")
            kb.inc(vc.tensor_mul(NXR[:, 2 * c * CH + 1: 2 * (c + 1) * CH: 2], V, W2), "vv")
        kb.wait(gp, "mm")
        for ms in range(GT):
            dma(gp, g.WB[0:64, 2048 + ms * 128: 2048 + (ms + 1) * 128],
                IN["w_bdt"][:, ms * 128:(ms + 1) * 128], "gwt")
        kb.wait(tn, "gwt")
        xring = g.AF[:, 16392:16904]
        for ms in range(GT):
            for c in range(NCH):
                sl = slice(c * CH, (c + 1) * CH)
                ps = g.PS[2]
                kb.wait(tn, "vv")
                kb.wait(tn, "ssc")
                kb.inc(tn.matmul(ps[:, 0:CH], g.WB[0:64, 2048 + ms * 128: 2048 + (ms + 1) * 128],
                                 XPB[0:64, :][:, sl], start=True, stop=True), "mm")
                bcol = g.SMALL[:, 8 + ms: 9 + ms]
                t1 = g.SQ[:, 0:CH]
                t2 = g.SQ[:, CH:2 * CH]
                t3 = g.LNS[:, 0:CH]
                kb.wait(vc, "mm")
                kb.inc(vc.tensor_scalar(t1, ps[:, 0:CH], bcol, None, ALU.add), "vv")
                kb.inc(vc.scalar_tensor_tensor(t3, t1, -1.0, t1, ALU.mult, ALU.min), "vv")
                kb.wait(sc, "vv")
                kb.inc(sc.activation(t2, t3, AF.Exp), "ssc")
                kb.inc(sc.activation(t2, t2, AF.Ln, bias=CB["one"]), "ssc")
                kb.wait(vc, "ssc")
                kb.inc(vc.tensor_scalar(t1, ps[:, 0:CH], bcol, 0.0, ALU.add, ALU.max), "vv")
                kb.inc(vc.tensor_add(t1, t1, t2), "vv")
                if phase == "attn_bdt":
                    kb.inc(vc.tensor_copy(xt(ms)[:, sl], t1), "vv")
                kb.wait(gp, "vv")
                dma(gp, xring, DRAM["d_dtx"][ms * 128:(ms + 1) * 128, sl], "gxr")
                kb.wait(vc, "gxr")
                kb.inc(vc.tensor_mul(t2, t1, xring), "vv")
                kb.wait(tn, "vv")
                for which, buf, dram in ((0, t1, DRAM["d_dt"]), (1, t2, DRAM["d_dtx2"])):
                    for j4 in range(4):
                        pt = g.PS[4 + which]
                        kb.inc(tn.transpose(pt[:, 0:128], buf[:, j4 * 128:(j4 + 1) * 128], g.ident()), "mm")
                        kb.wait(sc, "mm")
                        kb.wait(sc, "gst")
                        kb.wait(sc, "gso")
                        kb.inc(sc.copy(fstage[:, 0:128], pt[:, 0:128]), "ssc")
                        dst = dram[c * CH + j4 * 128: c * CH + (j4 + 1) * 128, ms * 128:(ms + 1) * 128]
                        kb.wait(sy, "ssc")
                        kb.inc(sy.dma_start(out=dst, in_=fstage[:, 0:128]), "gst", 16)
                        kb.wait(tn, "gst")

        if phase == "attn_bdt":
            return
        emit_scan()
        if phase == "attn_by":
            kb.wait(vc, "mm")
            for gi in range(GT):
                kb.inc(vc.tensor_copy(xt(gi), g.AB[:, gi * T: (gi + 1) * T]), "vv")
            return
        if phase == "attn_bh":
            XO = DRAM["x_out"]
            kb.wait(gp, "scv")
            kb.wait(gp, "ssc")
            kb.wait(gp, "vv")
            HSd = g.SQ[:, 0:256]
            MUd = g.SQ[:, 256:384]
            AFd = g.AF
            NXRd = g.XH[0:16, 0:8192].bitcast(F32)
            kb.inc(gp.dma_start(out=XO[0:128, 0:1024], in_=g.SQ[:]), "gout", 16)
            kb.inc(gp.dma_start(out=XO[128:256, 0:1536], in_=g.LNS[:]), "gout", 16)
            kb.inc(gp.dma_start(out=XO[256:384, 0:256], in_=g.SMALL[:]), "gout", 16)
            kb.inc(gp.dma_start(out=XO[384:512, :], in_=AFd[:, 4096:6144]), "gout", 16)
            kb.inc(gp.dma_start(out=XO[512:640, :], in_=AFd[:, 6144:8192]), "gout", 16)
            kb.inc(gp.dma_start(out=XO[640:768, :], in_=AFd[:, 8192:10240]), "gout", 16)
            kb.inc(gp.dma_start(out=XO[768:896, :], in_=AFd[:, 10240:12288]), "gout", 16)
            kb.inc(gp.dma_start(out=XO[896:912, :], in_=NXRd[:, 0:2048]), "gout", 16)
            kb.inc(gp.dma_start(out=XO[912:928, :], in_=NXRd[:, 2048:4096]), "gout", 16)
            kb.inc(gp.dma_start(out=XO[928:944, 0:2048], in_=AFd[:, 12288:14336][0:16, :]), "gout", 16)
            TINWd = g.XH[0:16, 8192:16384].bitcast(F32)
            kb.inc(gp.dma_start(out=XO[944:960, 0:2048], in_=TINWd[:, 0:2048]), "gout", 16)
            gp.wait_ge(kb.sem("gout"), kb.cnt["gout"])
            return

        kb.wait(gp, "mm")
        for k in range(GT):
            dma(gp, g.WB[:, k * 1024:(k + 1) * 1024], IN["w_bout"][k * 128:(k + 1) * 128, :], "gwt")
        zscr = g.AB[:, 16384 + 0: 16384 + CH]
        for k in range(GT):
            for c in range(NCH):
                sl = slice(c * CH, (c + 1) * CH)
                kb.wait(gp, "vv")
                dma(gp, xring, DRAM["d_z"][k * 128:(k + 1) * 128, sl], "gxr")
                kb.wait(sc, "gxr")
                kb.inc(sc.activation(zscr, xring, AF.Silu), "ssc")
                kb.wait(vc, "ssc")
                kb.inc(vc.tensor_mul(g.AB[:, k * T: k * T + T][:, sl],
                                     g.AB[:, k * T: k * T + T][:, sl], zscr), "vv")
        kb.wait(tn, "gwt")
        kb.wait(tn, "vv")
        for mo in range(GT):
            for c in range(NCH):
                sl = slice(c * CH, (c + 1) * CH)
                ps = g.PS[2]
                kb.wait(tn, "vv")
                kb.wait(tn, "ssc")
                for k in range(GT):
                    kb.inc(tn.matmul(ps[:, 0:CH], g.WB[:, k * 1024 + mo * 128: k * 1024 + (mo + 1) * 128],
                                     g.AB[:, k * T: k * T + T][:, sl],
                                     start=(k == 0), stop=(k == GT - 1)), "mm")
                kb.wait(gp, "vv")
                dma(gp, xring, DRAM["d_xsp"][mo * 128:(mo + 1) * 128, sl], "gxr")
                kb.wait(vc, "gxr")
                kb.wait(vc, "mm")
                kb.inc(vc.scalar_tensor_tensor(g.SQ[:, 0:CH], ps[:, 0:CH], g.SMALL[:, 16 + mo:17 + mo],
                                               xring, ALU.add, ALU.add), "vv")
                kb.inc(vc.tensor_copy(xt(mo)[:, sl], g.SQ[:, 0:CH]), "vv")

    def emit_scan():
        CHS = 16
        HF = 16
        NCHUNK = T // CHS
        DTA = g.AF[:, 0:2048]
        ALPH = [g.AF[:, 2048:4096]]
        BRE = g.AF[:, 4096:6144]
        BIM = g.AF[:, 6144:8192]
        CRE = g.AF[:, 8192:10240]
        CIM = g.AF[:, 10240:12288]
        DTX = [g.AF[:, 12288 + i * 2048: 12288 + (i + 1) * 2048] for i in range(2)]
        NXR = g.XH[0:16, 0:8192].bitcast(F32)
        HS = g.SQ[:, 0:256]
        MU = g.SQ[:, 256:384]
        UB = g.SQ[:, 384:512]
        U2B = g.SQ[:, 512:640]
        HH = g.SQ[:, 640:896]
        SCR = [g.LNS[:, 256 + i * 128: 256 + (i + 1) * 128] for i in range(10)]
        YB = g.AB[:, 0:16384]
        ANEG = g.SMALL[:, 224:225]
        H0R = g.SMALL[:, 225:226]
        H0I = g.SMALL[:, 226:227]
        CQ = g.SMALL[:, 228:230]
        TIN = g.LNS[0:16, 0:16]
        pcar = g.PS[0][0:16, 0:256]
        pcols = g.PS[1][:, 0:2]
        pys = [g.PS[2], g.PS[3]]

        kb.wait(gp, "vv")
        kb.wait(gp, "ssc")
        dma(gp, ANEG, IN["a_log"].unsqueeze(0).broadcast_to([8, 16, 1]), "gpl")
        dma(gp, H0R, IN["c_h0re"].unsqueeze(0).broadcast_to([8, 16, 1]), "gpl")
        dma(gp, H0I, IN["c_h0im"].unsqueeze(0).broadcast_to([8, 16, 1]), "gpl")

        def bc_dma(dst, row0):
            src = DRAM["d_xp"][row0: row0 + 31: 2, :].unsqueeze(0).broadcast_to([8, 16, T])
            dma(gp, dst, src, "gpl")
        kb.wait(gp, "gst")
        sy.wait_ge(kb.sem("gso"), kb.cnt.get("gso", 0))
        bc_dma(BRE, 64)
        bc_dma(BIM, 65)
        bc_dma(CRE, 96)
        bc_dma(CIM, 97)
        kb.wait(sc, "gpl")
        kb.inc(sc.activation(ANEG, ANEG, AF.Exp), "ssc")
        kb.wait(vc, "ssc")
        kb.wait(vc, "gst")
        kb.inc(vc.tensor_scalar_mul(ANEG, ANEG, -1.0), "vv")
        kb.inc(vc.tensor_copy(HS[:, 0:128], H0R.broadcast_to([128, 128])), "vv")
        kb.inc(vc.tensor_copy(HS[:, 128:256], H0I.broadcast_to([128, 128])), "vv")
        kb.wait(gp, "gst")
        kb.inc(gp.memset(MU[:], 0.01), "gvv")
        kb.wait(vc, "gvv")

        def chunk_dma(ci):
            base = ci * CHS
            src_dt = DRAM["d_dt"].rearrange("t (dq f) -> dq t f", dq=GT)
            src_dx = DRAM["d_dtx2"].rearrange("t (dq f) -> dq t f", dq=GT)
            for dqi in range(GT):
                s1_ = src_dt[dqi, base: base + CHS, :].unsqueeze(0).broadcast_to([16, CHS, 128])
                d1_ = DTA[dqi * 16:(dqi + 1) * 16, :].rearrange("p (t f) -> p t f", t=CHS)
                kb.inc(sy.dma_start(out=d1_, in_=s1_), "gsc", 16)
                s2_ = src_dx[dqi, base: base + CHS, :].unsqueeze(0).broadcast_to([16, CHS, 128])
                d2_ = DTX[ci % 2][dqi * 16:(dqi + 1) * 16, :].rearrange("p (t f) -> p t f", t=CHS)
                kb.inc(sy.dma_start(out=d2_, in_=s2_), "gsc", 16)

        SV = kb.sem("scv")
        SA = kb.sem("sca")
        SM = kb.sem("scm")
        SE = kb.sem("sce")
        nV, nA, nM = 7, 5, 3
        vb = ab = mb = 0
        LN10C = math.log(10.0)
        TINW = g.XH[0:16, 8192:16384].bitcast(F32)
        TW = [TINW[:, j * 128:(j + 1) * 128] for j in range(10)]
        TWn = [TINW[:, j * 128: j * 128 + 8] for j in range(5)]
        DT1n = TINW[:, 1280:1288]
        TIN2 = TINW[:, 1408:1410]
        pcols2 = g.PS[1][:, 0:2]

        chunk_dma(0)
        kb.wait(sc, "gsc")
        kb.inc(sc.activation(ALPH[0], DTA[:], AF.Exp, scale=ANEG), "ssc")
        kb.wait(tn, "vv")
        kb.wait(tn, "gvv")

        def emit_half(ci, hf, slot):
            nonlocal vb, ab, mb
            gstep0 = ci * CHS + hf * HF
            nstep = HF if _STEP_LIM is None else _STEP_LIM
            for i in range(nstep):
                t = gstep0 + i
                tn.wait_ge(SV, nV * gstep0 + i * nV)
                tn.matmul(pcar, g.En(), HS[:], start=True, stop=True).then_inc(SM, 1)
                vc.wait_ge(SM, nM * gstep0 + i * nM + 1)
                vc.reduce_sum(TIN2, pcar.rearrange("p (c f) -> p c f", c=2), axis=mybir.AxisListType.X)
                vc.tensor_scalar(UB[:], HS[:, 0:128], 1e-10, None, ALU.add)
                vc.tensor_mul(SCR[0], UB[:], UB[:])
                vc.tensor_mul(SCR[1], HS[:, 128:256], HS[:, 128:256])
                vc.tensor_add(SCR[0], SCR[0], SCR[1]).then_inc(SV, 1)
                sc.wait_ge(SV, nV * gstep0 + i * nV + 1)
                sc.wait_ge(SM, nM * gstep0 + i * nM)
                sc.activation(SCR[2], MU[:], AF.Square)
                sc.activation(SCR[3], SCR[2], AF.Ln, bias=CB["eps8"])
                sc.activation(SCR[2], SCR[3], AF.Exp, scale=0.5)
                sc.activation(SCR[3], SCR[2], AF.Ln, bias=CB["eps8"])
                sc.activation(SCR[4], SCR[0], AF.Ln).then_inc(SA, 1)
                nu_ap = NXR[:, 2 * t: 2 * t + 1]
                ni_ap = NXR[:, 2 * t + 1: 2 * t + 2]
                vc.wait_ge(SA, max(0, nA * (gstep0 + i) - 3))
                vc.tensor_scalar(TWn[0], TIN2[:, 0:1].broadcast_to([16, 8]), 1.0 / D, 1e-10, ALU.mult, ALU.add)
                vc.tensor_scalar(TWn[1], TIN2[:, 1:2].broadcast_to([16, 8]), 1.0 / D, None, ALU.mult)
                vc.tensor_mul(TWn[2], TWn[0], TWn[0])
                vc.tensor_mul(TWn[3], TWn[1], TWn[1])
                vc.tensor_scalar(DT1n, TWn[0], nu_ap, None, ALU.mult)
                vc.tensor_add(TWn[4], TWn[2], TWn[3])
                vc.scalar_tensor_tensor(TW[9], TWn[1][:, 0:1].broadcast_to([16, 128]), ni_ap,
                                        DT1n[:, 0:1].broadcast_to([16, 128]), ALU.mult, ALU.add).then_inc(SV, 1)
                sc.wait_ge(SV, nV * gstep0 + i * nV + 2)
                sc.activation(TW[5], TWn[4][:, 0:1].broadcast_to([16, 128]), AF.Ln)
                sc.activation(TW[5], TW[5], AF.Exp, scale=-0.5).then_inc(SA, 1)
                vc.wait_ge(SA, nA * gstep0 + i * nA + 2)
                vc.tensor_mul(TW[6], TW[9], TW[5])
                vc.tensor_scalar(TW[7], TW[6], KAPPA / 2, -KAPPA / 2, ALU.mult, ALU.add)
                vc.tensor_scalar(TW[8], TW[6], -KAPPA / 2, 1.0 + KAPPA / 2, ALU.mult, ALU.add).then_inc(SV, 1)
                tn.wait_ge(SV, nV * gstep0 + i * nV + 3)
                tn.matmul(pcols2, g.En2(), TINW[:, 1023:1025], start=True, stop=True).then_inc(SM, 1)
                vc.wait_ge(SM, nM * gstep0 + i * nM + 2)
                alp = ALPH[slot][:, i * 128:(i + 1) * 128]
                vc.tensor_scalar(SCR[5], alp, pcols2[:, 0:1], pcols2[:, 1:2], ALU.mult, ALU.add)
                vc.tensor_mul(SCR[6], SCR[5], SCR[3])
                vc.tensor_scalar(SCR[6], SCR[6], LN10C, None, ALU.min)
                vc.scalar_tensor_tensor(SCR[7], SCR[4], -0.5, SCR[6], ALU.mult, ALU.add).then_inc(SV, 1)
                sc.wait_ge(SV, nV * gstep0 + i * nV + 4)
                sc.activation(SCR[8], SCR[7], AF.Exp).then_inc(SA, 1)
                vc.wait_ge(SA, nA * gstep0 + i * nA + 3)
                dtxs = DTX[ci % 2][:, i * 128:(i + 1) * 128]
                bres = BRE[:, t: t + 1]
                bims = BIM[:, t: t + 1]
                vc.tensor_mul(SCR[9], SCR[8], UB[:])
                vc.scalar_tensor_tensor(HH[:, 0:128], dtxs, bres, SCR[9], ALU.mult, ALU.add)
                vc.tensor_mul(SCR[9], SCR[8], HS[:, 128:256])
                vc.scalar_tensor_tensor(HH[:, 128:256], dtxs, bims, SCR[9], ALU.mult, ALU.add)
                vc.tensor_mul(g.LNS[:, 256:512], HH[:], HH[:])
                vc.tensor_add(SCR[5], SCR[0], SCR[1]).then_inc(SV, 1)
                vc.tensor_scalar(U2B[:], HH[:, 0:128], 1e-10, None, ALU.add)
                vc.tensor_mul(SCR[6], U2B[:], U2B[:])
                vc.tensor_add(SCR[6], SCR[6], SCR[1]).then_inc(SV, 1)
                sc.wait_ge(SV, nV * gstep0 + i * nV + 5)
                sc.activation(SCR[7], SCR[5], AF.Ln, bias=CB["eps8"])
                sc.activation(SCR[5], SCR[7], AF.Exp, scale=0.5)
                sc.activation(SCR[7], SCR[5], AF.Exp, scale=2.0)
                sc.activation(SCR[7], SCR[7], AF.Ln, bias=CB["one"])
                sc.activation(SCR[7], SCR[7], AF.Exp, scale=-1.0).then_inc(SA, 1)
                sc.wait_ge(SV, nV * gstep0 + i * nV + 6)
                sc.activation(SCR[0], SCR[6], AF.Ln)
                sc.activation(SCR[0], SCR[0], AF.Exp, scale=-0.5).then_inc(SA, 1)
                vc.wait_ge(SA, nA * gstep0 + i * nA + 5)
                vc.tensor_scalar(MU[:], SCR[7], -2.0, 1.0, ALU.mult, ALU.add)
                vc.tensor_mul(SCR[1], MU[:], SCR[0])
                vc.tensor_mul(HS[:, 0:128], SCR[1], U2B[:])
                vc.tensor_mul(HS[:, 128:256], SCR[1], HH[:, 128:256])
                cres = CRE[:, t: t + 1]
                cims = CIM[:, t: t + 1]
                edq3 = g.Edq().unsqueeze(1).broadcast_to([128, 16, 8])
                vc.tensor_scalar(SCR[2].rearrange("p (r f) -> p r f", r=16), edq3, cres, None, ALU.mult)
                vc.tensor_scalar(SCR[3].rearrange("p (r f) -> p r f", r=16), edq3, cims, None, ALU.mult).then_inc(SV, 1)
                tn.wait_ge(SV, nV * gstep0 + i * nV + 7)
                ycol = (gstep0 % 64) * 8
                pyb = pys[(gstep0 // 64) % 2]
                oy = pyb[:, ycol + i * 8: ycol + i * 8 + 8]
                tn.matmul(oy, HS[:, 0:128], SCR[2][:, 0:8], start=True, stop=False)
                tn.matmul(oy, HS[:, 128:256], SCR[3][:, 0:8], start=False, stop=True).then_inc(SM, 1)
            vb += nV * nstep
            ab += nA * nstep
            mb += nM * nstep
            kb.cnt["scv"] = vb
            kb.cnt["sca"] = ab
            kb.cnt["scm"] = mb

        NRUN = NCHUNK if _SCAN_LIM is None else _SCAN_LIM
        for ci in range(NRUN):
            if ci % 4 == 0 and ci // 4 >= 2:
                tn.wait_ge(SE, ci // 4 - 1)
            if ci + 1 < NCHUNK:
                sy.wait_ge(kb.sem("ssc"), kb.cnt["ssc"])
                if ci >= 1:
                    sy.wait_ge(SV, vb)
                chunk_dma(ci + 1)
            emit_half(ci, 0, 0)
            if ci + 1 < NCHUNK:
                kb.wait(sc, "gsc")
                sc.wait_ge(SV, vb)
                kb.inc(sc.activation(ALPH[0], DTA[:], AF.Exp, scale=ANEG), "ssc")
            if (ci % 4) == 3:
                blk64 = ci // 4
                bank = blk64 % 2
                sc.wait_ge(SM, mb)
                yout = YB.rearrange("p (dq t) -> p dq t", dq=GT)[:, :, blk64 * 64:(blk64 + 1) * 64]
                yin = pys[bank].rearrange("p (t dq) -> p t dq", dq=GT)
                kb.inc(sc.copy(yout.transpose([0, 2, 1]), yin), "sce")
        kb.wait(vc, "sce")
        kb.wait(vc, "scm", mb)
        kb.wait(sc, "scm", mb)
        kb.wait(tn, "sce")

    def emit_ffn_call():
        kb.wait(sy, "vv")
        kb.wait(sy, "ssc")
        for gi in range(GT):
            kb.inc(sy.dma_start(out=DRAM["d_xsp"][gi * 128:(gi + 1) * 128, :], in_=xt(gi)), "gst", 16)
        if "scv" in kb.cnt:
            kb.wait(gp, "scv")
            kb.wait(gp, "scm")
        kb.wait(gp, "vv")
        kb.wait(gp, "ssc")
        for gi in range(GT):
            kb.inc(gp.memset(g.XH[:, gi * (T + 1):gi * (T + 1) + 1], 0.0), "gvv")
        kb.wait(tn, "gvv")
        emit_ln(lambda k: xt(k), lambda k: xh(k), out_f32_fn=lambda k: xt(k), wait_store=True)
        if phase == "ffnLN2":
            return
        if phase == "xh2":
            kb.wait(vc, "ssc")
            for gi in range(GT):
                kb.inc(vc.tensor_copy(xt(gi), xh(gi)), "vv")
            return
        subm = {"ffnA0": "A", "ffnK": "K", "ffnHN": "HN"}.get(phase)
        emit_ffn_layer(0, sub=subm)
        if phase in ("ffnA0", "ffnL0", "ffnK", "ffnHN"):
            return
        kb.wait(vc, "ssc")
        kb.wait(vc, "mm")
        for gi in range(GT):
            kb.inc(vc.tensor_copy(xh(gi), xt(gi)), "vv")
        if phase == "ffnL0x":
            return
        emit_ffn_layer(1)
        ring = g.AF[:, 16392:16904]
        for gi in range(GT):
            for c in range(NCH):
                sl = slice(c * CH, (c + 1) * CH)
                kb.wait(gp, "vv")
                dma(gp, ring, DRAM["d_xsp"][gi * 128:(gi + 1) * 128, sl], "gxr")
                kb.wait(vc, "gxr")
                kb.inc(vc.tensor_add(xt(gi)[:, sl], xt(gi)[:, sl], ring), "vv")

    if phase == "attn_a":
        emit_finish_from_x_early()
        return
    emit_ffn_call()
    if phase in ("ffn1", "ffnLN2", "ffnA0", "ffnL0", "ffnL0x", "ffnK", "xh2", "ffnHN"):
        emit_finish_from_x_early()
        return
    emit_attn_b()
    if phase == "attn_bh":
        return
    if phase.startswith("attn_b"):
        emit_finish_from_x_early()
        return
    emit_ffn_call()
    emit_finish_from_x_early()

    def emit_finish_from_x():
        kb.wait(gp, "vv")
        kb.wait(gp, "ssc")
        kb.wait(gp, "mm")
        for gi in range(GT):
            dma(gp, DRAM["x_out"][gi * 128:(gi + 1) * 128, :], xt(gi), "gout")
        gp.wait_ge(kb.sem("gout"), kb.cnt["gout"])




def kernel(**inputs):
    in_maps = host_prep(inputs)
    nc = _build(_PHASE)
    res = run_bass_kernel_spmd(nc, in_maps, core_ids=list(range(len(in_maps))))
    out = np.stack([np.ascontiguousarray(np.asarray(r["x_out"], dtype=np.float32).T)
                    for r in res.results], axis=0)
    return out


_PHASE = "full"
_SCAN_LIM = None
_STEP_LIM = None
_LN_DEBUG = None
_K_DEBUG = False


# version 03 (final)
# speedup vs baseline: 1.1583x; 1.0366x over previous
import math
from contextlib import ExitStack

import numpy as np
import ml_dtypes

import concourse.bass as bass
import concourse.mybir as mybir
from concourse.bass_utils import run_bass_kernel_spmd

F32 = mybir.dt.float32
BF16 = mybir.dt.bfloat16
AF = mybir.ActivationFunctionType
ALU = mybir.AluOpType

D = 1024
T = 2048
GT = 8
HEADS = 8
DH = 64
INNER = 512
HID = 2730
WIN = 512
NW = 4
NMODE = 16
KAPPA = 0.5
LN10 = math.log(10.0)

BIGNEG = -1.0e30


def _f32(x):
    return np.ascontiguousarray(x, dtype=np.float32)


def host_constants():
    c = {}
    c["c_ident"] = _f32(np.eye(128))
    c["c_ones"] = _f32(np.ones((128, 128)))
    c["c_ones_bf"] = np.ascontiguousarray(np.ones((128, 128)), dtype=ml_dtypes.bfloat16)
    bd = np.zeros((128, 128), np.float32)
    bd[:64, :64] = 1.0
    bd[64:, 64:] = 1.0
    c["c_bd64"] = _f32(bd)
    c["c_bd64_bf"] = bd.astype(ml_dtypes.bfloat16)
    P = np.zeros((128, 128), np.float32)
    for h0 in (0, 64):
        for i in range(32):
            P[h0 + i + 32, h0 + i] = -1.0
            P[h0 + i, h0 + i + 32] = 1.0
    c["c_rot"] = _f32(P)
    c["c_rot_bf"] = P.astype(ml_dtypes.bfloat16)
    m_lo = np.zeros((128, 128), np.float32)
    m_hi = np.zeros((128, 128), np.float32)
    for p in range(128):
        m_lo[p, :p] = BIGNEG
        m_hi[p, p + 1:] = BIGNEG
    c["c_mlo"] = _f32(m_lo)
    c["c_mhi"] = _f32(m_hi)
    c["c_tramp"] = _f32(np.arange(T)[None, :])
    halfcol = np.array([2 * ((p % 64) % 32) for p in range(128)], np.float32)[:, None]
    c["c_halfcol"] = _f32(halfcol)
    half64 = np.arange(0, 64, 2, dtype=np.float64)
    invf64 = 1.0 / 10000.0 ** (half64 / 64.0)
    fr = np.arange(2048, dtype=np.float64)[:, None] * invf64[None, :]
    jmap = np.array([(p % 64) % 32 for p in range(128)])
    kw = np.round(fr[:, jmap].T / (2 * np.pi))
    c["c_krnd"] = _f32(kw)
    En = np.zeros((128, 16), np.float32)
    for p in range(128):
        En[p, p % 16] = 1.0
    c["c_En"] = _f32(En)
    c["c_En2"] = _f32(En.T.copy())
    Edq = np.zeros((128, 8), np.float32)
    for p in range(128):
        Edq[p, p // 16] = 1.0
    c["c_Edq"] = _f32(Edq)
    phases = np.linspace(0.0, 2.0 * np.pi * (1.0 - 1.0 / NMODE), NMODE)
    h0re = (0.01 * np.cos(phases)).astype(np.float32)
    h0im = (0.01 * np.sin(phases)).astype(np.float32)
    c["c_h0re"] = _f32(h0re[:, None])
    c["c_h0im"] = _f32(h0im[:, None])
    return c


def host_prep(inputs):
    x = np.asarray(inputs["L0_attn_a_x"])
    nb = x.shape[0]
    consts = host_constants()

    def dvec(name):
        v = np.asarray(inputs[name])
        return _f32(v.reshape(GT, 128).T)

    common = {
        "w_aq": _f32(inputs["L0_attn_a_Wq"]),
        "w_akv": _f32(inputs["L0_attn_a_Wkv"]),
        "w_ao": _f32(inputs["L0_attn_a_Wo"]),
        "w_bin": _f32(inputs["L0_attn_b_W_in"]),
        "w_bxp": _f32(inputs["L0_attn_b_W_xproj"]),
        "w_bdt": _f32(inputs["L0_attn_b_W_dt"]),
        "b_bdt": dvec("L0_attn_b_b_dt"),
        "w_bout": _f32(inputs["L0_attn_b_W_out"]),
        "b_bout": dvec("L0_attn_b_b_out"),
        "w_bres": None,
        "a_log": _f32(np.asarray(inputs["L0_attn_b_A_log"])[:, None]),
        "w_qkv0": _f32(inputs["L0_ffn_wqkv"][0]),
        "w_qkv1": _f32(inputs["L0_ffn_wqkv"][1]),
        "qs0": _f32(np.asarray(inputs["L0_ffn_q_scale"][0])[:, None]),
        "qs1": _f32(np.asarray(inputs["L0_ffn_q_scale"][1])[:, None]),
        "ks0": _f32(np.asarray(inputs["L0_ffn_k_scale"][0])[:, None]),
        "ks1": _f32(np.asarray(inputs["L0_ffn_k_scale"][1])[:, None]),
        "w_fo0": _f32(inputs["L0_ffn_wo"][0]),
        "w_fo1": _f32(inputs["L0_ffn_wo"][1]),
        "lg0": _f32(np.asarray(inputs["L0_ffn_ln_g"][0]).reshape(GT, 128).T),
        "lg1": _f32(np.asarray(inputs["L0_ffn_ln_g"][1]).reshape(GT, 128).T),
        "lb0": _f32(np.asarray(inputs["L0_ffn_ln_b"][0]).reshape(GT, 128).T),
        "lb1": _f32(np.asarray(inputs["L0_ffn_ln_b"][1]).reshape(GT, 128).T),
    }
    wres = np.asarray(inputs["L0_attn_b_W_res"], np.float32)
    wra = np.zeros((D, 64), np.float32)
    wra[:, 0:16] = wres[:, 0::2]
    wra[:, 32:48] = wres[:, 1::2]
    common["w_bres"] = _f32(wra)

    HIDP = 2816
    w1 = np.asarray(inputs["L0_ffn_w1"])
    b1 = np.asarray(inputs["L0_ffn_b1"])
    w2 = np.asarray(inputs["L0_ffn_w2"])
    b2 = np.asarray(inputs["L0_ffn_b2"])
    for l in range(2):
        w1a = np.zeros((D, HIDP), np.float32); w1a[:, :HID] = w1[l][:, :HID]
        w1g = np.zeros((D, HIDP), np.float32); w1g[:, :HID] = w1[l][:, HID:]
        b1a = np.zeros((HIDP,), np.float32); b1a[:HID] = b1[l][:HID]
        b1g = np.zeros((HIDP,), np.float32); b1g[:HID] = b1[l][HID:]
        w2p = np.zeros((HIDP, D), np.float32); w2p[:HID, :] = w2[l]
        common[f"w_f1a{l}"] = _f32(w1a)
        common[f"w_f1g{l}"] = _f32(w1g)
        common[f"b_f1a{l}"] = _f32(b1a.reshape(22, 128).T)
        common[f"b_f1g{l}"] = _f32(b1g.reshape(22, 128).T)
        common[f"w_f2{l}"] = _f32(w2p)
        common[f"b_f2{l}"] = _f32(b2[l].reshape(GT, 128).T)

    in_maps = []
    for b in range(nb):
        m = {"x_in": _f32(x[b].T)}
        m.update(common)
        m.update(consts)
        in_maps.append(m)
    return in_maps



class KB:

    def __init__(self, nc, es):
        self.nc = nc
        self.es = es
        self.sems = {}
        self.cnt = {}

    def sem(self, name):
        if name not in self.sems:
            self.sems[name] = self.es.enter_context(self.nc.semaphore(name))
            self.cnt[name] = 0
        return self.sems[name]

    def inc(self, instr, name, n=1):
        instr.then_inc(self.sem(name), n)
        self.cnt[name] += n
        return self.cnt[name]

    def wait(self, eng, name, val=None):
        eng.wait_ge(self.sem(name), self.cnt[name] if val is None else val)

    def sb(self, name, shape, dtype):
        return self.es.enter_context(self.nc.sbuf_tensor(name, list(shape), dtype))

    def ps(self, name, shape, dtype=F32):
        return self.es.enter_context(self.nc.psum_tensor(name, list(shape), dtype))



def _build(phase="full"):
    nc = bass.Bass("TRN2", target_bir_lowering=False, detect_race_conditions=False)
    AP_IN = {}
    names_f32 = [
        ("x_in", [D, T]), ("w_aq", [D, INNER]), ("w_akv", [D, 2 * INNER]), ("w_ao", [INNER, D]),
        ("w_bin", [D, 2 * D]), ("w_bxp", [D, 128]), ("w_bdt", [64, D]), ("b_bdt", [128, GT]),
        ("w_bout", [D, D]), ("b_bout", [128, GT]), ("w_bres", [D, 64]), ("a_log", [16, 1]),
        ("w_qkv0", [D, 3 * INNER]), ("w_qkv1", [D, 3 * INNER]),
        ("qs0", [DH, 1]), ("qs1", [DH, 1]), ("ks0", [DH, 1]), ("ks1", [DH, 1]),
        ("w_fo0", [INNER, D]), ("w_fo1", [INNER, D]),
        ("lg0", [128, GT]), ("lg1", [128, GT]), ("lb0", [128, GT]), ("lb1", [128, GT]),
        ("w_f1a0", [D, 2816]), ("w_f1g0", [D, 2816]), ("w_f1a1", [D, 2816]), ("w_f1g1", [D, 2816]),
        ("b_f1a0", [128, 22]), ("b_f1g0", [128, 22]), ("b_f1a1", [128, 22]), ("b_f1g1", [128, 22]),
        ("w_f20", [2816, D]), ("w_f21", [2816, D]),
        ("b_f20", [128, GT]), ("b_f21", [128, GT]),
        ("c_ident", [128, 128]), ("c_ones", [128, 128]), ("c_bd64", [128, 128]),
        ("c_rot", [128, 128]), ("c_mlo", [128, 128]), ("c_mhi", [128, 128]),
        ("c_tramp", [1, T]), ("c_halfcol", [128, 1]), ("c_krnd", [128, T]),
        ("c_En", [128, 16]), ("c_En2", [16, 128]), ("c_Edq", [128, 8]),
        ("c_h0re", [16, 1]), ("c_h0im", [16, 1]),
    ]
    names_bf = [("c_ones_bf", [128, 128]), ("c_bd64_bf", [128, 128]), ("c_rot_bf", [128, 128])]
    for n, shp in names_f32:
        AP_IN[n] = nc.dram_tensor(n, shp, F32, kind="ExternalInput").ap()
    for n, shp in names_bf:
        AP_IN[n] = nc.dram_tensor(n, shp, BF16, kind="ExternalInput").ap()
    x_out = nc.dram_tensor("x_out", [D, T], F32, kind="ExternalOutput").ap()
    d_xsp = nc.dram_tensor("d_xsp", [D, T], F32, kind="Internal").ap()
    d_z = nc.dram_tensor("d_z", [D, T], F32, kind="Internal").ap()
    d_dt = nc.dram_tensor("d_dt", [T, D], F32, kind="Internal").ap()
    d_dtx = nc.dram_tensor("d_dtx", [D, T], F32, kind="Internal").ap()
    d_dtx2 = nc.dram_tensor("d_dtx2", [T, D], F32, kind="Internal").ap()
    d_xp = nc.dram_tensor("d_xp", [128, T], F32, kind="Internal").ap()
    d_xisp = nc.dram_tensor("d_xisp", [D, T], F32, kind="Internal").ap()
    d_ff = nc.dram_tensor("d_ff", [2816, T], BF16, kind="Internal").ap()

    with ExitStack() as es:
        kb = KB(nc, es)
        g = Glob(kb, AP_IN, dict(x_out=x_out, d_xsp=d_xsp, d_z=d_z, d_dt=d_dt,
                                 d_dtx=d_dtx, d_dtx2=d_dtx2, d_xp=d_xp, d_xisp=d_xisp, d_ff=d_ff))
        emit_program(g, phase)
    return nc


class Glob:

    def __init__(self, kb, IN, DRAM):
        self.kb = kb
        self.IN = IN
        self.DRAM = DRAM
        nc = kb.nc
        self.nc = nc
        self.XH = kb.sb("XH", [128, GT * (T + 1)], BF16)
        self.AF = kb.sb("AF", [128, 16904], F32)
        self.AB = kb.sb("AB", [128, 26624], BF16)
        self.LNS = kb.sb("LNS", [128, 1536], F32)
        self.SQ = kb.sb("SQ", [128, 1024], F32)
        self.CONST = kb.sb("CONST", [128, 1024], F32)
        self.CONSTB = kb.sb("CONSTB", [128, 512], BF16)
        self.ROT = kb.sb("ROT", [128, 4 * T], BF16)
        self.SMALL = kb.sb("SMALL", [128, 256], F32)
        self.WB = kb.sb("WB", [128, 8192], BF16)
        self.PS = [kb.ps(f"PS{i}", [128, 512], F32) for i in range(8)]
        self.PT = self.PS[7][:, 0:128]
        self.PT2 = self.PS[6]

    def ident(self):
        return self.CONST[:, 0:128]

    def ones(self):
        return self.CONST[:, 128:256]

    def bd64(self):
        return self.CONST[:, 256:384]

    def rot(self):
        return self.CONST[:, 384:512]

    def mlo(self):
        return self.CONST[:, 512:640]

    def mhi(self):
        return self.CONST[:, 640:768]

    def En(self):
        return self.CONST[:, 768:784]

    def Edq(self):
        return self.CONST[:, 784:792]

    def En2(self):
        return self.CONST[0:16, 800:928]

    def onesb(self):
        return self.CONSTB[:, 0:128]

    def bd64b(self):
        return self.CONSTB[:, 128:256]

    def rotb(self):
        return self.CONSTB[:, 256:384]

    def xh_tile(self, k, shift=False):
        base = k * (T + 1)
        off = 0 if shift else 1
        return self.XH[:, base + off: base + off + T]


NCH = 4
CH = 512

AF_X = 0
AF_X_END = 16392
AF_SCR = 16392


def af_xtile(g, gl, shift=False, pad=False):
    base = AF_X + gl * 2049 if False else AF_X + g * 2049
    if pad:
        return gl.AF[:, base: base + 2049]
    off = 0 if shift else 1
    return gl.AF[:, base + off: base + off + T]


class P:
    pass


def emit_program(g, phase):
    global _LN_DEBUG
    kb, nc = g.kb, g.nc
    IN, DRAM = g.IN, g.DRAM
    sy, gp, tn, vc, sc = nc.sync, nc.gpsimd, nc.tensor, nc.vector, nc.scalar

    def xt(gi, shift=False):
        base = AF_X + gi * 2049
        off = 0 if shift else 1
        return g.AF[:, base + off: base + off + T]

    def xt_pad(gi):
        return g.AF[:, AF_X + gi * 2049: AF_X + gi * 2049 + 1]

    def dma(eng, out, in_, cnt="gld"):
        kb.inc(eng.dma_start(out=out, in_=in_), cnt, 16)

    dma(sy, g.CONST[:, 0:128], IN["c_ident"])
    dma(sy, g.CONST[:, 128:256], IN["c_ones"])
    dma(sy, g.CONST[:, 256:384], IN["c_bd64"])
    dma(sy, g.CONST[:, 384:512], IN["c_rot"])
    dma(sy, g.CONST[:, 512:640], IN["c_mlo"])
    dma(sy, g.CONST[:, 640:768], IN["c_mhi"])
    dma(sy, g.CONST[:, 768:784], IN["c_En"])
    dma(sy, g.CONST[:, 784:792], IN["c_Edq"])
    dma(sy, g.CONST[0:16, 800:928], IN["c_En2"])
    dma(sy, g.CONST[:, 928:929], IN["c_halfcol"])
    dma(sy, g.CONSTB[:, 0:128], IN["c_ones_bf"])
    dma(sy, g.CONSTB[:, 128:256], IN["c_bd64_bf"])
    dma(sy, g.CONSTB[:, 256:384], IN["c_rot_bf"])
    for i, nm in enumerate(["qs0", "ks0", "qs1", "ks1"]):
        src = IN[nm].unsqueeze(0).broadcast_to([2, DH, 1])
        dma(gp, g.SMALL[:, i:i + 1], src, "gpl")
    dma(sy, g.SMALL[:, 8:16], IN["b_bdt"])
    dma(sy, g.SMALL[:, 16:24], IN["b_bout"])
    dma(sy, g.SMALL[:, 24:32], IN["b_f20"])
    dma(sy, g.SMALL[:, 32:40], IN["b_f21"])
    dma(sy, g.SMALL[:, 40:48], IN["lg0"])
    dma(sy, g.SMALL[:, 48:56], IN["lg1"])
    dma(sy, g.SMALL[:, 56:64], IN["lb0"])
    dma(sy, g.SMALL[:, 64:72], IN["lb1"])
    dma(sy, g.SMALL[:, 72:94], IN["b_f1a0"])
    dma(sy, g.SMALL[:, 94:116], IN["b_f1g0"])
    dma(sy, g.SMALL[:, 116:138], IN["b_f1a1"])
    dma(sy, g.SMALL[:, 138:160], IN["b_f1g1"])
    dma(sy, g.SMALL[0:16, 160:161], IN["c_h0re"])
    dma(sy, g.SMALL[0:16, 161:162], IN["c_h0im"])
    dma(sy, g.SMALL[0:16, 162:163], IN["a_log"])

    QS8 = [g.SMALL[:, 0:1], g.SMALL[:, 2:3]]
    KS = [g.SMALL[:, 1:2], g.SMALL[:, 3:4]]
    LG = [g.SMALL[:, 40:48], g.SMALL[:, 48:56]]
    LB = [g.SMALL[:, 56:64], g.SMALL[:, 64:72]]
    B2 = [g.SMALL[:, 24:32], g.SMALL[:, 32:40]]
    B1A = [g.SMALL[:, 72:94], g.SMALL[:, 116:138]]
    B1G = [g.SMALL[:, 94:116], g.SMALL[:, 138:160]]

    CB = {}
    _cb_vals = [("eps5", 1e-5), ("n16", -16.0), ("lnbt", (0.4 * DH) / (1.4 * DH)),
                ("eps8", 1e-8), ("eps10", 1e-10), ("one", 1.0), ("eps20", 1e-20)]
    for _i, (_nm, _v) in enumerate(_cb_vals):
        col = 168 + _i
        kb.inc(gp.memset(g.SMALL[:, col:col + 1], _v), "gvv")
        CB[_nm] = g.SMALL[:, col:col + 1]

    TR = g.AF[:, 0:2048]
    ANG = g.AF[:, 2048:4096]
    KF = g.AF[:, 4096:6144]
    SN = g.AF[:, 6144:8192]
    CS = g.AF[:, 8192:10240]
    POW = g.AF[:, 10240:12288]
    SCL = g.AF[:, 12288:14336]
    ISCL = g.AF[:, 14336:16384]
    ivf = g.SMALL[:, 164:165]
    lnb = g.SMALL[:, 165:166]
    dma(gp, TR, IN["c_tramp"].unsqueeze(0).broadcast_to([128, 1, T]), "gpl")
    kb.wait(sc, "gld")
    kb.inc(sc.activation(ivf, g.CONST[:, 928:929], AF.Exp, scale=-math.log(10000.0) / DH), "ssc")
    kb.wait(sc, "gvv")
    kb.inc(sc.activation(lnb, g.CONST[:, 928:929], AF.Ln, scale=1.0 / (1.4 * DH), bias=CB["lnbt"]), "ssc")
    kb.wait(vc, "gld")
    kb.wait(vc, "gpl")
    kb.wait(vc, "ssc")
    kb.inc(vc.tensor_scalar_mul(g.SMALL[:, 0:1], g.SMALL[:, 0:1], 8.0), "vv")
    kb.inc(vc.tensor_scalar_mul(g.SMALL[:, 2:3], g.SMALL[:, 2:3], 8.0), "vv")
    TWO_PI = 2.0 * math.pi
    kb.inc(vc.tensor_scalar_mul(ANG, TR, ivf), "vv")
    dma(sy, KF, IN["c_krnd"])
    kb.wait(vc, "gld")
    kb.inc(vc.scalar_tensor_tensor(ANG, KF, -TWO_PI, ANG, ALU.mult, ALU.add), "vv")
    kb.inc(vc.tensor_scalar(ANG, ANG, math.pi, -math.pi, ALU.min, ALU.max), "vv")
    kb.wait(sc, "vv")
    kb.inc(sc.activation(SN, ANG, AF.Sin), "ssc")
    kb.inc(vc.tensor_scalar_add(CS, ANG, math.pi / 2.0), "vv")
    kb.inc(vc.tensor_single_scalar(KF, CS, math.pi, ALU.is_gt), "vv")
    kb.inc(vc.scalar_tensor_tensor(CS, KF, -TWO_PI, CS, ALU.mult, ALU.add), "vv")
    kb.wait(sc, "vv")
    kb.inc(sc.activation(CS, CS, AF.Sin), "ssc")
    kb.inc(vc.tensor_scalar(POW, TR, 1.0 / (WIN // 2), -(T // 2) / (WIN // 2), ALU.mult, ALU.add), "vv")
    kb.inc(vc.tensor_scalar_mul(POW, POW, lnb), "vv")
    kb.wait(sc, "vv")
    kb.inc(sc.activation(SCL, POW, AF.Exp), "ssc")
    kb.inc(sc.activation(ISCL, POW, AF.Exp, scale=-1.0), "ssc")
    kb.wait(vc, "ssc")
    kb.inc(vc.tensor_mul(g.ROT[:, 0:T], CS, SCL), "vv")
    kb.inc(vc.tensor_mul(g.ROT[:, T:2 * T], SN, SCL), "vv")
    kb.inc(vc.tensor_mul(g.ROT[:, 2 * T:3 * T], CS, ISCL), "vv")
    kb.inc(vc.tensor_mul(g.ROT[:, 3 * T:4 * T], SN, ISCL), "vv")

    if phase == "rot":
        kb.wait(vc, "vv")
        for i4 in range(4):
            kb.inc(vc.tensor_copy(g.AF[:, AF_X + i4 * 2049 + 1: AF_X + i4 * 2049 + 1 + T],
                                  g.ROT[:, i4 * T:(i4 + 1) * T]), "vv")
        kb.inc(vc.tensor_copy(g.AF[:, AF_X + 4 * 2049 + 1: AF_X + 4 * 2049 + 1 + T], ANG), "vv")
        kb.inc(vc.tensor_copy(g.AF[:, AF_X + 5 * 2049 + 1: AF_X + 5 * 2049 + 1 + T], SN), "vv")
        kb.inc(vc.tensor_copy(g.AF[:, AF_X + 6 * 2049 + 1: AF_X + 6 * 2049 + 1 + T], SCL), "vv")
        kb.wait(gp, "vv")
        for gi in range(GT):
            kb.inc(gp.dma_start(out=DRAM["x_out"][gi * 128:(gi + 1) * 128, :],
                                in_=g.AF[:, AF_X + gi * 2049 + 1: AF_X + gi * 2049 + 1 + T]), "gout", 16)
        gp.wait_ge(kb.sem("gout"), kb.cnt["gout"])
        return

    kb.wait(gp, "vv")
    kb.wait(sy, "vv")
    for gi in range(GT):
        kb.inc(gp.memset(xt_pad(gi), 0.0), "gvv")
        dma(sy, xt(gi), IN["x_in"][gi * 128:(gi + 1) * 128, :])
    for gi in range(GT):
        kb.inc(gp.memset(g.XH[:, gi * (T + 1):gi * (T + 1) + 1], 0.0), "gvv")

    def emit_finish_from_x_early():
        kb.wait(gp, "vv")
        kb.wait(gp, "ssc")
        kb.wait(gp, "mm")
        for gi in range(GT):
            kb.inc(gp.dma_start(out=DRAM["x_out"][gi * 128:(gi + 1) * 128, :], in_=xt(gi)), "gout", 16)
        gp.wait_ge(kb.sem("gout"), kb.cnt["gout"])

    def emit_ln(src_fn, out_bf_fn, out_f32_fn=None, wait_store=False, gamma=None):
        mu = g.LNS[:, 0:CH]
        var = g.LNS[:, CH:2 * CH]
        rstd = g.LNS[:, 2 * CH:3 * CH]
        kb.wait(tn, "gld")
        kb.wait(vc, "gld")
        for c in range(NCH):
            sl = slice(c * CH, (c + 1) * CH)
            psS, psQ = g.PS[0], g.PS[1]
            kb.wait(tn, "vv")
            kb.wait(tn, "ssc")
            for k in range(GT):
                kb.inc(tn.matmul(psS[:, 0:CH], g.ones(), src_fn(k)[:, sl],
                                 start=(k == 0), stop=(k == GT - 1)), "mm")
            for k in range(GT):
                kb.wait(vc, "mm")
                kb.inc(vc.tensor_mul(g.SQ[:, 0:CH], src_fn(k)[:, sl], src_fn(k)[:, sl]), "vv")
                kb.wait(tn, "vv")
                kb.inc(tn.matmul(psQ[:, 0:CH], g.ones(), g.SQ[:, 0:CH],
                                 start=(k == 0), stop=(k == GT - 1)), "mm")
            kb.wait(vc, "mm")
            if _LN_DEBUG and c == 0:
                kb.inc(vc.tensor_copy(xt(0)[:, 0:CH], psS[:, 0:CH] if _LN_DEBUG == "sum" else psQ[:, 0:CH]), "vv")
            kb.inc(vc.tensor_scalar_mul(mu, psS[:, 0:CH], 1.0 / D), "vv")
            kb.inc(vc.tensor_scalar_mul(var, psQ[:, 0:CH], 1.0 / D), "vv")
            kb.inc(vc.tensor_mul(g.SQ[:, 0:CH], mu, mu), "vv")
            kb.inc(vc.tensor_sub(var, var, g.SQ[:, 0:CH]), "vv")
            kb.wait(sc, "vv")
            kb.inc(sc.activation(g.SQ[:, CH:2 * CH], var, AF.Ln, bias=CB["eps5"]), "ssc")
            kb.inc(sc.activation(rstd, g.SQ[:, CH:2 * CH], AF.Exp, scale=-0.5), "ssc")
            kb.wait(vc, "ssc")
            if wait_store:
                kb.wait(vc, "gst")
            for k in range(GT):
                kb.inc(vc.tensor_sub(g.SQ[:, 0:CH], src_fn(k)[:, sl], mu), "vv")
                if out_f32_fn is not None:
                    kb.inc(vc.tensor_mul(out_f32_fn(k)[:, sl], g.SQ[:, 0:CH], rstd), "vv")
                    kb.wait(sc, "vv")
                    kb.inc(sc.copy(out_bf_fn(k)[:, sl], out_f32_fn(k)[:, sl]), "ssc")
                elif gamma is not None:
                    kb.inc(vc.scalar_tensor_tensor(out_bf_fn(k)[:, sl], rstd, gamma[:, k:k + 1],
                                                   g.SQ[:, 0:CH], ALU.mult, ALU.mult), "vv")
                else:
                    kb.inc(vc.tensor_mul(out_bf_fn(k)[:, sl], g.SQ[:, 0:CH], rstd), "vv")

    def xh(k, shift=False):
        return g.xh_tile(k, shift)

    def load_w_mslice(wap, m, wb_off, kt=GT):
        for k in range(kt):
            dma(gp, g.WB[:, wb_off + k * 128: wb_off + (k + 1) * 128],
                wap[k * 128:(k + 1) * 128, m * 128:(m + 1) * 128], "gwt")
        return [g.WB[:, wb_off + k * 128: wb_off + (k + 1) * 128] for k in range(kt)]

    kb.wait(vc, "gld")
    kb.wait(vc, "gvv")
    emit_ln(lambda k: xt(k), lambda k: xh(k))

    if phase == "ln1":
        kb.wait(vc, "ssc")
        if not _LN_DEBUG:
            for gi in range(GT):
                kb.inc(vc.tensor_copy(xt(gi), xh(gi)), "vv")
        emit_finish_from_x_early()
        return

    AB_K = 0
    AB_VAUG = 8192
    AB_Q = 16384
    AB_AO = 18432
    AB_PT = 20480
    AB_VS = 21504

    def vaug_tile(h, tt):
        off = AB_VAUG + (h * 16 + tt) * 64
        return g.AB[:, off: off + 64]

    for m in range(8):
        wts = load_w_mslice(IN["w_akv"], m, 1024 * m)
    kb.wait(tn, "gwt")
    for m in range(8):
        wts = [g.WB[:, 1024 * m + k * 128: 1024 * m + (k + 1) * 128] for k in range(GT)]
        for c in range(NCH):
            sl = slice(c * CH, (c + 1) * CH)
            ps = g.PS[2]
            kb.wait(tn, "vv")
            kb.wait(tn, "ssc")
            for k in range(GT):
                kb.inc(tn.matmul(ps[:, 0:CH], wts[k], xh(k)[:, sl],
                                 start=(k == 0), stop=(k == GT - 1)), "mm")
            if m < 4:
                kraw = g.AB[:, AB_K + m * T: AB_K + (m + 1) * T]
                kb.wait(sc, "mm")
                kb.inc(sc.copy(kraw[:, sl], ps[:, 0:CH]), "ssc")
                kb.wait(vc, "ssc")
                kb.inc(vc.tensor_mul(g.SQ[:, 0:CH], kraw[:, sl], kraw[:, sl]), "vv")
                kb.wait(tn, "vv")
                kb.inc(tn.matmul(g.PS[3][:, 0:CH], g.bd64(), g.SQ[:, 0:CH],
                                 start=True, stop=True), "mm")
                kb.wait(vc, "mm")
                kb.inc(vc.tensor_scalar_max(g.SQ[:, 0:CH], g.PS[3][:, 0:CH], 1e-24), "vv")
                kb.wait(sc, "vv")
                kb.inc(sc.activation(g.SQ[:, CH:2 * CH], g.SQ[:, 0:CH], AF.Ln), "ssc")
                kb.inc(sc.activation(g.SQ[:, 0:CH], g.SQ[:, CH:2 * CH], AF.Exp, scale=-0.5), "ssc")
                kb.wait(vc, "ssc")
                kb.inc(vc.tensor_mul(kraw[:, sl], kraw[:, sl], g.SQ[:, 0:CH]), "vv")
            else:
                h0 = (m - 4) * 2
                vsc = g.AB[:, AB_VS: AB_VS + CH]
                kb.wait(sc, "mm")
                kb.wait(sc, "gst")
                kb.inc(sc.copy(vsc, ps[:, 0:CH]), "ssc")
                kb.wait(sy, "ssc")
                for hh in range(2):
                    for sb4 in range(4):
                        tt = c * 4 + sb4
                        src = vsc[hh * 64:(hh + 1) * 64, sb4 * 128:(sb4 + 1) * 128]
                        dst = vaug_tile(h0 + hh, tt)
                        kb.inc(sy.dma_start(out=dst, in_=src, transpose=True), "gst", 16)

    kb.wait(gp, "mm")
    for m in range(4):
        load_w_mslice(IN["w_aq"], m, 1024 * m)
    for k4 in range(4):
        for mo in range(8):
            dma(gp, g.WB[:, 4096 + k4 * 1024 + mo * 128: 4096 + k4 * 1024 + (mo + 1) * 128],
                IN["w_ao"][k4 * 128:(k4 + 1) * 128, mo * 128:(mo + 1) * 128], "gwt")

    for c in range(NCH):
        sl = slice(c * CH, (c + 1) * CH)
        kb.wait(tn, "gwt")
        for m in range(4):
            wts = [g.WB[:, 1024 * m + k * 128: 1024 * m + (k + 1) * 128] for k in range(GT)]
            ps = g.PS[2]
            kb.wait(tn, "vv")
            kb.wait(tn, "ssc")
            for k in range(GT):
                kb.inc(tn.matmul(ps[:, 0:CH], wts[k], xh(k)[:, sl],
                                 start=(k == 0), stop=(k == GT - 1)), "mm")
            qraw = g.AB[:, AB_Q + m * CH: AB_Q + (m + 1) * CH]
            kb.wait(sc, "mm")
            kb.inc(sc.copy(qraw, ps[:, 0:CH]), "ssc")
            kb.wait(vc, "ssc")
            kb.inc(vc.tensor_mul(g.SQ[:, 0:CH], qraw, qraw), "vv")
            kb.wait(tn, "vv")
            kb.inc(tn.matmul(g.PS[3][:, 0:CH], g.bd64(), g.SQ[:, 0:CH], start=True, stop=True), "mm")
            kb.wait(vc, "mm")
            kb.inc(vc.tensor_scalar_max(g.SQ[:, 0:CH], g.PS[3][:, 0:CH], 1e-24), "vv")
            kb.wait(sc, "vv")
            kb.inc(sc.activation(g.SQ[:, CH:2 * CH], g.SQ[:, 0:CH], AF.Ln), "ssc")
            kb.inc(sc.activation(g.SQ[:, 0:CH], g.SQ[:, CH:2 * CH], AF.Exp, scale=-0.5), "ssc")
            kb.wait(vc, "ssc")
            kb.inc(vc.tensor_mul(qraw, qraw, g.SQ[:, 0:CH]), "vv")
        kb.wait(tn, "gst")
        for h in range(8):
            km, hh = h // 2, h % 2
            hsl = slice(hh * 64, (hh + 1) * 64)
            tp = (0, hh * 64)
            ktile = g.AB[:, AB_K + km * T: AB_K + (km + 1) * T]
            qslice = g.AB[:, AB_Q + km * CH: AB_Q + (km + 1) * CH][hsl, :]
            psO = g.PS[4]
            psR = g.PS[5]
            for tt in range(16):
                psL = g.PT2
                kb.wait(tn, "vv")
                kb.wait(tn, "ssc")
                kb.inc(tn.matmul(psL[:, 0:CH], ktile[hsl, tt * 128:(tt + 1) * 128],
                                 qslice, start=True, stop=True), "mm")
                pt = g.AB[:, AB_PT + (tt % 2) * CH: AB_PT + (tt % 2) * CH + CH]
                kb.wait(sc, "mm")
                kb.inc(sc.activation(pt, psL[:, 0:CH], AF.Exp, scale=16.0, bias=CB["n16"]), "ssc")
                kb.wait(tn, "ssc")
                kb.inc(tn.matmul(psO[hsl, 0:CH], vaug_tile(h, tt), pt,
                                 start=(tt == 0), stop=(tt == 15), tile_position=tp), "mm")
                kb.inc(tn.matmul(psR[hsl, 0:CH], g.CONSTB[:, 0:64], pt,
                                 start=(tt == 0), stop=(tt == 15), tile_position=tp), "mm")
            kb.wait(vc, "mm")
            kb.inc(vc.tensor_scalar_max(g.LNS[hsl, 0:CH], psR[hsl, 0:CH], 1e-6), "vv")
            kb.wait(sc, "vv")
            kb.inc(sc.activation(g.LNS[hsl, CH:2 * CH], g.LNS[hsl, 0:CH], AF.Ln), "ssc")
            kb.inc(sc.activation(g.LNS[hsl, 0:CH], g.LNS[hsl, CH:2 * CH], AF.Exp, scale=-1.0), "ssc")
            kb.wait(vc, "ssc")
            ao = g.AB[:, AB_AO + km * CH: AB_AO + (km + 1) * CH][hsl, :]
            kb.inc(vc.tensor_mul(ao, psO[hsl, 0:CH], g.LNS[hsl, 0:CH]), "vv")
        for mo in range(8):
            ps = g.PS[2]
            kb.wait(tn, "vv")
            kb.wait(tn, "ssc")
            for k4 in range(4):
                wo = g.WB[:, 4096 + k4 * 1024 + mo * 128: 4096 + k4 * 1024 + (mo + 1) * 128]
                aot = g.AB[:, AB_AO + k4 * CH: AB_AO + (k4 + 1) * CH]
                kb.inc(tn.matmul(ps[:, 0:CH], wo, aot, start=(k4 == 0), stop=(k4 == 3)), "mm")
            kb.wait(vc, "mm")
            kb.inc(vc.tensor_add(xt(mo)[:, sl], xt(mo)[:, sl], ps[:, 0:CH]), "vv")

    def emit_ffn_layer(l, sub=None):
        WQKV = IN["w_qkv0"] if l == 0 else IN["w_qkv1"]
        WFO = IN["w_fo0"] if l == 0 else IN["w_fo1"]
        W1A = IN["w_f1a0"] if l == 0 else IN["w_f1a1"]
        W1G = IN["w_f1g0"] if l == 0 else IN["w_f1g1"]
        W2 = IN["w_f20"] if l == 0 else IN["w_f21"]
        qs8, ks = QS8[l], KS[l]
        lg, lb = LG[l], LB[l]
        b1a, b1g = B1A[l], B1G[l]
        b2 = B2[l]

        FB_K = 0
        FB_V = 8192
        FB_Q = 16384
        FB_AO = 18432
        FB_S1 = 20480
        FB_S2 = 20992
        FB_VS = 21504
        FB_PN = 22016
        FB_PT = 23296
        FB_LBB = 24576
        FB_FF = 0
        FB_W = 22528

        def vtok_tile(h, tt):
            off = FB_V + (h * 16 + tt) * 64
            return g.AB[:, off: off + 64]

        ROT_CSQ = g.ROT[:, 0:T]
        ROT_SNQ = g.ROT[:, T:2 * T]
        ROT_CSK = g.ROT[:, 2 * T:3 * T]
        ROT_SNK = g.ROT[:, 3 * T:4 * T]

        def xi(k, shift=False):
            return xt(k, shift)

        kb.wait(gp, "mm")
        for m in range(8):
            load_w_mslice(WQKV, m, 1024 * m)
        kb.wait(tn, "gwt")
        kb.wait(tn, "vv")

        def l2norm_gain_rot(m, c, ps, gaincol, cs_tab, sn_tab, dst_tile_ap):
            sl = slice(c * CH, (c + 1) * CH)
            raw = g.AB[:, FB_VS: FB_VS + CH]
            kb.wait(sc, "mm")
            if _K_DEBUG == 1:
                kb.inc(sc.copy(dst_tile_ap, ps[:, 0:CH]), "ssc")
                return
            kb.inc(sc.copy(raw, ps[:, 0:CH]), "ssc")
            kb.wait(vc, "ssc")
            kb.inc(vc.tensor_mul(g.SQ[:, 0:CH], raw, raw), "vv")
            kb.wait(tn, "vv")
            kb.inc(tn.matmul(g.PS[3][:, 0:CH], g.bd64(), g.SQ[:, 0:CH], start=True, stop=True), "mm")
            kb.wait(vc, "mm")
            kb.inc(vc.tensor_scalar_max(g.SQ[:, 0:CH], g.PS[3][:, 0:CH], 1e-24), "vv")
            kb.wait(sc, "vv")
            kb.inc(sc.activation(g.SQ[:, CH:2 * CH], g.SQ[:, 0:CH], AF.Ln), "ssc")
            kb.inc(sc.activation(g.SQ[:, 0:CH], g.SQ[:, CH:2 * CH], AF.Exp, scale=-0.5), "ssc")
            kb.wait(vc, "ssc")
            s1 = g.AB[:, FB_S1: FB_S1 + CH]
            s2 = g.AB[:, FB_S2: FB_S2 + CH]
            kb.inc(vc.scalar_tensor_tensor(s1, g.SQ[:, 0:CH], gaincol, raw, ALU.mult, ALU.mult), "vv")
            if _K_DEBUG == 2:
                kb.inc(vc.tensor_copy(dst_tile_ap, s1), "vv")
                return
            kb.wait(tn, "vv")
            kb.inc(tn.matmul(g.PT2[:, 0:CH], g.rotb(), s1, start=True, stop=True), "mm")
            kb.wait(vc, "mm")
            kb.inc(vc.tensor_mul(s2, s1, cs_tab[:, sl]), "vv")
            kb.inc(vc.tensor_mul(s1, g.PT2[:, 0:CH], sn_tab[:, sl]), "vv")
            kb.inc(vc.tensor_add(dst_tile_ap, s2, s1), "vv")

        for m in range(4, 12):
            if m < 8:
                wts = [g.WB[:, 1024 * m + k * 128: 1024 * m + (k + 1) * 128] for k in range(GT)]
            for c in range(NCH):
                sl = slice(c * CH, (c + 1) * CH)
                ps = g.PS[2]
                kb.wait(tn, "vv")
                kb.wait(tn, "ssc")
                if m >= 8:
                    wts = [g.WB[:, 1024 * (m - 8) + k * 128: 1024 * (m - 8) + (k + 1) * 128] for k in range(GT)]
                for k in range(GT):
                    kb.inc(tn.matmul(ps[:, 0:CH], wts[k], xh(k, shift=(k >= 4))[:, sl],
                                     start=(k == 0), stop=(k == GT - 1)), "mm")
                if m < 8:
                    km = m - 4
                    ktile = g.AB[:, FB_K + km * T: FB_K + (km + 1) * T]
                    l2norm_gain_rot(m, c, ps, ks, ROT_CSK, ROT_SNK, ktile[:, sl])
                else:
                    h0 = (m - 8) * 2
                    vsc = g.AB[:, FB_VS: FB_VS + CH]
                    kb.wait(sc, "mm")
                    kb.wait(sc, "gst")
                    kb.inc(sc.copy(vsc, ps[:, 0:CH]), "ssc")
                    kb.wait(sy, "ssc")
                    for hh in range(2):
                        for sb4 in range(4):
                            tt = c * 4 + sb4
                            kb.inc(sy.dma_start(out=vtok_tile(h0 + hh, tt),
                                                in_=vsc[hh * 64:(hh + 1) * 64, sb4 * 128:(sb4 + 1) * 128],
                                                transpose=True), "gst", 16)
            if m == 7:
                kb.wait(gp, "mm")
                for mv in range(4):
                    load_w_mslice(WQKV, 8 + mv, 1024 * mv)
                for k4 in range(4):
                    for mo in range(8):
                        dma(gp, g.WB[:, 4096 + k4 * 1024 + mo * 128: 4096 + k4 * 1024 + (mo + 1) * 128],
                            WFO[k4 * 128:(k4 + 1) * 128, mo * 128:(mo + 1) * 128], "gwt")
                kb.wait(tn, "gwt")

        if sub == "K":
            kb.wait(vc, "mm")
            kb.wait(vc, "ssc")
            for km in range(4):
                kb.inc(vc.tensor_copy(xt(km), g.AB[:, FB_K + km * T: FB_K + (km + 1) * T]), "vv")
            return

        kb.wait(gp, "mm")
        for m in range(4):
            load_w_mslice(WQKV, m, 1024 * m)
        kb.wait(tn, "gwt")
        for w in range(NW):
            wsl = slice(w * WIN, (w + 1) * WIN)
            for m in range(4):
                wts = [g.WB[:, 1024 * m + k * 128: 1024 * m + (k + 1) * 128] for k in range(GT)]
                ps = g.PS[2]
                kb.wait(tn, "vv")
                kb.wait(tn, "ssc")
                for k in range(GT):
                    kb.inc(tn.matmul(ps[:, 0:CH], wts[k], xh(k, shift=(k >= 4))[:, wsl],
                                     start=(k == 0), stop=(k == GT - 1)), "mm")
                qtile = g.AB[:, FB_Q + m * CH: FB_Q + (m + 1) * CH]
                l2norm_gain_rot(m, w, ps, qs8, ROT_CSQ, ROT_SNQ, qtile)
            for Tt0 in range(1):
                pass
            for km in range(4):
                ktile = g.AB[:, FB_K + km * T: FB_K + (km + 1) * T]
                for Tt in range(4):
                    gstart = w * WIN + Tt * 128
                    if w == 0:
                        blocks = list(range(Tt + 1))
                        nb_ = len(blocks)
                    else:
                        blocks = [(w - 1) * 4 + Tt + j for j in range(5)]
                        nb_ = 5
                    for hh in range(2):
                        h = 2 * km + hh
                        hsl = slice(hh * 64, (hh + 1) * 64)
                        tp = (0, hh * 64)
                        qsl = g.AB[:, FB_Q + km * CH + Tt * 128: FB_Q + km * CH + (Tt + 1) * 128][hsl, :]
                        kb.wait(tn, "vv")
                        kb.wait(tn, "ssc")
                        for j, bt in enumerate(blocks):
                            out = g.PT if j == nb_ - 1 else g.PS[4][:, j * 128:(j + 1) * 128]
                            kb.inc(tn.matmul(out, qsl, ktile[hsl, bt * 128:(bt + 1) * 128],
                                             start=True, stop=True), "mm")
                        kb.wait(vc, "mm")
                        kb.inc(vc.tensor_add(g.PT, g.PT, g.mhi()), "vv")
                        if w > 0:
                            kb.inc(vc.tensor_add(g.PS[4][:, 0:128], g.PS[4][:, 0:128], g.mlo()), "vv")
                        nm1 = g.LNS[:, 0:1]
                        nm2 = g.LNS[:, 1:2]
                        if nb_ > 1:
                            kb.inc(vc.reduce_max(nm1, g.PS[4][:, 0:(nb_ - 1) * 128],
                                                 axis=mybir.AxisListType.X), "vv")
                            kb.inc(vc.reduce_max(nm2, g.PT, axis=mybir.AxisListType.X), "vv")
                            kb.inc(vc.tensor_max(nm1, nm1, nm2), "vv")
                        else:
                            kb.inc(vc.reduce_max(nm1, g.PT, axis=mybir.AxisListType.X), "vv")
                        kb.inc(vc.tensor_scalar_mul(nm2, nm1, -1.0), "vv")
                        pn = g.AB[:, FB_PN + hh * 640: FB_PN + hh * 640 + 640]
                        rs1 = g.LNS[:, 2:3]
                        rs2 = g.LNS[:, 3:4]
                        kb.wait(sc, "vv")
                        if nb_ > 1:
                            kb.inc(sc.activation(pn[:, 0:(nb_ - 1) * 128], g.PS[4][:, 0:(nb_ - 1) * 128],
                                                 AF.Exp, bias=nm2, accum_out=rs1), "ssc")
                            kb.inc(sc.activation(pn[:, (nb_ - 1) * 128: nb_ * 128], g.PT,
                                                 AF.Exp, bias=nm2, accum_out=rs2), "ssc")
                            kb.wait(vc, "ssc")
                            kb.inc(vc.tensor_add(rs1, rs1, rs2), "vv")
                        else:
                            kb.inc(sc.activation(pn[:, 0:128], g.PT, AF.Exp,
                                                 bias=nm2, accum_out=rs1), "ssc")
                        kb.wait(sc, "vv")
                        kb.inc(sc.activation(rs2, rs1, AF.Ln), "ssc")
                        kb.inc(sc.activation(rs1, rs2, AF.Exp, scale=-1.0), "ssc")
                        kb.wait(vc, "ssc")
                        kb.inc(vc.tensor_scalar_mul(pn[:, 0:nb_ * 128], pn[:, 0:nb_ * 128], rs1), "vv")
                        kb.wait(sy, "vv")
                        ptb = g.AB[:, FB_PT + hh * 640: FB_PT + hh * 640 + 640]
                        for j in range(nb_):
                            kb.inc(sy.dma_start(out=ptb[:, j * 128:(j + 1) * 128],
                                                in_=pn[:, j * 128:(j + 1) * 128], transpose=True), "gst", 16)
                        kb.wait(tn, "gst")
                        kb.wait(tn, "vv")
                        for j, bt in enumerate(blocks):
                            kb.inc(tn.matmul(g.PS[5][hsl, Tt * 128:(Tt + 1) * 128], vtok_tile(h, bt),
                                             ptb[:, j * 128:(j + 1) * 128],
                                             start=(j == 0), stop=(j == nb_ - 1), tile_position=tp), "mm")
                ao = g.AB[:, FB_AO + km * CH: FB_AO + (km + 1) * CH]
                kb.wait(sc, "mm")
                kb.inc(sc.copy(ao, g.PS[5][:, 0:CH]), "ssc")
            kb.wait(tn, "ssc")
            for mo in range(8):
                ps = g.PS[2]
                kb.wait(tn, "vv")
                kb.wait(tn, "ssc")
                for k4 in range(4):
                    wo = g.WB[:, 4096 + k4 * 1024 + mo * 128: 4096 + k4 * 1024 + (mo + 1) * 128]
                    aot = g.AB[:, FB_AO + k4 * CH: FB_AO + (k4 + 1) * CH]
                    kb.inc(tn.matmul(ps[:, 0:CH], wo, aot, start=(k4 == 0), stop=(k4 == 3)), "mm")
                kb.wait(vc, "mm")
                kb.inc(vc.tensor_add(xi(mo)[:, wsl], xi(mo)[:, wsl], ps[:, 0:CH]), "vv")
        if sub == "A":
            return

        emit_ln(lambda k: xi(k, shift=(k >= 4)), lambda k: xh(k), gamma=lg)
        if sub == "HN":
            kb.wait(vc, "ssc")
            for gi in range(GT):
                kb.inc(vc.tensor_copy(xt(gi), xh(gi)), "vv")
            return

        FB_FF = 0
        FB_W1A = 22528
        FB_W1G = 23552
        FB_GL = 24576
        lbb = g.CONSTB[:, 384:392]
        kb.wait(sc, "vv")
        kb.inc(sc.copy(lbb, lb), "ssc")
        NH = 22
        bca = g.SMALL[:, 176:198]
        bcg = g.SMALL[:, 198:220]
        d_ff = DRAM["d_ff"]
        for th in range(2):
            for m in range(NH):
                kb.wait(gp, "mm")
                kb.wait(gp, "gst")
                for k in range(GT):
                    dma(gp, g.AB[:, FB_W1A + k * 128: FB_W1A + (k + 1) * 128],
                        W1A[k * 128:(k + 1) * 128, m * 128:(m + 1) * 128], "gwt")
                    dma(gp, g.AB[:, FB_W1G + k * 128: FB_W1G + (k + 1) * 128],
                        W1G[k * 128:(k + 1) * 128, m * 128:(m + 1) * 128], "gwt")
                kb.wait(tn, "gwt")
                if th == 0:
                    kb.wait(tn, "vv")
                    kb.wait(tn, "ssc")
                    for k in range(GT):
                        kb.inc(tn.matmul(g.PT[:, 0:1], g.AB[:, FB_W1A + k * 128: FB_W1A + (k + 1) * 128],
                                         lbb[:, k:k + 1], start=(k == 0), stop=(k == GT - 1)), "mm")
                    for k in range(GT):
                        kb.inc(tn.matmul(g.PT[:, 1:2], g.AB[:, FB_W1G + k * 128: FB_W1G + (k + 1) * 128],
                                         lbb[:, k:k + 1], start=(k == 0), stop=(k == GT - 1)), "mm")
                    kb.wait(vc, "mm")
                    kb.inc(vc.tensor_add(bca[:, m:m + 1], g.PT[:, 0:1], b1a[:, m:m + 1]), "vv")
                    kb.inc(vc.tensor_add(bcg[:, m:m + 1], g.PT[:, 1:2], b1g[:, m:m + 1]), "vv")
                for c2 in range(2):
                    sl = slice(th * 1024 + c2 * CH, th * 1024 + (c2 + 1) * CH)
                    psA, psG = g.PS[0], g.PS[1]
                    kb.wait(tn, "vv")
                    kb.wait(tn, "ssc")
                    for k in range(GT):
                        kb.inc(tn.matmul(psG[:, 0:CH], g.AB[:, FB_W1G + k * 128: FB_W1G + (k + 1) * 128],
                                         xh(k)[:, sl], start=(k == 0), stop=(k == GT - 1)), "mm")
                    for k in range(GT):
                        kb.inc(tn.matmul(psA[:, 0:CH], g.AB[:, FB_W1A + k * 128: FB_W1A + (k + 1) * 128],
                                         xh(k)[:, sl], start=(k == 0), stop=(k == GT - 1)), "mm")
                    gl = g.AB[:, FB_GL: FB_GL + CH]
                    kb.wait(sc, "mm")
                    kb.inc(sc.activation(gl, psG[:, 0:CH], AF.Gelu, bias=bcg[:, m:m + 1]), "ssc")
                    ffsl = g.AB[:, FB_FF + m * 1024 + c2 * CH: FB_FF + m * 1024 + (c2 + 1) * CH]
                    kb.wait(vc, "ssc")
                    kb.inc(vc.scalar_tensor_tensor(ffsl, psA[:, 0:CH], bca[:, m:m + 1], gl,
                                                   ALU.add, ALU.mult), "vv")
            kb.wait(sy, "vv")
            for m in range(NH):
                kb.inc(sy.dma_start(out=d_ff[m * 128:(m + 1) * 128, th * 1024:(th + 1) * 1024],
                                    in_=g.AB[:, FB_FF + m * 1024: FB_FF + (m + 1) * 1024]), "gst", 16)
        FB_FR = 0
        FB_W2 = 4096
        kb.wait(gp, "gst")
        for q in range(4):
            mos = [2 * q, 2 * q + 1]
            for m in range(NH):
                buf = m % 2
                kb.wait(gp, "mm")
                dma(gp, g.AB[:, FB_FR + buf * 2048: FB_FR + (buf + 1) * 2048],
                    d_ff[m * 128:(m + 1) * 128, :], "gwt")
                dma(gp, g.AB[:, FB_W2 + buf * 256: FB_W2 + buf * 256 + 256],
                    W2[m * 128:(m + 1) * 128, q * 256:(q + 1) * 256], "gwt")
                kb.wait(tn, "gwt")
                kb.wait(tn, "vv")
                for dd in range(2):
                    for c4 in range(4):
                        bank = g.PS[dd * 4 + c4]
                        kb.inc(tn.matmul(bank[:, 0:CH],
                                         g.AB[:, FB_W2 + buf * 256 + dd * 128: FB_W2 + buf * 256 + (dd + 1) * 128],
                                         g.AB[:, FB_FR + buf * 2048 + c4 * CH: FB_FR + buf * 2048 + (c4 + 1) * CH],
                                         start=(m == 0), stop=(m == NH - 1)), "mm")
                if m == NH - 1:
                    kb.wait(vc, "mm")
                    for dd in range(2):
                        mo = mos[dd]
                        for c4 in range(4):
                            sl = slice(c4 * CH, (c4 + 1) * CH)
                            kb.inc(vc.scalar_tensor_tensor(xi(mo)[:, sl], g.PS[dd * 4 + c4][:, 0:CH],
                                                           b2[:, mo:mo + 1], xi(mo)[:, sl],
                                                           ALU.add, ALU.add), "vv")

    def emit_attn_b():
        emit_ln(lambda k: xt(k), lambda k: xh(k))
        kb.wait(sy, "vv")
        kb.wait(sy, "ssc")
        for gi in range(GT):
            kb.inc(sy.dma_start(out=DRAM["d_xsp"][gi * 128:(gi + 1) * 128, :], in_=xt(gi)), "gst", 16)

        AB_XIN = 0
        W_IN = IN["w_bin"]
        fstage = g.SQ[:, 0:CH]
        for m in range(16):
            kb.wait(gp, "mm")
            wts = load_w_mslice(W_IN, m, 0)
            kb.wait(tn, "gwt")
            for c in range(NCH):
                sl = slice(c * CH, (c + 1) * CH)
                ps = g.PS[2]
                kb.wait(tn, "vv")
                kb.wait(tn, "ssc")
                for k in range(GT):
                    kb.inc(tn.matmul(ps[:, 0:CH], wts[k], xh(k)[:, sl],
                                     start=(k == 0), stop=(k == GT - 1)), "mm")
                if m < 8:
                    xin_t = g.AB[:, AB_XIN + m * T: AB_XIN + (m + 1) * T]
                    kb.wait(sc, "mm")
                    kb.inc(sc.activation(xin_t[:, sl], ps[:, 0:CH], AF.Silu), "ssc")
                    kb.wait(sc, "gst")
                    kb.inc(sc.activation(fstage, ps[:, 0:CH], AF.Silu), "ssc")
                    if phase == "attn_bxin":
                        kb.inc(sc.copy(xt(m)[:, sl], fstage), "ssc")
                    kb.wait(sy, "ssc")
                    kb.inc(sy.dma_start(out=DRAM["d_dtx"][m * 128:(m + 1) * 128, sl], in_=fstage), "gst", 16)
                else:
                    kb.wait(sc, "mm")
                    kb.wait(sc, "gst")
                    kb.inc(sc.copy(fstage, ps[:, 0:CH]), "ssc")
                    kb.wait(sy, "ssc")
                    kb.inc(sy.dma_start(out=DRAM["d_z"][(m - 8) * 128:(m - 7) * 128, sl], in_=fstage), "gst", 16)
        if phase == "attn_bxin":
            return

        AB_XP = 16384
        XPB = g.AB[:, AB_XP: AB_XP + T]
        kb.wait(gp, "mm")
        for k in range(GT):
            dma(gp, g.WB[:, k * 128: (k + 1) * 128], IN["w_bxp"][k * 128:(k + 1) * 128, :], "gwt")
        kb.wait(tn, "gwt")
        for c in range(NCH):
            sl = slice(c * CH, (c + 1) * CH)
            ps = g.PS[2]
            kb.wait(tn, "vv")
            kb.wait(tn, "ssc")
            for k in range(GT):
                kb.inc(tn.matmul(ps[:, 0:CH], g.WB[:, k * 128:(k + 1) * 128],
                                 g.AB[:, AB_XIN + k * T: AB_XIN + k * T + T][:, sl],
                                 start=(k == 0), stop=(k == GT - 1)), "mm")
            kb.wait(sc, "mm")
            kb.inc(sc.copy(XPB[:, sl], ps[:, 0:CH]), "ssc")
            kb.wait(sc, "gst")
            kb.inc(sc.copy(fstage, ps[:, 0:CH]), "ssc")
            kb.wait(sy, "ssc")
            kb.inc(sy.dma_start(out=DRAM["d_xp"][:, sl], in_=fstage), "gst", 16)

        NXR = g.XH[0:16, 0:8192].bitcast(F32)
        kb.wait(gp, "mm")
        for k in range(GT):
            dma(gp, g.WB[:, 1024 + k * 64: 1024 + (k + 1) * 64], IN["w_bres"][k * 128:(k + 1) * 128, :], "gwt")
        kb.wait(tn, "gwt")
        U = g.LNS[0:16, 0:CH]
        V = g.LNS[0:16, CH:2 * CH]
        W2 = g.LNS[0:16, 2 * CH:3 * CH]
        for c in range(NCH):
            sl = slice(c * CH, (c + 1) * CH)
            ps = g.PS[3]
            ps2 = g.PS[2]
            kb.wait(tn, "vv")
            kb.wait(tn, "ssc")
            for k in range(GT):
                kb.inc(tn.matmul(ps[0:64, 0:CH], g.WB[:, 1024 + k * 64: 1024 + (k + 1) * 64],
                                 g.AB[:, AB_XIN + k * T: AB_XIN + k * T + T][:, sl],
                                 start=(k == 0), stop=(k == GT - 1)), "mm")
            kb.wait(vc, "mm")
            kb.inc(vc.tensor_scalar(U, ps[0:16, 0:CH], 1e-10, None, ALU.add), "vv")
            kb.wait(sc, "mm")
            kb.inc(sc.copy(V, ps[32:48, 0:CH]), "ssc")
            kb.wait(vc, "ssc")
            kb.inc(vc.tensor_mul(W2, U, U), "vv")
            kb.inc(vc.tensor_mul(ps2[0:16, 0:CH], V, V), "vv")
            kb.inc(vc.tensor_add(W2, W2, ps2[0:16, 0:CH]), "vv")
            kb.wait(sc, "vv")
            kb.inc(sc.activation(W2, W2, AF.Ln), "ssc")
            kb.inc(sc.activation(W2, W2, AF.Exp, scale=-0.5), "ssc")
            kb.wait(vc, "ssc")
            kb.inc(vc.tensor_mul(NXR[:, 2 * c * CH: 2 * (c + 1) * CH: 2], U, W2), "vv")
            kb.inc(vc.tensor_mul(NXR[:, 2 * c * CH + 1: 2 * (c + 1) * CH: 2], V, W2), "vv")
        kb.wait(gp, "mm")
        for ms in range(GT):
            dma(gp, g.WB[0:64, 2048 + ms * 128: 2048 + (ms + 1) * 128],
                IN["w_bdt"][:, ms * 128:(ms + 1) * 128], "gwt")
        kb.wait(tn, "gwt")
        xring = g.AF[:, 16392:16904]
        for ms in range(GT):
            for c in range(NCH):
                sl = slice(c * CH, (c + 1) * CH)
                ps = g.PS[2]
                kb.wait(tn, "vv")
                kb.wait(tn, "ssc")
                kb.inc(tn.matmul(ps[:, 0:CH], g.WB[0:64, 2048 + ms * 128: 2048 + (ms + 1) * 128],
                                 XPB[0:64, :][:, sl], start=True, stop=True), "mm")
                bcol = g.SMALL[:, 8 + ms: 9 + ms]
                t1 = g.SQ[:, 0:CH]
                t2 = g.SQ[:, CH:2 * CH]
                t3 = g.LNS[:, 0:CH]
                kb.wait(vc, "mm")
                kb.inc(vc.tensor_scalar(t1, ps[:, 0:CH], bcol, None, ALU.add), "vv")
                kb.inc(vc.scalar_tensor_tensor(t3, t1, -1.0, t1, ALU.mult, ALU.min), "vv")
                kb.wait(sc, "vv")
                kb.inc(sc.activation(t2, t3, AF.Exp), "ssc")
                kb.inc(sc.activation(t2, t2, AF.Ln, bias=CB["one"]), "ssc")
                kb.wait(vc, "ssc")
                kb.inc(vc.tensor_scalar(t1, ps[:, 0:CH], bcol, 0.0, ALU.add, ALU.max), "vv")
                kb.inc(vc.tensor_add(t1, t1, t2), "vv")
                if phase == "attn_bdt":
                    kb.inc(vc.tensor_copy(xt(ms)[:, sl], t1), "vv")
                kb.wait(gp, "vv")
                dma(gp, xring, DRAM["d_dtx"][ms * 128:(ms + 1) * 128, sl], "gxr")
                kb.wait(vc, "gxr")
                kb.inc(vc.tensor_mul(t2, t1, xring), "vv")
                kb.wait(tn, "vv")
                for which, buf, dram in ((0, t1, DRAM["d_dt"]), (1, t2, DRAM["d_dtx2"])):
                    for j4 in range(4):
                        pt = g.PS[4 + which]
                        kb.inc(tn.transpose(pt[:, 0:128], buf[:, j4 * 128:(j4 + 1) * 128], g.ident()), "mm")
                        kb.wait(sc, "mm")
                        kb.wait(sc, "gst")
                        kb.wait(sc, "gso")
                        kb.inc(sc.copy(fstage[:, 0:128], pt[:, 0:128]), "ssc")
                        dst = dram[c * CH + j4 * 128: c * CH + (j4 + 1) * 128, ms * 128:(ms + 1) * 128]
                        kb.wait(sy, "ssc")
                        kb.inc(sy.dma_start(out=dst, in_=fstage[:, 0:128]), "gst", 16)
                        kb.wait(tn, "gst")

        if phase == "attn_bdt":
            return
        emit_scan()
        if phase == "attn_by":
            kb.wait(vc, "mm")
            for gi in range(GT):
                kb.inc(vc.tensor_copy(xt(gi), g.AB[:, gi * T: (gi + 1) * T]), "vv")
            return
        if phase == "attn_bh":
            XO = DRAM["x_out"]
            kb.wait(gp, "scv")
            kb.wait(gp, "ssc")
            kb.wait(gp, "vv")
            HSd = g.SQ[:, 0:256]
            MUd = g.SQ[:, 256:384]
            AFd = g.AF
            NXRd = g.XH[0:16, 0:8192].bitcast(F32)
            kb.inc(gp.dma_start(out=XO[0:128, 0:1024], in_=g.SQ[:]), "gout", 16)
            kb.inc(gp.dma_start(out=XO[128:256, 0:1536], in_=g.LNS[:]), "gout", 16)
            kb.inc(gp.dma_start(out=XO[256:384, 0:256], in_=g.SMALL[:]), "gout", 16)
            kb.inc(gp.dma_start(out=XO[384:512, :], in_=AFd[:, 4096:6144]), "gout", 16)
            kb.inc(gp.dma_start(out=XO[512:640, :], in_=AFd[:, 6144:8192]), "gout", 16)
            kb.inc(gp.dma_start(out=XO[640:768, :], in_=AFd[:, 8192:10240]), "gout", 16)
            kb.inc(gp.dma_start(out=XO[768:896, :], in_=AFd[:, 10240:12288]), "gout", 16)
            kb.inc(gp.dma_start(out=XO[896:912, :], in_=NXRd[:, 0:2048]), "gout", 16)
            kb.inc(gp.dma_start(out=XO[912:928, :], in_=NXRd[:, 2048:4096]), "gout", 16)
            kb.inc(gp.dma_start(out=XO[928:944, 0:2048], in_=AFd[:, 12288:14336][0:16, :]), "gout", 16)
            TINWd = g.XH[0:16, 8192:16384].bitcast(F32)
            kb.inc(gp.dma_start(out=XO[944:960, 0:2048], in_=TINWd[:, 0:2048]), "gout", 16)
            gp.wait_ge(kb.sem("gout"), kb.cnt["gout"])
            return

        kb.wait(gp, "mm")
        for k in range(GT):
            dma(gp, g.WB[:, k * 1024:(k + 1) * 1024], IN["w_bout"][k * 128:(k + 1) * 128, :], "gwt")
        zscr = g.AB[:, 16384 + 0: 16384 + CH]
        for k in range(GT):
            for c in range(NCH):
                sl = slice(c * CH, (c + 1) * CH)
                kb.wait(gp, "vv")
                dma(gp, xring, DRAM["d_z"][k * 128:(k + 1) * 128, sl], "gxr")
                kb.wait(sc, "gxr")
                kb.inc(sc.activation(zscr, xring, AF.Silu), "ssc")
                kb.wait(vc, "ssc")
                kb.inc(vc.tensor_mul(g.AB[:, k * T: k * T + T][:, sl],
                                     g.AB[:, k * T: k * T + T][:, sl], zscr), "vv")
        kb.wait(tn, "gwt")
        kb.wait(tn, "vv")
        for mo in range(GT):
            for c in range(NCH):
                sl = slice(c * CH, (c + 1) * CH)
                ps = g.PS[2]
                kb.wait(tn, "vv")
                kb.wait(tn, "ssc")
                for k in range(GT):
                    kb.inc(tn.matmul(ps[:, 0:CH], g.WB[:, k * 1024 + mo * 128: k * 1024 + (mo + 1) * 128],
                                     g.AB[:, k * T: k * T + T][:, sl],
                                     start=(k == 0), stop=(k == GT - 1)), "mm")
                kb.wait(gp, "vv")
                dma(gp, xring, DRAM["d_xsp"][mo * 128:(mo + 1) * 128, sl], "gxr")
                kb.wait(vc, "gxr")
                kb.wait(vc, "mm")
                kb.inc(vc.scalar_tensor_tensor(g.SQ[:, 0:CH], ps[:, 0:CH], g.SMALL[:, 16 + mo:17 + mo],
                                               xring, ALU.add, ALU.add), "vv")
                kb.inc(vc.tensor_copy(xt(mo)[:, sl], g.SQ[:, 0:CH]), "vv")

    def emit_scan():
        CHS = 16
        HF = 16
        NCHUNK = T // CHS
        DTA = g.AF[:, 0:2048]
        ALPH = [g.AF[:, 2048:4096]]
        BRE = g.AF[:, 4096:6144]
        BIM = g.AF[:, 6144:8192]
        CRE = g.AF[:, 8192:10240]
        CIM = g.AF[:, 10240:12288]
        DTX = [g.AF[:, 12288 + i * 2048: 12288 + (i + 1) * 2048] for i in range(2)]
        NXR = g.XH[0:16, 0:8192].bitcast(F32)
        HS = g.SQ[:, 0:256]
        MU = g.SQ[:, 256:384]
        UB = g.SQ[:, 384:512]
        U2B = g.SQ[:, 512:640]
        HH = g.SQ[:, 640:896]
        SCR = [g.LNS[:, 256 + i * 128: 256 + (i + 1) * 128] for i in range(10)]
        YB = g.AB[:, 0:16384]
        ANEG = g.SMALL[:, 224:225]
        H0R = g.SMALL[:, 225:226]
        H0I = g.SMALL[:, 226:227]
        CQ = g.SMALL[:, 228:230]
        TIN = g.LNS[0:16, 0:16]
        pcar = g.PS[0][0:16, 0:256]
        pcols = g.PS[1][:, 0:2]
        pys = [g.PS[2], g.PS[3]]

        kb.wait(gp, "vv")
        kb.wait(gp, "ssc")
        dma(gp, ANEG, IN["a_log"].unsqueeze(0).broadcast_to([8, 16, 1]), "gpl")
        dma(gp, H0R, IN["c_h0re"].unsqueeze(0).broadcast_to([8, 16, 1]), "gpl")
        dma(gp, H0I, IN["c_h0im"].unsqueeze(0).broadcast_to([8, 16, 1]), "gpl")

        def bc_dma(dst, row0):
            src = DRAM["d_xp"][row0: row0 + 31: 2, :].unsqueeze(0).broadcast_to([8, 16, T])
            dma(gp, dst, src, "gpl")
        kb.wait(gp, "gst")
        sy.wait_ge(kb.sem("gso"), kb.cnt.get("gso", 0))
        bc_dma(BRE, 64)
        bc_dma(BIM, 65)
        bc_dma(CRE, 96)
        bc_dma(CIM, 97)
        kb.wait(sc, "gpl")
        kb.inc(sc.activation(ANEG, ANEG, AF.Exp), "ssc")
        kb.wait(vc, "ssc")
        kb.wait(vc, "gst")
        kb.inc(vc.tensor_scalar_mul(ANEG, ANEG, -1.0), "vv")
        kb.inc(vc.tensor_copy(HS[:, 0:128], H0R.broadcast_to([128, 128])), "vv")
        kb.inc(vc.tensor_copy(HS[:, 128:256], H0I.broadcast_to([128, 128])), "vv")
        kb.wait(gp, "gst")
        kb.inc(gp.memset(MU[:], 0.01), "gvv")
        kb.wait(vc, "gvv")

        def chunk_dma(ci):
            base = ci * CHS
            src_dt = DRAM["d_dt"].rearrange("t (dq f) -> dq t f", dq=GT)
            src_dx = DRAM["d_dtx2"].rearrange("t (dq f) -> dq t f", dq=GT)
            for dqi in range(GT):
                s1_ = src_dt[dqi, base: base + CHS, :].unsqueeze(0).broadcast_to([16, CHS, 128])
                d1_ = DTA[dqi * 16:(dqi + 1) * 16, :].rearrange("p (t f) -> p t f", t=CHS)
                kb.inc(sy.dma_start(out=d1_, in_=s1_), "gsc", 16)
                s2_ = src_dx[dqi, base: base + CHS, :].unsqueeze(0).broadcast_to([16, CHS, 128])
                d2_ = DTX[ci % 2][dqi * 16:(dqi + 1) * 16, :].rearrange("p (t f) -> p t f", t=CHS)
                kb.inc(sy.dma_start(out=d2_, in_=s2_), "gsc", 16)

        SV = kb.sem("scv")
        SA = kb.sem("sca")
        SM = kb.sem("scm")
        SE = kb.sem("sce")
        nV, nA, nM = 6, 5, 3
        vb = ab = mb = 0
        LN10C = math.log(10.0)
        TINW = g.XH[0:16, 8192:16384].bitcast(F32)
        TW = [TINW[:, j * 128:(j + 1) * 128] for j in range(10)]
        TWn = [TINW[:, j * 128: j * 128 + 8] for j in range(5)]
        DT1n = TINW[:, 1280:1288]
        TIN2 = TINW[:, 1408:1410]
        pcols2 = g.PS[1][:, 0:2]

        chunk_dma(0)
        kb.wait(sc, "gsc")
        kb.inc(sc.activation(ALPH[0], DTA[:], AF.Exp, scale=ANEG), "ssc")
        kb.wait(tn, "vv")
        kb.wait(tn, "gvv")

        def emit_half(ci, hf, slot):
            nonlocal vb, ab, mb
            gstep0 = ci * CHS + hf * HF
            nstep = HF if _STEP_LIM is None else _STEP_LIM
            for i in range(nstep):
                t = gstep0 + i
                tn.wait_ge(SV, nV * gstep0 + i * nV)
                tn.matmul(pcar, g.En(), HS[:], start=True, stop=True).then_inc(SM, 1)
                nu_ap = NXR[:, 2 * t: 2 * t + 1]
                ni_ap = NXR[:, 2 * t + 1: 2 * t + 2]
                vc.wait_ge(SM, nM * gstep0 + i * nM + 1)
                vc.reduce_sum(TIN2, pcar.rearrange("p (c f) -> p c f", c=2), axis=mybir.AxisListType.X)
                vc.tensor_scalar(UB[:], HS[:, 0:128], 1e-10, None, ALU.add)
                vc.tensor_mul(SCR[0], UB[:], UB[:])
                vc.tensor_mul(SCR[1], HS[:, 128:256], HS[:, 128:256])
                vc.tensor_add(SCR[0], SCR[0], SCR[1])
                vc.tensor_scalar(TWn[0], TIN2[:, 0:1].broadcast_to([16, 8]), 1.0 / D, 1e-10, ALU.mult, ALU.add)
                vc.tensor_scalar(TWn[1], TIN2[:, 1:2].broadcast_to([16, 8]), 1.0 / D, None, ALU.mult)
                vc.tensor_mul(TWn[2], TWn[0], TWn[0])
                vc.tensor_mul(TWn[3], TWn[1], TWn[1])
                vc.tensor_scalar(DT1n, TWn[0], nu_ap, None, ALU.mult)
                vc.tensor_add(TWn[4], TWn[2], TWn[3])
                vc.scalar_tensor_tensor(TW[9], TWn[1][:, 0:1].broadcast_to([16, 128]), ni_ap,
                                        DT1n[:, 0:1].broadcast_to([16, 128]), ALU.mult, ALU.add).then_inc(SV, 1)
                sc.wait_ge(SV, nV * gstep0 + i * nV + 1)
                sc.activation(TW[5], TWn[4][:, 0:1].broadcast_to([16, 128]), AF.Ln)
                sc.activation(TW[5], TW[5], AF.Exp, scale=-0.5).then_inc(SA, 1)
                sc.wait_ge(SM, nM * gstep0 + i * nM)
                sc.activation(SCR[2], MU[:], AF.Square)
                sc.activation(SCR[3], SCR[2], AF.Ln, bias=CB["eps8"])
                sc.activation(SCR[2], SCR[3], AF.Exp, scale=0.5)
                sc.activation(SCR[3], SCR[2], AF.Ln, bias=CB["eps8"])
                sc.activation(SCR[4], SCR[0], AF.Ln).then_inc(SA, 1)
                vc.wait_ge(SA, nA * gstep0 + i * nA + 1)
                vc.tensor_mul(TW[6], TW[9], TW[5])
                vc.tensor_scalar(TW[7], TW[6], KAPPA / 2, -KAPPA / 2, ALU.mult, ALU.add)
                vc.tensor_scalar(TW[8], TW[6], -KAPPA / 2, 1.0 + KAPPA / 2, ALU.mult, ALU.add).then_inc(SV, 1)
                tn.wait_ge(SV, nV * gstep0 + i * nV + 2)
                tn.matmul(pcols2, g.En2(), TINW[:, 1023:1025], start=True, stop=True).then_inc(SM, 1)
                vc.wait_ge(SM, nM * gstep0 + i * nM + 2)
                vc.wait_ge(SA, nA * gstep0 + i * nA + 2)
                alp = ALPH[slot][:, i * 128:(i + 1) * 128]
                vc.tensor_scalar(SCR[5], alp, pcols2[:, 0:1], pcols2[:, 1:2], ALU.mult, ALU.add)
                vc.tensor_mul(SCR[6], SCR[5], SCR[3])
                vc.scalar_tensor_tensor(SCR[7], SCR[4], -0.5, SCR[6], ALU.mult, ALU.add).then_inc(SV, 1)
                sc.wait_ge(SV, nV * gstep0 + i * nV + 3)
                sc.activation(SCR[8], SCR[7], AF.Exp).then_inc(SA, 1)
                vc.wait_ge(SA, nA * gstep0 + i * nA + 3)
                dtxs = DTX[ci % 2][:, i * 128:(i + 1) * 128]
                bres = BRE[:, t: t + 1]
                bims = BIM[:, t: t + 1]
                vc.tensor_mul(SCR[9], SCR[8], UB[:])
                vc.scalar_tensor_tensor(HH[:, 0:128], dtxs, bres, SCR[9], ALU.mult, ALU.add)
                vc.tensor_mul(SCR[9], SCR[8], HS[:, 128:256])
                vc.scalar_tensor_tensor(HH[:, 128:256], dtxs, bims, SCR[9], ALU.mult, ALU.add)
                vc.tensor_mul(g.LNS[:, 256:512], HH[:], HH[:])
                vc.tensor_add(SCR[5], SCR[0], SCR[1]).then_inc(SV, 1)
                vc.tensor_scalar(U2B[:], HH[:, 0:128], 1e-10, None, ALU.add)
                vc.scalar_tensor_tensor(SCR[6], HH[:, 0:128], 2e-10, SCR[5], ALU.mult, ALU.add).then_inc(SV, 1)
                sc.wait_ge(SV, nV * gstep0 + i * nV + 4)
                sc.activation(SCR[7], SCR[5], AF.Ln, bias=CB["eps8"])
                sc.activation(SCR[5], SCR[7], AF.Exp, scale=0.5)
                sc.activation(SCR[7], SCR[5], AF.Exp, scale=2.0)
                sc.activation(SCR[7], SCR[7], AF.Ln, bias=CB["one"])
                sc.activation(SCR[7], SCR[7], AF.Exp, scale=-1.0).then_inc(SA, 1)
                sc.wait_ge(SV, nV * gstep0 + i * nV + 5)
                sc.activation(SCR[0], SCR[6], AF.Ln, bias=CB["eps20"])
                sc.activation(SCR[0], SCR[0], AF.Exp, scale=-0.5).then_inc(SA, 1)
                vc.wait_ge(SA, nA * gstep0 + i * nA + 5)
                vc.tensor_scalar(MU[:], SCR[7], -2.0, 1.0, ALU.mult, ALU.add)
                vc.tensor_mul(SCR[1], MU[:], SCR[0])
                vc.tensor_mul(HS[:, 0:128], SCR[1], U2B[:])
                vc.tensor_mul(HS[:, 128:256], SCR[1], HH[:, 128:256])
                cres = CRE[:, t: t + 1]
                cims = CIM[:, t: t + 1]
                edq3 = g.Edq().unsqueeze(1).broadcast_to([128, 16, 8])
                vc.tensor_scalar(SCR[2].rearrange("p (r f) -> p r f", r=16), edq3, cres, None, ALU.mult)
                vc.tensor_scalar(SCR[3].rearrange("p (r f) -> p r f", r=16), edq3, cims, None, ALU.mult).then_inc(SV, 1)
                tn.wait_ge(SV, nV * gstep0 + i * nV + 6)
                ycol = (gstep0 % 64) * 8
                pyb = pys[(gstep0 // 64) % 2]
                oy = pyb[:, ycol + i * 8: ycol + i * 8 + 8]
                tn.matmul(oy, HS[:, 0:128], SCR[2][:, 0:8], start=True, stop=False)
                tn.matmul(oy, HS[:, 128:256], SCR[3][:, 0:8], start=False, stop=True).then_inc(SM, 1)
            vb += nV * nstep
            ab += nA * nstep
            mb += nM * nstep
            kb.cnt["scv"] = vb
            kb.cnt["sca"] = ab
            kb.cnt["scm"] = mb

        NRUN = NCHUNK if _SCAN_LIM is None else _SCAN_LIM
        for ci in range(NRUN):
            if ci % 4 == 0 and ci // 4 >= 2:
                tn.wait_ge(SE, ci // 4 - 1)
            if ci + 1 < NCHUNK:
                sy.wait_ge(kb.sem("ssc"), kb.cnt["ssc"])
                if ci >= 1:
                    sy.wait_ge(SV, vb)
                chunk_dma(ci + 1)
            emit_half(ci, 0, 0)
            if ci + 1 < NCHUNK:
                kb.wait(sc, "gsc")
                sc.wait_ge(SV, vb)
                kb.inc(sc.activation(ALPH[0], DTA[:], AF.Exp, scale=ANEG), "ssc")
            if (ci % 4) == 3:
                blk64 = ci // 4
                bank = blk64 % 2
                sc.wait_ge(SM, mb)
                yout = YB.rearrange("p (dq t) -> p dq t", dq=GT)[:, :, blk64 * 64:(blk64 + 1) * 64]
                yin = pys[bank].rearrange("p (t dq) -> p t dq", dq=GT)
                kb.inc(sc.copy(yout.transpose([0, 2, 1]), yin), "sce")
        kb.wait(vc, "sce")
        kb.wait(vc, "scm", mb)
        kb.wait(sc, "scm", mb)
        kb.wait(tn, "sce")

    def emit_ffn_call():
        kb.wait(sy, "vv")
        kb.wait(sy, "ssc")
        for gi in range(GT):
            kb.inc(sy.dma_start(out=DRAM["d_xsp"][gi * 128:(gi + 1) * 128, :], in_=xt(gi)), "gst", 16)
        if "scv" in kb.cnt:
            kb.wait(gp, "scv")
            kb.wait(gp, "scm")
        kb.wait(gp, "vv")
        kb.wait(gp, "ssc")
        for gi in range(GT):
            kb.inc(gp.memset(g.XH[:, gi * (T + 1):gi * (T + 1) + 1], 0.0), "gvv")
        kb.wait(tn, "gvv")
        emit_ln(lambda k: xt(k), lambda k: xh(k), out_f32_fn=lambda k: xt(k), wait_store=True)
        if phase == "ffnLN2":
            return
        if phase == "xh2":
            kb.wait(vc, "ssc")
            for gi in range(GT):
                kb.inc(vc.tensor_copy(xt(gi), xh(gi)), "vv")
            return
        subm = {"ffnA0": "A", "ffnK": "K", "ffnHN": "HN"}.get(phase)
        emit_ffn_layer(0, sub=subm)
        if phase in ("ffnA0", "ffnL0", "ffnK", "ffnHN"):
            return
        kb.wait(vc, "ssc")
        kb.wait(vc, "mm")
        for gi in range(GT):
            kb.inc(vc.tensor_copy(xh(gi), xt(gi)), "vv")
        if phase == "ffnL0x":
            return
        emit_ffn_layer(1)
        ring = g.AF[:, 16392:16904]
        for gi in range(GT):
            for c in range(NCH):
                sl = slice(c * CH, (c + 1) * CH)
                kb.wait(gp, "vv")
                dma(gp, ring, DRAM["d_xsp"][gi * 128:(gi + 1) * 128, sl], "gxr")
                kb.wait(vc, "gxr")
                kb.inc(vc.tensor_add(xt(gi)[:, sl], xt(gi)[:, sl], ring), "vv")

    if phase == "attn_a":
        emit_finish_from_x_early()
        return
    emit_ffn_call()
    if phase in ("ffn1", "ffnLN2", "ffnA0", "ffnL0", "ffnL0x", "ffnK", "xh2", "ffnHN"):
        emit_finish_from_x_early()
        return
    emit_attn_b()
    if phase == "attn_bh":
        return
    if phase.startswith("attn_b"):
        emit_finish_from_x_early()
        return
    emit_ffn_call()
    emit_finish_from_x_early()

    def emit_finish_from_x():
        kb.wait(gp, "vv")
        kb.wait(gp, "ssc")
        kb.wait(gp, "mm")
        for gi in range(GT):
            dma(gp, DRAM["x_out"][gi * 128:(gi + 1) * 128, :], xt(gi), "gout")
        gp.wait_ge(kb.sem("gout"), kb.cnt["gout"])




def kernel(**inputs):
    in_maps = host_prep(inputs)
    nc = _build(_PHASE)
    res = run_bass_kernel_spmd(nc, in_maps, core_ids=list(range(len(in_maps))))
    out = np.stack([np.ascontiguousarray(np.asarray(r["x_out"], dtype=np.float32).T)
                    for r in res.results], axis=0)
    return out


_PHASE = "full"
_SCAN_LIM = None
_STEP_LIM = None
_LN_DEBUG = None
_K_DEBUG = False
